# version 07
# speedup vs baseline: 1.3907x; 1.0455x over previous
import contextlib
import numpy as np
import ml_dtypes
import concourse.bass as bass
import concourse.mybir as mybir
import bass_rust

F32 = mybir.dt.float32
BF16 = mybir.dt.bfloat16
AF = mybir.ActivationFunctionType
ALU = mybir.AluOpType
AX = mybir.AxisListType

D = 1024
NC_CHUNKS = D // 128

LIF_T = (1.0 / 3.709875, 1.95 / 3.709875, 2.95 / 3.709875, 3.8525 / 3.709875)


class Sem:
    __slots__ = ("h", "n", "name")

    def __init__(self, handle, name):
        self.h = handle
        self.n = 0
        self.name = name

    def at(self):
        return (self, self.n)


class Prog:
    def __init__(self, nc, stack):
        self.nc = nc
        self.stack = stack
        self.ops = {e: [] for e in ("sync", "tensor", "vector", "scalar", "gpsimd")}
        self._semi = 0
        self.prefix = ""
        self._gate = []
        self._gated = set()
        self._all_sems = []

    def sem(self, name=None):
        if name is None:
            name = f"s{self._semi}"
            self._semi += 1
        name = self.prefix + name
        h = self.stack.enter_context(self.nc.semaphore(name))
        sm = Sem(h, name)
        self._all_sems.append(sm)
        return sm

    def snapshot_all(self):
        return [(sm, sm.n) for sm in self._all_sems if sm.n > 0]

    def set_gate(self, gate, prefix):
        self._gate = list(gate)
        self._gated = set()
        self.prefix = prefix

    def emit(self, eng, fn, waits=(), incs=()):
        waits = list(waits)
        if eng not in self._gated:
            self._gated.add(eng)
            waits = self._gate + waits
        rec = []
        for s, amt in incs:
            s.n += amt
            rec.append((s, amt))
        self.ops[eng].append((fn, [(s, v) for (s, v) in waits if v > 0], rec))

    def finalize(self, final_waits=()):
        nc = self.nc
        with nc.Block() as block:
            for eng_name in ("sync", "tensor", "vector", "scalar", "gpsimd"):
                ops = self.ops[eng_name]
                if not ops and not (eng_name == "sync" and final_waits):
                    continue

                def make_body(eng_name, ops):
                    def body(eng):
                        last_wait = {}
                        for fn, waits, incs in ops:
                            for s, v in waits:
                                if last_wait.get(s.name, 0) >= v:
                                    continue
                                eng.wait_ge(s.h, v)
                                last_wait[s.name] = v
                            inst = fn(eng)
                            for s, amt in incs:
                                assert inst is not None
                                inst = inst.then_inc(s.h, amt)
                        if eng_name == "sync":
                            for s, v in final_waits:
                                if v > 0:
                                    eng.wait_ge(s.h, v)

                    return body

                getattr(block, eng_name)(make_body(eng_name, ops))


class ArenaAlloc:

    def __init__(self, nc, stack, total_bf16_cols=106336):
        self.t = stack.enter_context(nc.sbuf_tensor("the_arena", [128, total_bf16_cols], BF16))
        self.total = total_bf16_cols
        self.off = 0

    def reset(self):
        self.off = 0

    def alloc(self, shape, dtype):
        parts = shape[0]
        elems = 1
        for d_ in shape[1:]:
            elems *= d_
        esz = 4 if dtype == F32 else 2
        cols_bf = (elems * esz + 1) // 2
        off = (self.off + 31) & ~31
        self.off = off + cols_bf
        assert self.off <= self.total, f"arena overflow {self.off} > {self.total}"
        ap = self.t[0:parts, off:off + cols_bf]
        if dtype != BF16:
            ap = ap.bitcast(dtype)
        if len(shape) == 3:
            ap = ap.rearrange("p (a b) -> p a b", b=shape[2])
        elif len(shape) == 4:
            ap = ap.rearrange("p (a b c) -> p a b c", b=shape[2], c=shape[3])
        return ap


class DmaSlots:

    def __init__(self, p, name, n):
        self.sems = [p.sem(f"{name}{i}") for i in range(n)]

    def __getitem__(self, i):
        return self.sems[i % len(self.sems)]



class LNState:

    def __init__(self, p, nbuf=2):
        self.nbuf = nbuf
        self.stats = [p.AR.alloc([128, 12], F32) for i in range(nbuf)]
        self.mv = [p.AR.alloc([128, 2], F32) for i in range(nbuf)]
        self.rstd = [p.AR.alloc([128, 1], F32) for i in range(nbuf)]


def emit_ln_stats(p, ln, b, srcs, eps, dve, act, waits):
    nsrc = len(srcs)
    assert nsrc in (2, 4)
    for i, ap in enumerate(srcs):
        w = list(waits) if i == 0 else ()
        inc = [(dve, 1)] if i == nsrc - 1 else ()
        p.emit("vector", (lambda ap=ap, i=i, b=b: lambda v: v.bn_stats(ln.stats[b][:, 6 * i:6 * i + 6], ap))(), waits=w, incs=inc)
    p.emit("vector", lambda v, b=b: v.bn_aggr(ln.mv[b][:], ln.stats[b][:, 0:6 * nsrc]),
           waits=[dve.at()], incs=[(dve, 1)])
    aggr = dve.at()
    p.emit("scalar", lambda s, b=b: s.activation(out=ln.rstd[b][:], in_=ln.mv[b][:, 1:2], func=AF.Ln, bias=eps[:]),
           waits=[aggr], incs=[(act, 1)])
    p.emit("scalar", lambda s, b=b: s.activation(out=ln.rstd[b][:], in_=ln.rstd[b][:], func=AF.Exp, scale=-0.5),
           waits=[act.at()], incs=[(act, 1)])
    return act.at(), aggr



def emit_phase1(nc, p, stack, P, tns, act, dve, S, x_in, x_out, W):
    TT = S // 128
    QC = S // 512
    NK = S // 128
    qw, qb, kw, kb, vw, vb, ow, ob, eye8 = (W["q_w"], W["q_b"], W["k_w"], W["k_b"],
                                            W["v_w"], W["v_b"], W["o_w"], W["o_b"], W["eye8"])

    xnbf = nc.dram_tensor("p1_xnbf", [S, D], BF16, kind="Internal").ap()
    qcT_h = nc.dram_tensor("p1_qcT_h", [D, S], BF16, kind="Internal").ap()
    kcT_h = nc.dram_tensor("p1_kcT_h", [D, S], BF16, kind="Internal").ap()
    vblk_h = nc.dram_tensor("p1_vblk_h", [TT, 128, 16 * 65], BF16, kind="Internal").ap()

    if True:
        sb = lambda n, shp, d: p.AR.alloc(shp, d)

        eps = sb("eps", [128, 1], F32)
        neg4 = sb("neg4", [128, 1], F32)
        ones_bf = sb("ones_bf", [1, 128], BF16)

        xt = [sb(f"xt{i}", [128, D], F32) for i in range(2)]
        xn_sb = [sb(f"xn{i}", [128, D], BF16) for i in range(2)]
        ln_pre = LNState(p)

        xnT = sb("xnT", [128, NC_CHUNKS, S], BF16)

        qw_sb = sb("qw_sb", [128, NC_CHUNKS, D], BF16)
        kw_sb = sb("kw_sb", [128, NC_CHUNKS, D], BF16)
        vw_sb = sb("vw_sb", [128, NC_CHUNKS, D], BF16)
        ow_sb = vw_sb
        qb_sb = sb("qb_sb", [128, NC_CHUNKS], F32)
        kb_sb = sb("kb_sb", [128, NC_CHUNKS], F32)
        qb8 = sb("qb8", [8, 128], F32)
        kb8 = sb("kb8", [8, 128], F32)
        eye8_sb = sb("eye8_sb", [8, 8], F32)
        vb_row = sb("vb_row", [1, D], BF16)
        ob_row = sb("ob_row", [1, D], BF16)

        bi = [sb(f"bi{i}", [128, 512], BF16) for i in range(4)]
        c01 = sb("c01", [128, 512], BF16)
        c23 = sb("c23", [128, 512], BF16)
        cnt_sb = [sb(f"cnt{i}", [128, 512], BF16) for i in range(2)]
        vblk_sb = [sb(f"vblk{i}", [128, 16, 65], BF16) for i in range(2)]
        ps_qk = [P[0], P[1], P[2], P[3]]
        ps_v = [P[0], P[1], P[2], P[3]]

        qcT_sb = [sb("qcT0", [128, S], BF16)]
        kcT_sb = [sb("kcT0", [128, S], BF16)]
        vbA_sb = [sb("vbA0", [128, NK, 65], BF16)]
        vbB_sb = [sb("vbB0", [128, NK, 65], BF16)]
        Eab = [sb(f"Eab{i}", [128, 1024], BF16) for i in range(2)]
        rs4 = sb("rs4", [2, 512], F32)
        rs4b = sb("rs4b", [1, 512], F32)
        recA = sb("recA", [1, 512], F32)
        recB = sb("recB", [1, 512], F32)
        rhiA = sb("rhiA", [1, 512], BF16)
        rloA = sb("rloA", [1, 512], BF16)
        rhiB = sb("rhiB", [1, 512], BF16)
        rloB = sb("rloB", [1, 512], BF16)
        rbc = sb("rbc", [128, 512], F32)
        o_buf = xnT
        ps_s = [P[0], P[2]]
        ps_s2 = [P[1], P[3]]
        ps_sf = [p.PQ[0][:, 0:1024], p.PQ[1][:, 0:1024]]
        ps_oA = P[4]
        ps_oB = P[5]
        ps_r = P[6]

        xi2 = [sb(f"xi2{i}", [128, D], F32) for i in range(2)]
        xo_sb = [sb(f"xo{i}", [128, D], F32) for i in range(2)]
        ln_post = LNState(p)
        ps_y = [P[0], P[1], P[2], P[3]]

        ld_w = p.sem("ld_w")
        ld_c = p.sem("ld_c")
        ldx = DmaSlots(p, "ldx", 2)
        stn = DmaSlots(p, "stn", 2)
        tr = p.sem("tr")
        stqk = DmaSlots(p, "stqk", 2)
        stv = DmaSlots(p, "stv", 2)
        ldb = DmaSlots(p, "ldb", 2)
        ldbg = DmaSlots(p, "ldbg", 2)
        ldx2 = DmaSlots(p, "ldx2", 2)
        stx = DmaSlots(p, "stx", 2)

        p.emit("vector", lambda v: v.memset(eps[:], 1e-5))
        p.emit("vector", lambda v: v.memset(neg4[:], -4.0))
        p.emit("vector", lambda v: v.memset(ones_bf[:], 1.0))
        for i in range(2):
            p.emit("vector", (lambda i=i: lambda v: v.memset(vblk_sb[i][:, :, 64:65], 1.0))(), incs=[(dve, 1)])
        CONSTS = dve.at()

        for w_dram, w_tile in ((qw, qw_sb), (kw, kw_sb), (vw, vw_sb)):
            p.emit("gpsimd", (lambda w_dram=w_dram, w_tile=w_tile: lambda g: g.dma_start(
                out=w_tile[:], in_=w_dram.rearrange("(c p) d -> p c d", p=128)))(), incs=[(ld_w, 16)])
        for b_dram, b_tile in ((qb, qb8), (kb, kb8)):
            p.emit("gpsimd", (lambda b_dram=b_dram, b_tile=b_tile: lambda g: g.dma_start(
                out=b_tile[:], in_=b_dram.rearrange("(c p) -> c p", p=128)))(), incs=[(ld_w, 16)])
        p.emit("sync", lambda s_: s_.dma_start(out=eye8_sb[:], in_=eye8), incs=[(ld_c, 16)])
        for b_dram, b_tile in ((vb, vb_row), (ob, ob_row)):
            p.emit("gpsimd", (lambda b_dram=b_dram, b_tile=b_tile: lambda g: g.dma_start(
                out=b_tile[:], in_=b_dram.rearrange("(u d) -> u d", u=1)))(), incs=[(ld_w, 16)])
        W_ALL = ld_w.at()
        C_ALL = ld_c.at()
        p.emit("tensor", lambda t: t.matmul(P[7][0:128, 0:8], qb8[:], eye8_sb[:], start=True, stop=True),
               waits=[W_ALL, C_ALL], incs=[(tns, 1)])
        p.emit("vector", lambda v: v.tensor_copy(qb_sb[:], P[7][0:128, 0:8]), waits=[tns.at()], incs=[(dve, 1)])
        BQ = dve.at()
        p.emit("tensor", lambda t: t.matmul(P[7][0:128, 0:8], kb8[:], eye8_sb[:], start=True, stop=True),
               waits=[BQ], incs=[(tns, 1)])
        p.emit("vector", lambda v: v.tensor_copy(kb_sb[:], P[7][0:128, 0:8]), waits=[tns.at()], incs=[(dve, 1)])
        BIAS_T = dve.at()

        prol_done = []
        for t in range(TT):
            b = t % 2
            war = [prol_done[t - 2]] if t >= 2 else []
            war2 = [stn[b].at()]
            p.emit("sync", (lambda t=t, b=b: lambda s: s.dma_start(out=xt[b][:], in_=x_in[t * 128:(t + 1) * 128, :]))(),
                   waits=war, incs=[(ldx[b], 16)])
            rstd_at, aggr_at = emit_ln_stats(p, ln_pre, b,
                                    [xt[b][:, 0:512], xt[b][:, 512:1024]],
                                    eps, dve, act, waits=[ldx[b].at()])
            p.emit("vector", (lambda b=b: lambda v: v.tensor_scalar(
                out=xn_sb[b][:], in0=xt[b][:], scalar1=ln_pre.mv[b][:, 0:1], scalar2=ln_pre.rstd[b][:],
                op0=ALU.subtract, op1=ALU.mult))(), waits=[rstd_at, aggr_at] + war2, incs=[(dve, 1)])
            prol_done.append(dve.at())
            p.emit("sync", (lambda t=t, b=b: lambda s: s.dma_start(out=xnbf[t * 128:(t + 1) * 128, :], in_=xn_sb[b][:]))(),
                   waits=[prol_done[-1]], incs=[(stn[b], 16)])

        PROL_ST = [stn[0].at(), stn[1].at()]
        for c in range(NC_CHUNKS):
            p.emit("sync", (lambda c=c: lambda s: s.dma_start(
                out=xnT[:, c, :], in_=xnbf[:, c * 128:(c + 1) * 128], transpose=True))(),
                   waits=PROL_ST, incs=[(tr, 16)])
        TR_ALL = tr.at()

        unit_lif = []
        unit_idx = 0
        for w_tile, b_tile, outT in ((qw_sb, qb_sb, qcT_h), (kw_sb, kb_sb, kcT_h)):
            for oc in range(NC_CHUNKS):
                for nq in range(QC):
                    bank = unit_idx % 4
                    war = [unit_lif[unit_idx - 4]] if unit_idx >= 4 else []
                    for ic in range(NC_CHUNKS):
                        p.emit("tensor", (lambda w_tile=w_tile, ic=ic, oc=oc, nq=nq, bank=bank: lambda t: t.matmul(
                            ps_qk[bank][:], w_tile[:, ic, oc * 128:(oc + 1) * 128],
                            xnT[:, ic, nq * 512:(nq + 1) * 512],
                            start=(ic == 0), stop=(ic == NC_CHUNKS - 1)))(),
                            waits=([TR_ALL, W_ALL, BIAS_T] + war) if ic == 0 else (),
                            incs=[(tns, 1)] if ic == NC_CHUNKS - 1 else ())
                    mm = tns.at()
                    for j in range(4):
                        p.emit("vector", (lambda j=j, bank=bank, b_tile=b_tile, oc=oc: lambda v: v.tensor_scalar(
                            out=bi[j][:], in0=ps_qk[bank][:], scalar1=b_tile[:, oc:oc + 1], scalar2=float(LIF_T[j]),
                            op0=ALU.add, op1=ALU.is_gt))(),
                            waits=[mm, BIAS_T] if j == 0 else (), incs=[(dve, 1)] if j == 3 else ())
                    p.emit("vector", lambda v: v.tensor_add(c01[:], bi[0][:], bi[1][:]), waits=[dve.at()])
                    p.emit("vector", lambda v: v.tensor_add(c23[:], bi[2][:], bi[3][:]), incs=[(dve, 1)])
                    cb = unit_idx % 2
                    war3 = [stqk[cb].at()]
                    p.emit("vector", (lambda cb=cb: lambda v: v.tensor_add(cnt_sb[cb][:], c01[:], c23[:]))(),
                           waits=war3 + [dve.at()], incs=[(dve, 1)])
                    unit_lif.append(dve.at())
                    p.emit("sync", (lambda outT=outT, oc=oc, nq=nq, cb=cb: lambda s: s.dma_start(
                        out=outT[oc * 128:(oc + 1) * 128, nq * 512:(nq + 1) * 512], in_=cnt_sb[cb][:]))(),
                        waits=[unit_lif[-1]], incs=[(stqk[cb], 16)])
                    unit_idx += 1

        n_qk_stores = 2 * NC_CHUNKS * QC
        v_units = []
        for t in range(TT):
            vbuf = t % 2
            for dh in range(2):
                uidx = t * 2 + dh
                bank = uidx % 4
                war = ([v_units[uidx - 4]] if uidx >= 4 else []) + ([unit_lif[-4 + (uidx + 0)]] if uidx < 4 else [])
                for ic in range(NC_CHUNKS):
                    p.emit("tensor", (lambda ic=ic, t=t, dh=dh, bank=bank: lambda tn: tn.matmul(
                        ps_v[bank][:], xnT[:, ic, t * 128:(t + 1) * 128],
                        vw_sb[:, ic, dh * 512:(dh + 1) * 512],
                        start=(ic == 0), stop=False))(),
                        waits=([TR_ALL, W_ALL] + war) if ic == 0 else ())
                p.emit("tensor", (lambda dh=dh, bank=bank: lambda tn: tn.matmul(
                    ps_v[bank][:], ones_bf[0:1, 0:128], vb_row[0:1, dh * 512:(dh + 1) * 512],
                    start=False, stop=True))(), incs=[(tns, 1)])
                mm = tns.at()
                for j in range(4):
                    p.emit("vector", (lambda j=j, bank=bank: lambda v: v.tensor_scalar(
                        out=bi[j][:], in0=ps_v[bank][:], scalar1=float(LIF_T[j]), scalar2=None,
                        op0=ALU.is_gt))(), waits=[mm] if j == 0 else (), incs=[(dve, 1)] if j == 3 else ())
                p.emit("vector", lambda v: v.tensor_add(c01[:], bi[0][:], bi[1][:]), waits=[dve.at()])
                p.emit("vector", lambda v: v.tensor_add(c23[:], bi[2][:], bi[3][:]), incs=[(dve, 1)])
                war3 = [stv[t % 2].at()] if dh == 0 else []
                p.emit("vector", (lambda vbuf=vbuf, dh=dh: lambda v: v.tensor_tensor(
                    vblk_sb[vbuf][:, dh * 8:(dh + 1) * 8, 0:64],
                    c01[:].rearrange("p (h d) -> p h d", h=8),
                    c23[:].rearrange("p (h d) -> p h d", h=8), ALU.add))(),
                    waits=war3 + [dve.at()] + ([CONSTS] if t == 0 and dh == 0 else []), incs=[(dve, 1)])
                v_units.append(dve.at())
            p.emit("sync", (lambda t=t, vbuf=vbuf: lambda s: s.dma_start(
                out=vblk_h[t, :, :], in_=vblk_sb[vbuf][:].rearrange("p h c -> p (h c)")))(),
                waits=[v_units[-1]], incs=[(stv[vbuf], 16)])

        STEPA_ST = [stqk[0].at(), stqk[1].at(), stv[0].at(), stv[1].at()]
        V_MM_END = tns.at()
        p.emit("gpsimd", lambda g: g.dma_start(
            out=ow_sb[:], in_=ow.rearrange("(c p) d -> p c d", p=128)), waits=[V_MM_END], incs=[(ld_w, 16)])
        OW_LD = ld_w.at()

        hp_mm_end = [None] * 8
        o_written = []
        exp_bank_gate = {}
        scores_bank_gate = {}
        for hp in range(8):
            lb = 0
            war = [hp_mm_end[hp - 1]] if hp >= 1 else []
            p.emit("sync", (lambda hp=hp, lb=lb: lambda s: s.dma_start(
                out=qcT_sb[lb][:], in_=qcT_h[hp * 128:(hp + 1) * 128, :]))(),
                waits=STEPA_ST + war, incs=[(ldb[lb], 16)])
            p.emit("sync", (lambda hp=hp, lb=lb: lambda s: s.dma_start(
                out=kcT_sb[lb][:], in_=kcT_h[hp * 128:(hp + 1) * 128, :]))(), incs=[(ldb[lb], 16)])
            for head, vb_t in ((2 * hp, vbA_sb), (2 * hp + 1, vbB_sb)):
                p.emit("gpsimd", (lambda head=head, vb_t=vb_t, lb=lb: lambda g: g.dma_start(
                    out=vb_t[lb][:], in_=vblk_h[:, :, head * 65:(head + 1) * 65].rearrange("t p c -> p t c")))(),
                    waits=STEPA_ST + war, incs=[(ldbg[lb], 16)])
            LDB = ldb[lb].at()
            LDBG = ldbg[lb].at()

            for qc in range(QC):
                acc_war = [o_written[-1]] if o_written else []
                mm2_at = {}
                exp_at = {}
                def emit_mm1(kt):
                    bank = kt % 2
                    w1 = [LDB, LDBG] + (acc_war if kt == 0 else [])
                    g = scores_bank_gate.get(bank)
                    if g is not None:
                        w1 = w1 + [g]
                    p.emit("tensor", (lambda lb=lb, kt=kt, qc=qc, bank=bank: lambda t: t.matmul(
                        ps_s[bank][:], kcT_sb[lb][0:64, kt * 128:(kt + 1) * 128],
                        qcT_sb[lb][0:64, qc * 512:(qc + 1) * 512], start=True, stop=True))(),
                        waits=w1)
                    p.emit("tensor", (lambda lb=lb, kt=kt, qc=qc, bank=bank: lambda t: t.matmul(
                        ps_s2[bank][:], kcT_sb[lb][64:128, kt * 128:(kt + 1) * 128],
                        qcT_sb[lb][64:128, qc * 512:(qc + 1) * 512], start=True, stop=True,
                        tile_position=(64, 0)))(), incs=[(tns, 1)])
                    mm1 = tns.at()
                    ew = exp_bank_gate.get(bank)
                    p.emit("scalar", (lambda bank=bank: lambda s: s.activation(
                        out=Eab[bank][:], in_=ps_sf[bank], func=AF.Exp, scale=1.0 / 128.0, bias=neg4[:]))(),
                        waits=[mm1] + ([ew] if ew else []), incs=[(act, 1)])
                    exp_at[kt] = act.at()
                    scores_bank_gate[bank] = exp_at[kt]
                def emit_mm2(kt):
                    bank = kt % 2
                    p.emit("tensor", (lambda lb=lb, kt=kt, bank=bank: lambda t: t.matmul(
                        ps_oA[0:65, :], vbA_sb[lb][:, kt, :], Eab[bank][:, 0:512],
                        start=(kt == 0), stop=(kt == NK - 1)))(), waits=[exp_at[kt]])
                    p.emit("tensor", (lambda lb=lb, kt=kt, bank=bank: lambda t: t.matmul(
                        ps_oB[0:65, :], vbB_sb[lb][:, kt, :], Eab[bank][:, 512:1024],
                        start=(kt == 0), stop=(kt == NK - 1)))(), incs=[(tns, 1)])
                    exp_bank_gate[bank] = tns.at()
                emit_mm1(0)
                for kt in range(1, NK):
                    emit_mm1(kt)
                    emit_mm2(kt - 1)
                emit_mm2(NK - 1)
                MM2_ALL = tns.at()
                p.emit("vector", lambda v: v.tensor_scalar(
                    out=rs4[0:1, :], in0=ps_oA[64:65, :], scalar1=4.0, scalar2=None, op0=ALU.mult),
                    waits=[MM2_ALL], incs=[(dve, 1)])
                p.emit("vector", lambda v: v.tensor_scalar(
                    out=rs4b[:], in0=ps_oB[64:65, :], scalar1=4.0, scalar2=None, op0=ALU.mult),
                    incs=[(dve, 1)])
                p.emit("vector", lambda v: v.reciprocal(recA[:], rs4[0:1, :]), waits=[dve.at()], incs=[(dve, 1)])
                p.emit("vector", lambda v: v.reciprocal(recB[:], rs4b[:]), incs=[(dve, 1)])
                p.emit("vector", lambda v: v.tensor_copy(rhiA[:], recA[:]), waits=[dve.at()], incs=[(dve, 1)])
                p.emit("vector", lambda v: v.tensor_copy(rhiB[:], recB[:]), incs=[(dve, 1)])
                p.emit("vector", lambda v: v.tensor_tensor(rloA[:], recA[:], rhiA[:], ALU.subtract),
                       waits=[dve.at()], incs=[(dve, 1)])
                p.emit("vector", lambda v: v.tensor_tensor(rloB[:], recB[:], rhiB[:], ALU.subtract),
                       incs=[(dve, 1)])
                RL = dve.at()
                p.emit("tensor", lambda t: t.matmul(ps_r[0:64, :], ones_bf[0:1, 0:64], rhiA[:], start=True, stop=False),
                       waits=[RL])
                p.emit("tensor", lambda t: t.matmul(ps_r[0:64, :], ones_bf[0:1, 0:64], rloA[:], start=False, stop=True))
                p.emit("tensor", lambda t: t.matmul(ps_r[64:128, :], ones_bf[0:1, 0:64], rhiB[:], start=True, stop=False,
                                                    tile_position=(0, 64)))
                p.emit("tensor", lambda t: t.matmul(ps_r[64:128, :], ones_bf[0:1, 0:64], rloB[:], start=False, stop=True,
                                                    tile_position=(0, 64)), incs=[(tns, 1)])
                p.emit("scalar", lambda s: s.copy(out=rbc[:], in_=ps_r[:]), waits=[tns.at()], incs=[(act, 1)])
                RBC = act.at()
                p.emit("vector", (lambda hp=hp, qc=qc: lambda v: v.tensor_tensor(
                    o_buf[0:64, hp, qc * 512:(qc + 1) * 512], ps_oA[0:64, :], rbc[0:64, :], ALU.mult))(),
                    waits=[RBC], incs=[(dve, 1)])
                p.emit("vector", (lambda hp=hp, qc=qc: lambda v: v.tensor_tensor(
                    o_buf[64:128, hp, qc * 512:(qc + 1) * 512], ps_oB[0:64, :], rbc[64:128, :], ALU.mult))(),
                    incs=[(dve, 1)])
                o_written.append(dve.at())
            hp_mm_end[hp] = (tns, tns.n)

        O_ALL = o_written[-1]

        epi_done = []
        for t in range(TT):
            b = t % 2
            war = [epi_done[t - 2]] if t >= 2 else []
            p.emit("sync", (lambda t=t, b=b: lambda s: s.dma_start(out=xi2[b][:], in_=x_in[t * 128:(t + 1) * 128, :]))(),
                   waits=war + [stx[b].at()], incs=[(ldx2[b], 16)])
            LDX = ldx2[b].at()
            mm_at = []
            for dh in range(2):
                bank = (t * 2 + dh) % 4
                for ic in range(NC_CHUNKS):
                    p.emit("tensor", (lambda ic=ic, t=t, dh=dh, bank=bank: lambda tn: tn.matmul(
                        ps_y[bank][:], o_buf[:, ic, t * 128:(t + 1) * 128],
                        ow_sb[:, ic, dh * 512:(dh + 1) * 512], start=(ic == 0), stop=False))(),
                        waits=([O_ALL, OW_LD] + war) if ic == 0 else ())
                p.emit("tensor", (lambda dh=dh, bank=bank: lambda tn: tn.matmul(
                    ps_y[bank][:], ones_bf[0:1, 0:128], ob_row[0:1, dh * 512:(dh + 1) * 512],
                    start=False, stop=True))(), incs=[(tns, 1)])
                mm_at.append(tns.at())
            b0 = (t * 2) % 4
            b1 = (t * 2 + 1) % 4
            rstd_at, aggr_at = emit_ln_stats(p, ln_post, b,
                                    [ps_y[b0][:], ps_y[b1][:]],
                                    eps, dve, act, waits=[mm_at[1]])
            STX_PREV = stx[b].at()
            for dh, bank in ((0, b0), (1, b1)):
                p.emit("vector", (lambda b=b, dh=dh, bank=bank: lambda v: v.tensor_scalar(
                    out=xo_sb[b][:, dh * 512:(dh + 1) * 512], in0=ps_y[bank][:],
                    scalar1=ln_post.mv[b][:, 0:1], scalar2=ln_post.rstd[b][:],
                    op0=ALU.subtract, op1=ALU.mult))(), waits=[rstd_at, aggr_at, STX_PREV] if dh == 0 else (),
                    incs=[(dve, 1)] if dh == 1 else ())
            p.emit("vector", (lambda b=b: lambda v: v.tensor_add(xo_sb[b][:], xo_sb[b][:], xi2[b][:]))(),
                   waits=[LDX, dve.at()], incs=[(dve, 1)])
            epi_done.append(dve.at())
            p.emit("sync", (lambda t=t, b=b: lambda s: s.dma_start(out=x_out[t * 128:(t + 1) * 128, :], in_=xo_sb[b][:]))(),
                   waits=[epi_done[-1]], incs=[(stx[b], 16)])

        return [stx[0].at(), stx[1].at()]



def emit_prologue_ln(p, S, x_in, xnbf, xt, xn_sb, ln_pre, eps, ldx, stn, dve, act):
    TT = S // 128
    prol_done = []
    for t in range(TT):
        b = t % 2
        war = [prol_done[t - 2]] if t >= 2 else []
        war2 = [stn[b].at()]
        p.emit("sync", (lambda t=t, b=b: lambda s: s.dma_start(out=xt[b][:], in_=x_in[t * 128:(t + 1) * 128, :]))(),
               waits=war, incs=[(ldx[b], 16)])
        rstd_at, aggr_at = emit_ln_stats(p, ln_pre, b,
                                         [xt[b][:, 0:512], xt[b][:, 512:1024]],
                                         eps, dve, act, waits=[ldx[b].at()])
        p.emit("vector", (lambda b=b: lambda v: v.tensor_scalar(
            out=xn_sb[b][:], in0=xt[b][:], scalar1=ln_pre.mv[b][:, 0:1], scalar2=ln_pre.rstd[b][:],
            op0=ALU.subtract, op1=ALU.mult))(), waits=[rstd_at, aggr_at] + war2, incs=[(dve, 1)])
        prol_done.append(dve.at())
        p.emit("sync", (lambda t=t, b=b: lambda s: s.dma_start(out=xnbf[t * 128:(t + 1) * 128, :], in_=xn_sb[b][:]))(),
               waits=[prol_done[-1]], incs=[(stn[b], 16)])
    return [stn[0].at(), stn[1].at()]


def emit_postln_residual(p, t, b, ps_pair, x_in, x_out, xi2, xo_sb, ln_post, eps,
                         ldx2, stx, dve, act, mm_wait, extra_war):
    p.emit("sync", (lambda t=t, b=b: lambda s: s.dma_start(out=xi2[b][:], in_=x_in[t * 128:(t + 1) * 128, :]))(),
           waits=list(extra_war) + [stx[b].at()], incs=[(ldx2[b], 16)])
    LDX = ldx2[b].at()
    STX_PREV = stx[b].at()
    rstd_at, aggr_at = emit_ln_stats(p, ln_post, b, [ps_pair[0][:], ps_pair[1][:]],
                                     eps, dve, act, waits=[mm_wait])
    for dh in range(2):
        p.emit("vector", (lambda b=b, dh=dh: lambda v: v.tensor_scalar(
            out=xo_sb[b][:, dh * 512:(dh + 1) * 512], in0=ps_pair[dh][:],
            scalar1=ln_post.mv[b][:, 0:1], scalar2=ln_post.rstd[b][:],
            op0=ALU.subtract, op1=ALU.mult))(), waits=[rstd_at, aggr_at, STX_PREV] if dh == 0 else (),
            incs=[(dve, 1)] if dh == 1 else ())
    p.emit("vector", (lambda b=b: lambda v: v.tensor_add(xo_sb[b][:], xo_sb[b][:], xi2[b][:]))(),
           waits=[LDX, dve.at()], incs=[(dve, 1)])
    done = dve.at()
    p.emit("sync", (lambda t=t, b=b: lambda s: s.dma_start(out=x_out[t * 128:(t + 1) * 128, :], in_=xo_sb[b][:]))(),
           waits=[done], incs=[(stx[b], 16)])
    return done



def emit_phase2(nc, p, stack, P, tns, act, dve, S, x_in, x_out, W):
    TT = S // 128
    NSLAB = S // 512
    NF = 16384 // 128
    debug = False
    rmT, rupT = W["rmT"], W["rupT"]
    wcTf = W["wcTf"]

    xnbf = nc.dram_tensor("p2_xnbf", [S, D], BF16, kind="Internal").ap()
    wcT = nc.dram_tensor("p2_wcT", [16384, D], BF16, kind="Internal").ap()

    if True:
        sb = lambda n, shp, d: p.AR.alloc(shp, d)
        eps = sb("eps", [128, 1], F32)
        xt = [sb(f"xt{i}", [128, D], F32) for i in range(2)]
        xn_sb = [sb(f"xn{i}", [128, D], BF16) for i in range(2)]
        ln_pre = LNState(p)
        ln_post = LNState(p)
        xi2 = xt
        xo_sb = [sb(f"xo{i}", [128, D], F32) for i in range(2)]

        wcpc = [sb(f"wcpc{i}", [128, D], BF16) for i in range(2)]
        m_stat = sb("m_stat", [128, 128 * 128], BF16)
        hT = sb("hT", [128, NF, 512], BF16)
        hT_flat = hT[:].rearrange("p a b -> p (a b)")
        aT_sb = hT_flat[0:16, 0:4096]
        bT_sb = hT_flat[0:16, 4096:4096 + 16384]
        mst_sb = hT_flat[0:16, 20480:20480 + 16384]
        xnT_slab = sb("xnT_slab", [128, NC_CHUNKS, 512], BF16)
        wtile = [sb(f"wtile{i}", [128, D], BF16) for i in range(2)]

        ld_w = p.sem("ld_w")
        ld_c = p.sem("ld_c")
        ldx = DmaSlots(p, "ldx", 2)
        stn = DmaSlots(p, "stn", 2)
        ldx2 = DmaSlots(p, "ldx2", 2)
        stx = DmaSlots(p, "stx", 2)
        stwc = DmaSlots(p, "stwc", 2)
        ldwt = DmaSlots(p, "ldwt", 2)
        trx = p.sem("trx")

        p.emit("vector", lambda v: v.memset(eps[:], 1e-5))

        mdone = []
        gelu_gate = {0: 0, 1: 0}
        ldwc = DmaSlots(p, "ldwc", 2)
        PREP_ST = []
        for fc in range(NF):
            bpc = fc % 2
            p.emit("gpsimd", (lambda fc=fc, bpc=bpc: lambda g: g.dma_start(
                out=wcpc[bpc], in_=wcTf[fc * 128:(fc + 1) * 128, :]))(),
                waits=[stwc[bpc].at()], incs=[(ldwc[bpc], 16)])
            p.emit("scalar", (lambda fc=fc, bpc=bpc: lambda s: s.dma_start(
                out=wcT[fc * 128:(fc + 1) * 128, :], in_=wcpc[bpc]))(),
                waits=[ldwc[bpc].at()], incs=[(stwc[bpc], 16)])
            PREP_ST.append([stwc[0].at(), stwc[1].at()])
        WCT_ST = PREP_ST[-1]
        if debug:
            p.emit("sync", lambda s: s.dma_start(out=wct_dbg, in_=wcT), waits=WCT_ST, incs=[(trx, 16)])

        PROL = emit_prologue_ln(p, S, x_in, xnbf, xt, xn_sb, ln_pre, eps, ldx, stn, dve, act)
        PROL_DVE = dve.at()


        p.emit("gpsimd", lambda g: g.dma_start(out=aT_sb, in_=rmT), incs=[(ld_w, 16)])
        p.emit("gpsimd", lambda g: g.dma_start(out=bT_sb, in_=rupT), incs=[(ld_w, 16)])
        TRDONE = ld_w.at()
        for hr in range(256):
            bm = hr % 2
            war = [(dve, dve.n)] if False else ([mdone[hr - 2]] if hr >= 2 else [])
            p.emit("tensor", (lambda hr=hr, bm=bm: lambda t: t.matmul(
                P[6 + bm][0:16, 0:64], aT_sb[0:16, hr * 16:(hr + 1) * 16],
                bT_sb[0:16, hr * 64:(hr + 1) * 64], start=True, stop=True))(),
                waits=[TRDONE] + war, incs=[(tns, 1)])
            p.emit("vector", (lambda hr=hr, bm=bm: lambda v: v.tensor_copy(
                mst_sb[0:16, hr * 64:(hr + 1) * 64], P[6 + bm][0:16, 0:64]))(),
                waits=[tns.at()], incs=[(dve, 1)])
            mdone.append(dve.at())
        MDONE = dve.at()
        p.emit("vector", lambda v: v.memset(m_stat[:], 0.0), incs=[(dve, 1)])
        MZ = dve.at()
        for j in range(4):
            for side in range(2):
                pos = 2 * j + side
                out_ap = m_stat[16 * pos:16 * pos + 16, :].rearrange(
                    "p (h g f c) -> p h g f c", h=4, g=8, f=4)[:, :, :, j, side * 64:side * 64 + 64]
                in_ap = mst_sb.rearrange(
                    "p (h g e c) -> p h g e c", h=4, g=8, e=8)[:, :, :, 2 * j + side, :]
                p.emit("sync", (lambda out_ap=out_ap, in_ap=in_ap: lambda s_: s_.dma_start(
                    out=out_ap, in_=in_ap))(), waits=[MDONE, MZ], incs=[(trx, 16)])
        MSTAT = trx.at()

        ldxr = DmaSlots(p, "ldxr", 2)
        epi_done = []
        rails_end = []
        for sl in range(NSLAB):
            xb = 0
            war_xr = [rails_end[sl - 1]] if sl >= 1 else []
            for c in range(NC_CHUNKS):
                p.emit("sync", (lambda sl=sl, c=c: lambda s: s.dma_start(
                    out=xnT_slab[:, c, :],
                    in_=xnbf[sl * 512:(sl + 1) * 512, c * 128:(c + 1) * 128], transpose=True))(),
                    waits=(PROL + war_xr) if c == 0 else (), incs=[(ldxr[xb], 16)])
            XNR = ldxr[xb].at()
            for pr in range(128):
                bank = pr % 2
                w = [MSTAT, XNR] if pr == 0 else []
                if sl > 0 and pr == 0:
                    w = w + [epi_done[-1]]
                war = [(act, gelu_gate[bank])] if pr >= 2 or sl > 0 else []
                c = (pr % 32) // 4
                p.emit("tensor", (lambda pr=pr, c=c, bank=bank: lambda t: t.matmul(
                    P[bank][:], m_stat[:, pr * 128:(pr + 1) * 128],
                    xnT_slab[:, c, :],
                    start=True, stop=True))(), waits=w + war, incs=[(tns, 1)])
                p.emit("scalar", (lambda pr=pr, bank=bank: lambda s: s.activation(
                    out=hT[:, pr, :], in_=P[bank][:], func=AF.Gelu))(),
                    waits=[tns.at()], incs=[(act, 1)])
                gelu_gate[bank] = act.n
            rails_end.append(tns.at())
            GELU_ALL = act.at()
            if debug and sl == 0:
                p.emit("sync", lambda s: s.dma_start(out=h_dbg, in_=hT[:].rearrange("p a b -> p (a b)")),
                       waits=[GELU_ALL], incs=[(trx, 16)])
                p.emit("sync", lambda s: s.dma_start(out=m_dbg, in_=m_stat[:]), waits=[GELU_ALL], incs=[(trx, 16)])
            mm_done = {}
            for fc in range(NF):
                bwt = fc % 2
                p.emit("sync", (lambda fc=fc, bwt=bwt: lambda s: s.dma_start(
                    out=wtile[bwt][:], in_=wcT[fc * 128:(fc + 1) * 128, :]))(),
                    waits=(PREP_ST[fc] if sl == 0 else WCT_ST) +
                          ([mm_done[fc - 2]] if fc >= 2 else ([epi_done[-1]] if epi_done else [])),
                    incs=[(ldwt[bwt], 16)])
                LW = ldwt[bwt].at()
                for qt in range(4):
                    for dh in range(2):
                        bank = qt * 2 + dh
                        w0 = []
                        if qt == 0 and dh == 0:
                            w0 = [LW, GELU_ALL]
                            if fc == 0 and epi_done:
                                w0 = w0 + [epi_done[-1]]
                        p.emit("tensor", (lambda fc=fc, qt=qt, dh=dh, bank=bank, bwt=bwt: lambda t: t.matmul(
                            P[bank][:], hT[:, fc, qt * 128:(qt + 1) * 128],
                            wtile[bwt][:, dh * 512:(dh + 1) * 512],
                            start=(fc == 0), stop=(fc == NF - 1)))(),
                            waits=w0,
                            incs=[(tns, 1)] if (qt == 3 and dh == 1) else ())
                mm_done[fc] = tns.at()
            CPROJ = tns.at()
            for qt in range(4):
                t_glob = sl * 4 + qt
                b = t_glob % 2
                done = emit_postln_residual(
                    p, t_glob, b, [P[qt * 2], P[qt * 2 + 1]], x_in, x_out,
                    xi2, xo_sb, ln_post, eps, ldx2, stx, dve, act, CPROJ,
                    ([epi_done[-2]] if len(epi_done) >= 2 else [PROL_DVE]))
                epi_done.append(done)

        return [stx[0].at(), stx[1].at()]



def emit_phase3(nc, p, stack, P, tns, act, dve, S, x_in, x_out, W):
    TT = S // 128
    QC = S // 512
    NK = S // 128
    NSTR = S // 32
    wqkv, bqkv, wout, bout = W["w_qkv"], W["b_qkv"], W["w_out"], W["b_out"]
    eye8, eyeb, bandm, stridm = W["eye8"], W["eyeb"], W["bandm"], W["stridm"]

    xnbf = nc.dram_tensor("p3_xnbf", [S, D], BF16, kind="Internal").ap()
    qT_h = nc.dram_tensor("p3_qT_h", [D, S], BF16, kind="Internal").ap()
    kT_h = nc.dram_tensor("p3_kT_h", [D, S], BF16, kind="Internal").ap()
    kstr_h = nc.dram_tensor("p3_kstr_h", [D, 128], BF16, kind="Internal").ap()
    vblk_h = nc.dram_tensor("p3_vblk_h", [TT, 128, 16 * 65], BF16, kind="Internal").ap()

    if True:
        sb = lambda n, shp, d: p.AR.alloc(shp, d)

        eps = sb("eps", [128, 1], F32)
        nC = sb("nC", [128, 1], F32)
        ones_bf = sb("ones_bf", [1, 128], BF16)

        xt = [sb(f"xt{i}", [128, D], F32) for i in range(2)]
        xn_sb = [sb(f"xn{i}", [128, D], BF16) for i in range(2)]
        ln_pre = LNState(p)
        ln_post = LNState(p)
        xi2 = xt
        xo_sb = [sb(f"xo{i}", [128, D], F32) for i in range(2)]

        xnT = sb("xnT", [128, NC_CHUNKS, S], BF16)
        o_buf = xnT

        w_sb = sb("w_sb", [128, NC_CHUNKS, 3 * D], BF16)
        wo_view = w_sb[:].rearrange("p c d -> p (c d)")
        qb_sb = sb("qb_sb", [128, NC_CHUNKS], F32)
        kb_sb = sb("kb_sb", [128, NC_CHUNKS], F32)
        b8 = sb("b8", [8, 128], F32)
        eye8_sb = sb("eye8_sb", [8, 8], F32)
        eyeb_sb = sb("eyeb_sb", [128, 128], BF16)
        vb_row = sb("vb_row", [1, D], BF16)
        ob_row = sb("ob_row", [1, D], BF16)
        bandm_sb = sb("bandm_sb", [128, 6, 512], BF16)
        stridm_sb = sb("stridm_sb", [128, QC, 512], BF16)

        out_sb = [sb(f"out{i}", [128, 512], BF16) for i in range(2)]
        vblk_sb = [sb(f"vblk{i}", [128, 16, 65], BF16) for i in range(2)]
        kstr_sb = sb("kstr_sb", [128, 128], BF16)

        qT_sb = sb("qT_sb", [128, S], BF16)
        kT_sb = sb("kT_sb", [128, S], BF16)
        vbA_sb = sb("vbA_sb", [128, NK, 65], BF16)
        vbB_sb = sb("vbB_sb", [128, NK, 65], BF16)
        kstrp = sb("kstrp", [128, 128], BF16)
        vstrA = sb("vstrA", [128, 65], BF16)
        vstrB = sb("vstrB", [128, 65], BF16)
        Eab = [sb(f"Eab{i}", [128, 1024], BF16) for i in range(2)]
        rs4 = sb("rs4", [1, 512], F32)
        rs4b = sb("rs4b", [1, 512], F32)
        recA = sb("recA", [1, 512], F32)
        recB = sb("recB", [1, 512], F32)
        rhiA = sb("rhiA", [1, 512], BF16)
        rloA = sb("rloA", [1, 512], BF16)
        rhiB = sb("rhiB", [1, 512], BF16)
        rloB = sb("rloB", [1, 512], BF16)
        rbc = sb("rbc", [128, 512], F32)

        ps_qk = [P[0], P[1], P[2], P[3]]
        ps_v = [P[0], P[1], P[2], P[3]]
        ps_s = [P[0], P[2]]
        ps_s2 = [P[1], P[3]]
        ps_sf = [p.PQ[0][:, 0:1024], p.PQ[1][:, 0:1024]]
        ps_oA = P[4]
        ps_oB = P[5]
        ps_r = P[6]
        ps_y = [P[0], P[1], P[2], P[3]]

        ld_w = p.sem("ld_w")
        ld_c = p.sem("ld_c")
        ldx = DmaSlots(p, "ldx", 2)
        stn = DmaSlots(p, "stn", 2)
        tr = p.sem("tr")
        stqk = DmaSlots(p, "stqk", 2)
        stks = DmaSlots(p, "stks", 1)
        stv = DmaSlots(p, "stv", 2)
        ldb = DmaSlots(p, "ldb", 2)
        ldbg = DmaSlots(p, "ldbg", 2)
        ldx2 = DmaSlots(p, "ldx2", 2)
        stx = DmaSlots(p, "stx", 2)

        p.emit("vector", lambda v: v.memset(eps[:], 1e-5))
        p.emit("vector", lambda v: v.memset(nC[:], -16.0))
        p.emit("vector", lambda v: v.memset(ones_bf[:], 1.0))
        for i in range(2):
            p.emit("vector", (lambda i=i: lambda v: v.memset(vblk_sb[i][:, :, 64:65], 1.0))())
        p.emit("vector", lambda v: v.memset(vstrA[:], 0.0))
        p.emit("vector", lambda v: v.memset(vstrB[:], 0.0), incs=[(dve, 1)])
        CONSTS = dve.at()

        p.emit("gpsimd", lambda g: g.dma_start(
            out=w_sb[:], in_=wqkv.rearrange("(c p) d -> p c d", p=128)), incs=[(ld_w, 16)])
        p.emit("gpsimd", lambda g: g.dma_start(
            out=vb_row[:], in_=bqkv[2 * D:3 * D].rearrange("(u d) -> u d", u=1)), incs=[(ld_w, 16)])
        p.emit("gpsimd", lambda g: g.dma_start(
            out=ob_row[:], in_=bout.rearrange("(u d) -> u d", u=1)), incs=[(ld_w, 16)])
        p.emit("gpsimd", lambda g: g.dma_start(
            out=bandm_sb[:], in_=bandm.rearrange("m p c -> p m c")), incs=[(ld_w, 16)])
        p.emit("gpsimd", lambda g: g.dma_start(
            out=stridm_sb[:], in_=stridm.rearrange("m p c -> p m c")), incs=[(ld_w, 16)])
        p.emit("sync", lambda s_: s_.dma_start(out=eye8_sb[:], in_=eye8), incs=[(ld_c, 16)])
        p.emit("sync", lambda s_: s_.dma_start(out=eyeb_sb[:], in_=eyeb), incs=[(ld_c, 16)])
        p.emit("sync", lambda s_: s_.dma_start(
            out=b8[:], in_=bqkv[0:D].rearrange("(c p) -> c p", p=128)), incs=[(ld_c, 16)])
        W_ALL = ld_w.at()
        C_ALL = ld_c.at()
        p.emit("tensor", lambda t: t.matmul(P[7][0:128, 0:8], b8[:], eye8_sb[:], start=True, stop=True),
               waits=[W_ALL, C_ALL], incs=[(tns, 1)])
        p.emit("vector", lambda v: v.tensor_copy(qb_sb[:], P[7][0:128, 0:8]), waits=[tns.at()], incs=[(dve, 1)])
        BQ = dve.at()
        p.emit("sync", lambda s_: s_.dma_start(
            out=b8[:], in_=bqkv[D:2 * D].rearrange("(c p) -> c p", p=128)), waits=[BQ], incs=[(ld_c, 16)])
        p.emit("tensor", lambda t: t.matmul(P[7][0:128, 0:8], b8[:], eye8_sb[:], start=True, stop=True),
               waits=[ld_c.at(), BQ], incs=[(tns, 1)])
        p.emit("vector", lambda v: v.tensor_copy(kb_sb[:], P[7][0:128, 0:8]), waits=[tns.at()], incs=[(dve, 1)])
        BIAS_T = dve.at()

        PROL = emit_prologue_ln(p, S, x_in, xnbf, xt, xn_sb, ln_pre, eps, ldx, stn, dve, act)
        for c in range(NC_CHUNKS):
            p.emit("sync", (lambda c=c: lambda s: s.dma_start(
                out=xnT[:, c, :], in_=xnbf[:, c * 128:(c + 1) * 128], transpose=True))(),
                   waits=PROL, incs=[(tr, 16)])
        TR_ALL = tr.at()

        unit_done = []
        unit_idx = 0
        for proj, b_tile, outT in (("q", qb_sb, qT_h), ("k", kb_sb, kT_h)):
            woff = 0 if proj == "q" else D
            for oc in range(NC_CHUNKS):
                nstr_done = 0
                for nq in range(QC):
                    bank = unit_idx % 4
                    war = [unit_done[unit_idx - 4]] if unit_idx >= 4 else []
                    for ic in range(NC_CHUNKS):
                        p.emit("tensor", (lambda ic=ic, oc=oc, nq=nq, bank=bank, woff=woff: lambda t: t.matmul(
                            ps_qk[bank][:], w_sb[:, ic, woff + oc * 128:woff + (oc + 1) * 128],
                            xnT[:, ic, nq * 512:(nq + 1) * 512],
                            start=(ic == 0), stop=(ic == NC_CHUNKS - 1)))(),
                            waits=([TR_ALL, W_ALL, BIAS_T] + war) if ic == 0 else (),
                            incs=[(tns, 1)] if ic == NC_CHUNKS - 1 else ())
                    mm = tns.at()
                    ob_ = unit_idx % 2
                    war3 = [stqk[ob_].at()]
                    p.emit("vector", (lambda bank=bank, b_tile=b_tile, oc=oc, ob_=ob_: lambda v: v.tensor_scalar(
                        out=out_sb[ob_][:], in0=ps_qk[bank][:], scalar1=b_tile[:, oc:oc + 1], scalar2=None,
                        op0=ALU.add))(), waits=[mm] + war3, incs=[(dve, 1)])
                    if proj == "k":
                        kw_ = [stks[0].at()] if nq == 0 else []
                        p.emit("vector", (lambda bank=bank, b_tile=b_tile, oc=oc, nq=nq: lambda v: v.tensor_scalar(
                            out=kstr_sb[:, nq * 16:(nq + 1) * 16],
                            in0=ps_qk[bank][:].rearrange("p (a b) -> p a b", b=32)[:, :, 0],
                            scalar1=b_tile[:, oc:oc + 1], scalar2=None, op0=ALU.add))(), waits=kw_, incs=[(dve, 1)])
                    unit_done.append(dve.at())
                    p.emit("sync", (lambda outT=outT, oc=oc, nq=nq, ob_=ob_: lambda s: s.dma_start(
                        out=outT[oc * 128:(oc + 1) * 128, nq * 512:(nq + 1) * 512], in_=out_sb[ob_][:]))(),
                        waits=[unit_done[-1]], incs=[(stqk[ob_], 16)])
                    unit_idx += 1
                if proj == "k":
                    p.emit("sync", (lambda oc=oc: lambda s: s.dma_start(
                        out=kstr_h[oc * 128:(oc + 1) * 128, 0:NSTR], in_=kstr_sb[:, 0:NSTR]))(),
                        waits=[unit_done[-1]], incs=[(stks[0], 16)])

        v_units = []
        for t in range(TT):
            vbuf = t % 2
            for dh in range(2):
                uidx = t * 2 + dh
                bank = uidx % 4
                war = ([v_units[uidx - 4]] if uidx >= 4 else []) + ([unit_done[-4 + uidx]] if uidx < 4 else [])
                for ic in range(NC_CHUNKS):
                    p.emit("tensor", (lambda ic=ic, t=t, dh=dh, bank=bank: lambda tn: tn.matmul(
                        ps_v[bank][:], xnT[:, ic, t * 128:(t + 1) * 128],
                        w_sb[:, ic, 2 * D + dh * 512:2 * D + (dh + 1) * 512],
                        start=(ic == 0), stop=False))(),
                        waits=([TR_ALL, W_ALL] + war) if ic == 0 else ())
                p.emit("tensor", (lambda dh=dh, bank=bank: lambda tn: tn.matmul(
                    ps_v[bank][:], ones_bf[0:1, 0:128], vb_row[0:1, dh * 512:(dh + 1) * 512],
                    start=False, stop=True))(), incs=[(tns, 1)])
                mm = tns.at()
                war3 = [stv[t % 2].at()] if dh == 0 else []
                p.emit("vector", (lambda vbuf=vbuf, dh=dh, bank=bank: lambda v: v.tensor_copy(
                    vblk_sb[vbuf][:, dh * 8:(dh + 1) * 8, 0:64],
                    ps_v[bank][:].rearrange("p (h d) -> p h d", h=8)))(),
                    waits=[mm] + war3 + ([CONSTS] if t == 0 and dh == 0 else []), incs=[(dve, 1)])
                v_units.append(dve.at())
            p.emit("sync", (lambda t=t, vbuf=vbuf: lambda s: s.dma_start(
                out=vblk_h[t, :, :], in_=vblk_sb[vbuf][:].rearrange("p h c -> p (h c)")))(),
                waits=[v_units[-1]], incs=[(stv[vbuf], 16)])

        STEPA_ST = [stqk[0].at(), stqk[1].at(), stv[0].at(), stv[1].at(), stks[0].at()]
        V_MM_END = tns.at()
        p.emit("gpsimd", lambda g: g.dma_start(
            out=w_sb[:, :, 0:D],
            in_=wout.rearrange("(c p) d -> p c d", p=128)), waits=[V_MM_END], incs=[(ld_w, 16)])
        OW_LD = ld_w.at()

        hp_mm_end = [None] * 8
        o_written = []
        exp_bank_gate = {}
        scores_bank_gate = {}
        for hp in range(8):
            war = [hp_mm_end[hp - 1]] if hp >= 1 else []
            p.emit("sync", (lambda hp=hp: lambda s: s.dma_start(
                out=qT_sb[:], in_=qT_h[hp * 128:(hp + 1) * 128, :]))(),
                waits=STEPA_ST + war, incs=[(ldb[0], 16)])
            p.emit("sync", (lambda hp=hp: lambda s: s.dma_start(
                out=kT_sb[:], in_=kT_h[hp * 128:(hp + 1) * 128, :]))(), incs=[(ldb[0], 16)])
            p.emit("sync", (lambda hp=hp: lambda s: s.dma_start(
                out=kstrp[:, 0:NSTR], in_=kstr_h[hp * 128:(hp + 1) * 128, 0:NSTR]))(), incs=[(ldb[0], 16)])
            for head, vb_t in ((2 * hp, vbA_sb), (2 * hp + 1, vbB_sb)):
                p.emit("gpsimd", (lambda head=head, vb_t=vb_t: lambda g: g.dma_start(
                    out=vb_t[:], in_=vblk_h[:, :, head * 65:(head + 1) * 65].rearrange("t p c -> p t c")))(),
                    waits=STEPA_ST + war, incs=[(ldbg[0], 16)])
            for head, vs_t in ((2 * hp, vstrA), (2 * hp + 1, vstrB)):
                for rc in range(4):
                    p.emit("gpsimd", (lambda head=head, vs_t=vs_t, rc=rc: lambda g: g.dma_start(
                        out=vs_t[rc:NSTR:4, :], in_=vblk_h[:, rc * 32, head * 65:(head + 1) * 65]))(),
                        waits=[ldbg[0].at(), ld_w.at(), CONSTS], incs=[(ldbg[0], 16)])
            LDB = ldb[0].at()
            LDBG = ldbg[0].at()

            for qc in range(QC):
                acc_war = [o_written[-1]] if o_written else []
                kts = [kt for kt in range(max(0, 4 * qc - 1), min(NK, 4 * qc + 5))]
                tiles = [("band", kt) for kt in kts] + [("str", 0)]
                ntile = len(tiles)
                exp_at = {}
                tinfo = []
                for ti, (kind, kt) in enumerate(tiles):
                    if kind == "band":
                        midx = kt - (4 * qc - 1)
                        tinfo.append((kT_sb[0:64, kt * 128:(kt + 1) * 128],
                                      kT_sb[64:128, kt * 128:(kt + 1) * 128],
                                      bandm_sb[:, midx, :], 128,
                                      vbA_sb[:, kt, :], vbB_sb[:, kt, :]))
                    else:
                        tinfo.append((kstrp[0:64, 0:NSTR], kstrp[64:128, 0:NSTR],
                                      stridm_sb[:, qc, :], NSTR,
                                      vstrA[0:NSTR, :], vstrB[0:NSTR, :]))

                def emit_mm1(ti):
                    bank = ti % 2
                    lhsA, lhsB, maskap, rows, vA, vB = tinfo[ti]
                    w1 = [LDB, LDBG] + (acc_war if ti == 0 else [])
                    g = scores_bank_gate.get(bank)
                    if g is not None:
                        w1 = w1 + [g]
                    p.emit("tensor", (lambda lhsA=lhsA, qc=qc, bank=bank, rows=rows: lambda t: t.matmul(
                        ps_s[bank][0:rows, :], lhsA, qT_sb[0:64, qc * 512:(qc + 1) * 512],
                        start=True, stop=False))(), waits=w1)
                    p.emit("tensor", (lambda maskap=maskap, bank=bank, rows=rows: lambda t: t.matmul(
                        ps_s[bank][0:rows, :], eyeb_sb[:, 0:rows], maskap,
                        start=False, stop=True))())
                    p.emit("tensor", (lambda lhsB=lhsB, qc=qc, bank=bank, rows=rows: lambda t: t.matmul(
                        ps_s2[bank][0:rows, :], lhsB, qT_sb[64:128, qc * 512:(qc + 1) * 512],
                        start=True, stop=False, tile_position=(64, 0)))())
                    p.emit("tensor", (lambda maskap=maskap, bank=bank, rows=rows: lambda t: t.matmul(
                        ps_s2[bank][0:rows, :], eyeb_sb[:, 0:rows], maskap,
                        start=False, stop=True))(), incs=[(tns, 1)])
                    mm1 = tns.at()
                    ew = exp_bank_gate.get(bank)
                    p.emit("scalar", (lambda bank=bank, rows=rows: lambda s: s.activation(
                        out=Eab[bank][0:rows, :], in_=ps_sf[bank][0:rows, :], func=AF.Exp, scale=0.125,
                        bias=nC[0:rows, :]))(), waits=[mm1] + ([ew] if ew else []), incs=[(act, 1)])
                    exp_at[ti] = act.at()
                    scores_bank_gate[bank] = exp_at[ti]

                def emit_mm2(ti):
                    bank = ti % 2
                    lhsA, lhsB, maskap, rows, vA, vB = tinfo[ti]
                    p.emit("tensor", (lambda vA=vA, bank=bank, ti=ti, rows=rows: lambda t: t.matmul(
                        ps_oA[0:65, :], vA[0:rows, :], Eab[bank][0:rows, 0:512],
                        start=(ti == 0), stop=(ti == ntile - 1)))(), waits=[exp_at[ti]])
                    p.emit("tensor", (lambda vB=vB, bank=bank, ti=ti, rows=rows: lambda t: t.matmul(
                        ps_oB[0:65, :], vB[0:rows, :], Eab[bank][0:rows, 512:1024],
                        start=(ti == 0), stop=(ti == ntile - 1)))(), incs=[(tns, 1)])
                    exp_bank_gate[bank] = tns.at()

                emit_mm1(0)
                for ti in range(1, ntile):
                    emit_mm1(ti)
                    emit_mm2(ti - 1)
                emit_mm2(ntile - 1)
                MM2_ALL = tns.at()
                p.emit("vector", lambda v: v.tensor_copy(rs4[:], ps_oA[64:65, :]), waits=[MM2_ALL], incs=[(dve, 1)])
                p.emit("vector", lambda v: v.tensor_copy(rs4b[:], ps_oB[64:65, :]), incs=[(dve, 1)])
                p.emit("vector", lambda v: v.reciprocal(recA[:], rs4[:]), waits=[dve.at()], incs=[(dve, 1)])
                p.emit("vector", lambda v: v.reciprocal(recB[:], rs4b[:]), incs=[(dve, 1)])
                p.emit("vector", lambda v: v.tensor_copy(rhiA[:], recA[:]), waits=[dve.at()], incs=[(dve, 1)])
                p.emit("vector", lambda v: v.tensor_copy(rhiB[:], recB[:]), incs=[(dve, 1)])
                p.emit("vector", lambda v: v.tensor_tensor(rloA[:], recA[:], rhiA[:], ALU.subtract),
                       waits=[dve.at()], incs=[(dve, 1)])
                p.emit("vector", lambda v: v.tensor_tensor(rloB[:], recB[:], rhiB[:], ALU.subtract),
                       incs=[(dve, 1)])
                RL = dve.at()
                p.emit("tensor", lambda t: t.matmul(ps_r[0:64, :], ones_bf[0:1, 0:64], rhiA[:], start=True, stop=False),
                       waits=[RL])
                p.emit("tensor", lambda t: t.matmul(ps_r[0:64, :], ones_bf[0:1, 0:64], rloA[:], start=False, stop=True))
                p.emit("tensor", lambda t: t.matmul(ps_r[64:128, :], ones_bf[0:1, 0:64], rhiB[:], start=True, stop=False,
                                                    tile_position=(0, 64)))
                p.emit("tensor", lambda t: t.matmul(ps_r[64:128, :], ones_bf[0:1, 0:64], rloB[:], start=False, stop=True,
                                                    tile_position=(0, 64)), incs=[(tns, 1)])
                p.emit("scalar", lambda s: s.copy(out=rbc[:], in_=ps_r[:]), waits=[tns.at()], incs=[(act, 1)])
                RBC = act.at()
                p.emit("vector", (lambda hp=hp, qc=qc: lambda v: v.tensor_tensor(
                    o_buf[0:64, hp, qc * 512:(qc + 1) * 512], ps_oA[0:64, :], rbc[0:64, :], ALU.mult))(),
                    waits=[RBC], incs=[(dve, 1)])
                p.emit("vector", (lambda hp=hp, qc=qc: lambda v: v.tensor_tensor(
                    o_buf[64:128, hp, qc * 512:(qc + 1) * 512], ps_oB[0:64, :], rbc[64:128, :], ALU.mult))(),
                    incs=[(dve, 1)])
                o_written.append(dve.at())
            hp_mm_end[hp] = (tns, tns.n)

        O_ALL = o_written[-1]

        epi_done = []
        for t in range(TT):
            b = t % 2
            war = [epi_done[t - 2]] if t >= 2 else []
            mm_at = []
            for dh in range(2):
                bank = (t * 2 + dh) % 4
                for ic in range(NC_CHUNKS):
                    p.emit("tensor", (lambda ic=ic, t=t, dh=dh, bank=bank: lambda tn: tn.matmul(
                        ps_y[bank][:], o_buf[:, ic, t * 128:(t + 1) * 128],
                        w_sb[:, ic, dh * 512:(dh + 1) * 512], start=(ic == 0), stop=False))(),
                        waits=([O_ALL, OW_LD] + war) if ic == 0 else ())
                p.emit("tensor", (lambda dh=dh, bank=bank: lambda tn: tn.matmul(
                    ps_y[bank][:], ones_bf[0:1, 0:128], ob_row[0:1, dh * 512:(dh + 1) * 512],
                    start=False, stop=True))(), incs=[(tns, 1)])
                mm_at.append(tns.at())
            done = emit_postln_residual(p, t, b, [ps_y[(t * 2) % 4], ps_y[(t * 2 + 1) % 4]],
                                        x_in, x_out, xi2, xo_sb, ln_post, eps, ldx2, stx,
                                        dve, act, mm_at[1], war)
            epi_done.append(done)

        return [stx[0].at(), stx[1].at()]


def phase3_masks(S):
    QC = S // 512
    NEG = np.float32(-1e30)
    bandm = np.zeros((6, 128, 512), np.float32)
    for idx in range(6):
        m = 1 - idx
        r = np.arange(128)[:, None]
        c = np.arange(512)[None, :]
        diff = c - r + 128 * m
        bandm[idx] = np.where(np.abs(diff) <= 64, 0.0, NEG)
    stridm = np.zeros((QC, 128, 512), np.float32)
    for qc in range(QC):
        r = np.arange(128)[:, None]
        c = np.arange(512)[None, :]
        i = qc * 512 + c
        diff = i - 32 * r
        keep = ~(np.abs(diff) <= 64)
        if S // 32 < 128:
            keep = keep & (r < S // 32)
        stridm[qc] = np.where(keep, 0.0, NEG)
    bf = lambda a: a.astype(ml_dtypes.bfloat16)
    return bf(bandm), bf(stridm)



def emit_phase4(nc, p, stack, P, tns, act, dve, S, x_in, x_out, W):
    TT = S // 128
    NSLAB = S // 512
    NW = S // 512
    HID = 2730
    NH2 = 22
    INNER = 512
    wqkv, qscale, kscale, wo, lng, lnb = (W["wqkv"], W["q_scale"], W["k_scale"],
                                          W["wo"], W["ln_g"], W["ln_b"])
    w1, b1, w2, b2, eyeb = W["w1"], W["b1"], W["w2"], W["b2"], W["eyeb"]
    tqc, tqs, tkc, tks = W["tqc"], W["tqs"], W["tkc"], W["tks"]
    mask4 = W["mask4"]

    x4 = nc.dram_tensor("p4_x4p", [S + 128, D], F32, kind="Internal").ap()
    x4bf = nc.dram_tensor("p4_x4bfp", [S + 128, D], BF16, kind="Internal").ap()
    qn_h = nc.dram_tensor("p4_qn_h", [S, INNER], BF16, kind="Internal").ap()
    kn_h = nc.dram_tensor("p4_kn_h", [S, INNER], BF16, kind="Internal").ap()
    qT_h = nc.dram_tensor("p4_qT_h", [INNER, S], BF16, kind="Internal").ap()
    kT_h = nc.dram_tensor("p4_kT_h", [INNER, S], BF16, kind="Internal").ap()
    o4T_h = nc.dram_tensor("p4_o4T_h", [INNER, S], BF16, kind="Internal").ap()
    w1bfA = [nc.dram_tensor(f"p4_w1bfA{l}", [D, 2816], BF16, kind="Internal").ap() for l in range(2)]
    w1bfB = [nc.dram_tensor(f"p4_w1bfB{l}", [D, 2816], BF16, kind="Internal").ap() for l in range(2)]
    w2bf = [nc.dram_tensor(f"p4_w2bf{l}", [HID, D], BF16, kind="Internal").ap() for l in range(2)]
    OFF = 128

    if True:
        sb = lambda n, shp, d: p.AR.alloc(shp, d)

        eps = sb("eps", [128, 1], F32)
        ones_bf = sb("ones_bf", [1, 512], BF16)
        ones_f = sb("ones_f", [1, 128], F32)

        xt = [sb(f"xt{i}", [128, D], F32) for i in range(2)]
        xn_sb = [sb(f"xn{i}", [128, D], BF16) for i in range(2)]
        ln_pre = LNState(p)
        ln_post = LNState(p)
        xo_sb = [sb(f"xo{i}", [128, D], F32) for i in range(2)]

        tq_c = sb("tq_c", [128, TT, 64], F32)
        tq_s = sb("tq_s", [128, TT, 64], F32)
        tk_c = sb("tk_c", [128, TT, 64], F32)
        tk_s = sb("tk_s", [128, TT, 64], F32)
        qs_bc = sb("qs_bc", [128, 64], F32)
        ks_bc = sb("ks_bc", [128, 64], F32)
        g_bc = sb("g_bc", [128, D], F32)
        b_bc = sb("b_bc", [128, D], F32)
        sc_row = sb("sc_row", [1, 64], F32)
        gb_row = sb("gb_row", [1, 512], F32)

        b2_row = sb("b2_row", [1, D], BF16)
        mask_sb = sb("mask_sb", [128, 8, 512], BF16)
        eyeb_sb = sb("eyeb_sb", [128, 128], BF16)

        AR = max(20480 + 2 * S + TT * 520 + 2 * 512, 13312 + NH2 * 512 + 2048)
        arena = sb("arena", [128, AR], BF16)
        wq_sb = arena[:, 0:12288].rearrange("p (c d) -> p c d", d=3 * INNER)
        wo_sb = arena[:, 12288:16384].rearrange("p (c d) -> p c d", d=D)
        hT_slab = arena[:, 16384:20480].rearrange("p (c n) -> p c n", n=512)
        qT_sb = arena[:, 20480:20480 + S]
        kT_sb = arena[:, 20480 + S:20480 + 2 * S]
        _vb0 = 20480 + 2 * S
        vblk = arena[:, _vb0:_vb0 + TT * 520].rearrange("p (t h c) -> p t h c", h=8, c=65)
        o_stage = [arena[:, _vb0 + TT * 520 + i * 512:_vb0 + TT * 520 + (i + 1) * 512] for i in range(2)]
        w1pa = [arena[:, i * 2048:(i + 1) * 2048].rearrange("p (c d) -> p c d", d=256) for i in range(2)]
        w1pg = [arena[:, 4096 + i * 2048:4096 + (i + 1) * 2048].rearrange("p (c d) -> p c d", d=256) for i in range(2)]
        b1p = [arena[0:1, 8192 + i * 512:8192 + (i + 1) * 512] for i in range(2)]
        hlnT_slab = arena[:, 9216:9216 + 4096].rearrange("p (c n) -> p c n", n=512)
        h2T_slab = arena[:, 13312:13312 + NH2 * 512].rearrange("p (c n) -> p c n", n=512)
        w2p = [arena[:, 24576 + i * 1024:24576 + (i + 1) * 1024] for i in range(2)]
        oT_t = [sb(f"oT_t{i}", [128, 4, 128], BF16) for i in range(2)]

        ss8 = sb("ss8", [128, 8], F32)
        rq8 = sb("rq8", [128, 8], F32)
        t0 = sb("t0", [128, 512], F32)
        t1 = sb("t1", [128, 512], F32)
        t2 = sb("t2", [128, 512], F32)
        qn_sb2 = [sb(f"qn_sb{i}", [128, 512], BF16) for i in range(2)]
        hln_sb = [sb(f"hln{i}", [128, D], BF16) for i in range(2)]
        gact = sb("gact", [128, 512], BF16)
        Eab = [sb(f"Eab{i}", [128, 1024], BF16) for i in range(2)]
        cmx = sb("cmx", [128, 16], F32)
        cm1 = sb("cm1", [128, 1], F32)
        cm1b = sb("cm1b", [128, 1], BF16)
        negC = sb("negC", [128, 1], F32)
        rs4b = sb("rs4b", [1, 512], F32)
        recA = sb("recA", [1, 512], F32)
        recB = sb("recB", [1, 512], F32)
        rhiA = sb("rhiA", [1, 512], BF16)
        rloA = sb("rloA", [1, 512], BF16)
        rhiB = sb("rhiB", [1, 512], BF16)
        rloB = sb("rloB", [1, 512], BF16)
        rbc = sb("rbc", [128, 512], F32)
        rs4 = rbc[0:1, :]
        y4 = [sb(f"y4_{i}", [128, D], F32) for i in range(2)]

        w1stg = [sb(f"w1stg{i}", [128, 256], BF16) for i in range(2)]
        pw_ld = DmaSlots(p, "pw_ld", 2)
        pw_st = DmaSlots(p, "pw_st", 2)

        ld_w = p.sem("ld_w")
        ld_c = p.sem("ld_c")
        ldx = DmaSlots(p, "ldx", 2)
        stn = DmaSlots(p, "stn", 2)
        trh = DmaSlots(p, "trh", 1)
        stqn = DmaSlots(p, "stqn", 2)
        trqk = DmaSlots(p, "trqk", 1)
        ldb = DmaSlots(p, "ldb", 1)
        ldw1 = DmaSlots(p, "ldw1", 2)
        ldw2 = DmaSlots(p, "ldw2", 2)
        ldx2 = DmaSlots(p, "ldx2", 2)
        ldx3 = DmaSlots(p, "ldx3", 2)
        stx = DmaSlots(p, "stx", 2)
        ldms = DmaSlots(p, "ldms", 1)
        ldwq = DmaSlots(p, "ldwq", 1)
        ldb2s = DmaSlots(p, "ldb2s", 1)

        p.emit("vector", lambda v: v.memset(eps[:], 1e-5))
        p.emit("vector", lambda v: v.memset(ones_bf[:], 1.0))
        p.emit("vector", lambda v: v.memset(ones_f[:], 1.0))
        p.emit("vector", lambda v: v.memset(xo_sb[0][:], 0.0))
        p.emit("vector", lambda v: v.memset(xn_sb[0][:], 0.0))
        p.emit("vector", lambda v: v.memset(cmx[:], -3e38), incs=[(dve, 1)])
        CONSTS = dve.at()
        for tbl, dst in ((tqc, tq_c), (tqs, tq_s), (tkc, tk_c), (tks, tk_s)):
            p.emit("sync", (lambda tbl=tbl, dst=dst: lambda s: s.dma_start(
                out=dst[:], in_=tbl.rearrange("(tt p) d -> p tt d", p=128)))(), incs=[(ld_c, 16)])
        p.emit("sync", lambda s: s.dma_start(out=eyeb_sb[:], in_=eyeb), incs=[(ld_c, 16)])
        p.emit("gpsimd", lambda g: g.dma_start(out=mask_sb[:], in_=mask4.rearrange("m p c -> p m c")),
               incs=[(ldms[0], 16)])
        p.emit("sync", lambda s: s.dma_start(out=x4[0:128, :], in_=xo_sb[0][:]), waits=[CONSTS], incs=[(stn[0], 16)])
        p.emit("sync", lambda s: s.dma_start(out=x4bf[0:128, :], in_=xn_sb[0][:]), incs=[(stn[0], 16)])

        prol_done = []
        for t in range(TT):
            b = t % 2
            war = [prol_done[t - 2]] if t >= 2 else []
            p.emit("sync", (lambda t=t, b=b: lambda s: s.dma_start(out=xt[b][:], in_=x_in[t * 128:(t + 1) * 128, :]))(),
                   waits=war + [stn[b].at()], incs=[(ldx[b], 16)])
            rstd_at, aggr_at = emit_ln_stats(p, ln_pre, b, [xt[b][:, 0:512], xt[b][:, 512:1024]],
                                             eps, dve, act, waits=[ldx[b].at()])
            war2 = [stn[b].at()]
            p.emit("vector", (lambda b=b: lambda v: v.tensor_scalar(
                out=xt[b][:], in0=xt[b][:], scalar1=ln_pre.mv[b][:, 0:1], scalar2=ln_pre.rstd[b][:],
                op0=ALU.subtract, op1=ALU.mult))(), waits=[rstd_at, aggr_at] + war2, incs=[(dve, 1)])
            p.emit("vector", (lambda b=b: lambda v: v.tensor_copy(xn_sb[b][:], xt[b][:]))(),
                   waits=[dve.at()], incs=[(dve, 1)])
            prol_done.append(dve.at())
            p.emit("sync", (lambda t=t, b=b: lambda s: s.dma_start(
                out=x4[OFF + t * 128:OFF + (t + 1) * 128, :], in_=xt[b][:]))(),
                waits=[prol_done[-1]], incs=[(stn[b], 16)])
            p.emit("sync", (lambda t=t, b=b: lambda s: s.dma_start(
                out=x4bf[OFF + t * 128:OFF + (t + 1) * 128, :], in_=xn_sb[b][:]))(), incs=[(stn[b], 16)])
        X4_ST = [stn[0].at(), stn[1].at()]

        ldbg = DmaSlots(p, "ldbg", 1)
        sto4 = DmaSlots(p, "sto4", 2)
        ldo4 = DmaSlots(p, "ldo4", 2)

        LAYER_GATE = None
        for l in range(2):
            X4_CUR = [stn[0].at(), stn[1].at()]
            lg = [LAYER_GATE] if LAYER_GATE else []
            p.emit("gpsimd", (lambda l=l: lambda g: g.dma_start(
                out=wq_sb[:], in_=wqkv[l].rearrange("(c p) d -> p c d", p=128)))(), waits=lg, incs=[(ldwq[0], 16)])
            p.emit("gpsimd", (lambda l=l: lambda g: g.dma_start(
                out=wo_sb[:], in_=wo[l].rearrange("(c p) d -> p c d", p=128)))(), incs=[(ldwq[0], 16)])
            WQKV = ldwq[0].at()
            if l == 0:
                PIECES = [(k * 256, 256, 256) for k in range(10)] + [(2560, 170, 256)]
                pj = 0
                prep_jobs = []
                for l2 in range(2):
                    for r in range(8):
                        for (a0, aw, dw) in PIECES:
                            prep_jobs.append((w1[l2][r * 128:(r + 1) * 128, a0:a0 + aw],
                                              w1bfA[l2][r * 128:(r + 1) * 128, a0:a0 + dw], 128))
                            prep_jobs.append((w1[l2][r * 128:(r + 1) * 128, HID + a0:HID + a0 + aw],
                                              w1bfB[l2][r * 128:(r + 1) * 128, a0:a0 + dw], 128))
                for l2 in range(2):
                    for i2 in range(NH2):
                        wi2 = 128 if i2 < 21 else 42
                        for qh in range(4):
                            prep_jobs.append((w2[l2][i2 * 128:i2 * 128 + wi2, qh * 256:(qh + 1) * 256],
                                              w2bf[l2][i2 * 128:i2 * 128 + wi2, qh * 256:(qh + 1) * 256], wi2))
                for (src_ap, dst_ap, np_) in prep_jobs:
                    s_ = pj % 2
                    p.emit("gpsimd", (lambda src_ap=src_ap, np_=np_, s_=s_: lambda g: g.dma_start(
                        out=w1stg[s_][0:np_, 0:src_ap.shape[1]], in_=src_ap))(),
                        waits=[pw_st[s_].at()], incs=[(pw_ld[s_], 16)])
                    p.emit("gpsimd", (lambda dst_ap=dst_ap, np_=np_, s_=s_: lambda g: g.dma_start(
                        out=dst_ap, in_=w1stg[s_][0:np_, 0:dst_ap.shape[1]]))(),
                        waits=[pw_ld[s_].at()], incs=[(pw_st[s_], 16)])
                    pj += 1
                PREP_W1 = [pw_st[0].at(), pw_st[1].at()]
            p.emit("sync", (lambda l=l: lambda s: s.dma_start(
                out=sc_row[:], in_=qscale[l].rearrange("(u d) -> u d", u=1)))(), waits=lg, incs=[(ld_c, 16)])
            SC1 = ld_c.at()
            p.emit("tensor", lambda t: t.matmul(P[7][:, 0:64], ones_f[0:1, 0:128], sc_row[:],
                                                start=True, stop=True), waits=[SC1, CONSTS], incs=[(tns, 1)])
            p.emit("vector", lambda v: v.tensor_copy(qs_bc[:], P[7][:, 0:64]), waits=[tns.at()], incs=[(dve, 1)])
            QSB = dve.at()
            p.emit("sync", (lambda l=l: lambda s: s.dma_start(
                out=sc_row[:], in_=kscale[l].rearrange("(u d) -> u d", u=1)))(), waits=[QSB], incs=[(ld_c, 16)])
            p.emit("tensor", lambda t: t.matmul(P[7][:, 0:64], ones_f[0:1, 0:128], sc_row[:],
                                                start=True, stop=True), waits=[ld_c.at(), QSB], incs=[(tns, 1)])
            p.emit("vector", lambda v: v.tensor_copy(ks_bc[:], P[7][:, 0:64]), waits=[tns.at()], incs=[(dve, 1)])
            KSB = dve.at()

            vblk_written = []
            qn_store = []
            for sl in range(NSLAB):
                war_t = [qn_store[-1]] if sl >= 1 else []
                for c in range(NC_CHUNKS):
                    off = OFF if c < 4 else OFF - 1
                    p.emit("sync", (lambda sl=sl, c=c, off=off: lambda s: s.dma_start(
                        out=hT_slab[:, c, :],
                        in_=x4bf[off + sl * 512:off + (sl + 1) * 512, c * 128:(c + 1) * 128], transpose=True))(),
                        waits=(X4_CUR + war_t + [WQKV]) if c == 0 else (), incs=[(trh[0], 16)])
                TRH = trh[0].at()
                for t4 in range(4):
                    t = sl * 4 + t4
                    for oc in range(3):
                        for ic in range(NC_CHUNKS):
                            p.emit("tensor", (lambda ic=ic, oc=oc, t4=t4: lambda tn: tn.matmul(
                                P[oc][:], hT_slab[:, ic, t4 * 128:(t4 + 1) * 128],
                                wq_sb[:, ic, oc * 512:(oc + 1) * 512],
                                start=(ic == 0), stop=(ic == NC_CHUNKS - 1)))(),
                                waits=([TRH, WQKV] + ([vblk_written[-1]] if vblk_written else []) +
                                       ([QSB, KSB] if t == 0 else [])) if (oc == 0 and ic == 0) else (),
                                incs=[(tns, 1)] if ic == NC_CHUNKS - 1 else ())
                    MMQKV = tns.at()
                    for which, bank, tbl_c, tbl_s, s_bc, dst_h in (
                            ("q", 0, tq_c, tq_s, qs_bc, qn_h), ("k", 1, tk_c, tk_s, ks_bc, kn_h)):
                        p.emit("scalar", (lambda bank=bank: lambda s: s.activation(
                            out=t1[:], in_=P[bank][:], func=AF.Square))(),
                            waits=[MMQKV, dve.at()], incs=[(act, 1)])
                        p.emit("vector", (lambda: lambda v: v.reduce_sum(
                            ss8[:], t1[:].rearrange("p (h d) -> p h d", h=8), axis=AX.X))(),
                            waits=[act.at()], incs=[(dve, 1)])
                        p.emit("scalar", lambda s: s.activation(out=ss8[:], in_=ss8[:], func=AF.Sqrt),
                               waits=[dve.at()], incs=[(act, 1)])
                        p.emit("vector", lambda v: v.tensor_scalar(
                            out=ss8[:], in0=ss8[:], scalar1=1e-12, scalar2=None, op0=ALU.max),
                            waits=[act.at()], incs=[(dve, 1)])
                        p.emit("vector", lambda v: v.reciprocal(rq8[:], ss8[:]), waits=[dve.at()], incs=[(dve, 1)])
                        RQ = dve.at()
                        p.emit("vector", (lambda bank=bank: lambda v: v.tensor_tensor(
                            t0[:].rearrange("p (h d) -> p h d", h=8),
                            P[bank][:].rearrange("p (h d) -> p h d", h=8),
                            rq8[:].rearrange("p (h u) -> p h u", u=1).broadcast_to((128, 8, 64)), ALU.mult))(),
                            waits=[RQ], incs=[(dve, 1)])
                        p.emit("vector", (lambda s_bc=s_bc: lambda v: v.tensor_tensor(
                            t0[:].rearrange("p (h d) -> p h d", h=8),
                            t0[:].rearrange("p (h d) -> p h d", h=8),
                            s_bc[:].rearrange("p (u d) -> p u d", u=1).broadcast_to((128, 8, 64)), ALU.mult))(),
                            waits=[dve.at()], incs=[(dve, 1)])
                        p.emit("vector", (lambda tbl_c=tbl_c, t=t: lambda v: v.tensor_tensor(
                            t1[:].rearrange("p (h d) -> p h d", h=8),
                            t0[:].rearrange("p (h d) -> p h d", h=8),
                            tbl_c[:, t, :].rearrange("p (u d) -> p u d", u=1).broadcast_to((128, 8, 64)), ALU.mult))(),
                            waits=[dve.at()], incs=[(dve, 1)])
                        p.emit("vector", (lambda tbl_s=tbl_s, t=t: lambda v: v.tensor_tensor(
                            t2[:].rearrange("p (h d) -> p h d", h=8)[:, :, 0:32],
                            t0[:].rearrange("p (h d) -> p h d", h=8)[:, :, 32:64],
                            tbl_s[:, t, 0:32].rearrange("p (u d) -> p u d", u=1).broadcast_to((128, 8, 32)), ALU.mult))(),
                            waits=[dve.at()], incs=[(dve, 1)])
                        p.emit("vector", (lambda tbl_s=tbl_s, t=t: lambda v: v.tensor_tensor(
                            t2[:].rearrange("p (h d) -> p h d", h=8)[:, :, 32:64],
                            t0[:].rearrange("p (h d) -> p h d", h=8)[:, :, 0:32],
                            tbl_s[:, t, 32:64].rearrange("p (u d) -> p u d", u=1).broadcast_to((128, 8, 32)), ALU.mult))(),
                            incs=[(dve, 1)])
                        TROT = dve.at()
                        war_qn = [stqn[t % 2].at()]
                        qb_ = t % 2
                        p.emit("vector", (lambda qb_=qb_: lambda v: v.tensor_tensor(
                            qn_sb2[qb_][:].rearrange("p (h d) -> p h d", h=8)[:, :, 0:32],
                            t1[:].rearrange("p (h d) -> p h d", h=8)[:, :, 0:32],
                            t2[:].rearrange("p (h d) -> p h d", h=8)[:, :, 0:32], ALU.subtract))(),
                            waits=[TROT] + war_qn)
                        p.emit("vector", (lambda qb_=qb_: lambda v: v.tensor_tensor(
                            qn_sb2[qb_][:].rearrange("p (h d) -> p h d", h=8)[:, :, 32:64],
                            t1[:].rearrange("p (h d) -> p h d", h=8)[:, :, 32:64],
                            t2[:].rearrange("p (h d) -> p h d", h=8)[:, :, 32:64], ALU.add))(),
                            incs=[(dve, 1)])
                        QN = dve.at()
                        p.emit("sync", (lambda dst_h=dst_h, t=t, qb_=qb_: lambda s: s.dma_start(
                            out=dst_h[t * 128:(t + 1) * 128, :], in_=qn_sb2[qb_][:]))(),
                            waits=[QN], incs=[(stqn[t % 2], 16)])
                    p.emit("vector", (lambda t=t: lambda v: v.memset(vblk[:, t, :, 64:65], 1.0))(),
                           waits=[MMQKV])
                    p.emit("vector", (lambda t=t: lambda v: v.tensor_copy(
                        vblk[:, t, :, 0:64], P[2][:].rearrange("p (h d) -> p h d", h=8)))(),
                        incs=[(dve, 1)])
                    vblk_written.append(dve.at())
                qn_store.append((tns, tns.n))
            QKV_ALL_ST = [stqn[0].at(), stqn[1].at()]
            VBLK_ALL = vblk_written[-1]

            for src_h, dst_h in ((qn_h, qT_h), (kn_h, kT_h)):
                for c in range(4):
                    for sl2 in range(NSLAB):
                        p.emit("sync", (lambda src_h=src_h, c=c, sl2=sl2: lambda s: s.dma_start(
                            out=qn_sb2[0][:], in_=src_h[sl2 * 512:(sl2 + 1) * 512, c * 128:(c + 1) * 128],
                            transpose=True))(),
                            waits=QKV_ALL_ST + [trqk[0].at()], incs=[(trqk[0], 16)])
                        p.emit("sync", (lambda dst_h=dst_h, c=c, sl2=sl2: lambda s: s.dma_start(
                            out=dst_h[c * 128:(c + 1) * 128, sl2 * 512:(sl2 + 1) * 512], in_=qn_sb2[0][:]))(),
                            waits=[trqk[0].at()], incs=[(trqk[0], 16)])
            TRQK = trqk[0].at()

            o_written = []
            hp_mm_end = [None] * 4
            exp_bank_gate = {}
            scores_bank_gate = {}
            prev_rbc = None
            for hp in range(4):
                war = [hp_mm_end[hp - 1]] if hp >= 1 else []
                p.emit("sync", (lambda hp=hp: lambda s: s.dma_start(
                    out=qT_sb, in_=qT_h[hp * 128:(hp + 1) * 128, :]))(),
                    waits=[TRQK] + war, incs=[(ldb[0], 16)])
                p.emit("sync", (lambda hp=hp: lambda s: s.dma_start(
                    out=kT_sb, in_=kT_h[hp * 128:(hp + 1) * 128, :]))(), incs=[(ldb[0], 16)])
                LDB = ldb[0].at()
                for w in range(NW):
                    acc_war = [o_written[-1]] if o_written else []
                    kts = [kt for kt in range(max(0, 4 * (w - 1)), 4 * w + 4)]
                    ntile = len(kts)
                    for ti, kt in enumerate(kts):
                        bank = ti % 2
                        w1_ = [LDB, VBLK_ALL, ldms[0].at()] + (acc_war if ti == 0 else [])
                        g = scores_bank_gate.get(("p1", bank))
                        if g is not None:
                            w1_ = w1_ + [g]
                        g2 = scores_bank_gate.get(bank)
                        if g2 is not None:
                            w1_ = w1_ + [g2]
                        p.emit("tensor", (lambda kt=kt, w=w, bank=bank: lambda t: t.matmul(
                            P[bank][:], kT_sb[0:64, kt * 128:(kt + 1) * 128],
                            qT_sb[0:64, w * 512:(w + 1) * 512], start=True, stop=True))(), waits=w1_)
                        p.emit("tensor", (lambda kt=kt, w=w, bank=bank: lambda t: t.matmul(
                            P[2 + bank][:], kT_sb[64:128, kt * 128:(kt + 1) * 128],
                            qT_sb[64:128, w * 512:(w + 1) * 512], start=True, stop=True,
                            tile_position=(64, 0)))(), incs=[(tns, 1)])
                        mm1 = tns.at()
                        p.emit("vector", (lambda bank=bank, ti=ti: lambda v: v.reduce_max(
                            cmx[:, ti:ti + 1], P[bank][:].rearrange("p (a b) -> p a b", b=4)[:, :, 0], axis=AX.X))(),
                            waits=[mm1], incs=[(dve, 1)])
                        p.emit("vector", (lambda bank=bank, ti=ti: lambda v: v.reduce_max(
                            cmx[:, 8 + ti:9 + ti], P[2 + bank][:].rearrange("p (a b) -> p a b", b=4)[:, :, 0], axis=AX.X))(),
                            incs=[(dve, 1)])
                        scores_bank_gate[("p1", bank)] = dve.at()
                    MM1_ALL = tns.at()
                    CMX = dve.at()
                    p.emit("vector", (lambda ntile=ntile: lambda v: v.reduce_max(
                        cm1[:], cmx[:, 0:8 + ntile] if False else cmx[:, 0:16], axis=AX.X))(),
                        waits=[CMX], incs=[(dve, 1)])
                    p.emit("vector", lambda v: v.tensor_copy(cm1b[:], cm1[:]), waits=[dve.at()], incs=[(dve, 1)])
                    p.emit("tensor", lambda t: t.matmul(P[6][0:1, 0:128], cm1b[:], eyeb_sb[:],
                                                        start=True, stop=True),
                           waits=[dve.at()] + ([prev_rbc] if prev_rbc else []), incs=[(tns, 1)])
                    p.emit("vector", lambda v: v.reduce_max(negC[0:1, :], P[6][0:1, 0:128], axis=AX.X),
                           waits=[tns.at()], incs=[(dve, 1)])
                    p.emit("vector", lambda v: v.tensor_copy(cm1b[0:1, :], negC[0:1, :]),
                           waits=[dve.at()], incs=[(dve, 1)])
                    p.emit("tensor", lambda t: t.matmul(P[6][0:128, 0:1], ones_bf[0:1, 0:128], cm1b[0:1, 0:1],
                                                        start=True, stop=True), waits=[dve.at()], incs=[(tns, 1)])
                    p.emit("vector", lambda v: v.tensor_scalar(
                        out=negC[:], in0=P[6][0:128, 0:1], scalar1=-8.0, scalar2=None, op0=ALU.mult),
                        waits=[tns.at()], incs=[(dve, 1)])
                    NEGC = dve.at()
                    exp_at = {}

                    def emit_mm1b(ti):
                        kt = kts[ti]
                        bank = ti % 2
                        rel = kt - 4 * (w - 1)
                        w2_ = []
                        g = scores_bank_gate.get(bank)
                        if g is not None:
                            w2_ = w2_ + [g]
                        if ti in (0, 1):
                            w2_ = w2_ + [scores_bank_gate[("p1", (ntile - 1) % 2)], CMX, NEGC]
                        p.emit("tensor", (lambda kt=kt, w=w, bank=bank: lambda t: t.matmul(
                            P[2 * bank][:], kT_sb[0:64, kt * 128:(kt + 1) * 128],
                            qT_sb[0:64, w * 512:(w + 1) * 512], start=True, stop=False))(), waits=w2_)
                        p.emit("tensor", (lambda rel=rel, bank=bank: lambda t: t.matmul(
                            P[2 * bank][:], eyeb_sb[:], mask_sb[:, rel, :],
                            start=False, stop=True))())
                        p.emit("tensor", (lambda kt=kt, w=w, bank=bank: lambda t: t.matmul(
                            P[2 * bank + 1][:], kT_sb[64:128, kt * 128:(kt + 1) * 128],
                            qT_sb[64:128, w * 512:(w + 1) * 512], start=True, stop=False,
                            tile_position=(64, 0)))())
                        p.emit("tensor", (lambda rel=rel, bank=bank: lambda t: t.matmul(
                            P[2 * bank + 1][:], eyeb_sb[:], mask_sb[:, rel, :],
                            start=False, stop=True))(), incs=[(tns, 1)])
                        mm1b = tns.at()
                        ew = exp_bank_gate.get(bank)
                        p.emit("scalar", (lambda bank=bank: lambda s: s.activation(
                            out=Eab[bank][:], in_=p.PQ[bank][:, 0:1024], func=AF.Exp, scale=8.0, bias=negC[:]))(),
                            waits=[NEGC, mm1b] + ([ew] if ew else []), incs=[(act, 1)])
                        exp_at[ti] = act.at()
                        scores_bank_gate[bank] = exp_at[ti]

                    def emit_mm2b(ti):
                        kt = kts[ti]
                        bank = ti % 2
                        p.emit("tensor", (lambda kt=kt, hp=hp, bank=bank, ti=ti: lambda t: t.matmul(
                            P[4][0:65, :], vblk[:, kt, 2 * hp, :], Eab[bank][:, 0:512],
                            start=(ti == 0), stop=(ti == ntile - 1)))(), waits=[exp_at[ti]])
                        p.emit("tensor", (lambda kt=kt, hp=hp, bank=bank, ti=ti: lambda t: t.matmul(
                            P[5][0:65, :], vblk[:, kt, 2 * hp + 1, :], Eab[bank][:, 512:1024],
                            start=(ti == 0), stop=(ti == ntile - 1)))(), incs=[(tns, 1)])
                        exp_bank_gate[bank] = tns.at()

                    emit_mm1b(0)
                    for ti in range(1, ntile):
                        emit_mm1b(ti)
                        emit_mm2b(ti - 1)
                    emit_mm2b(ntile - 1)
                    MM2_ALL = tns.at()
                    RBC_PREV = o_written[-1] if o_written else (dve, 0)
                    p.emit("vector", lambda v: v.tensor_copy(rs4, P[4][64:65, :]), waits=[MM2_ALL, RBC_PREV], incs=[(dve, 1)])
                    p.emit("vector", lambda v: v.tensor_copy(rs4b[:], P[5][64:65, :]), incs=[(dve, 1)])
                    p.emit("vector", lambda v: v.reciprocal(recA[:], rs4), waits=[dve.at()], incs=[(dve, 1)])
                    p.emit("vector", lambda v: v.reciprocal(recB[:], rs4b[:]), incs=[(dve, 1)])
                    p.emit("vector", lambda v: v.tensor_copy(rhiA[:], recA[:]), waits=[dve.at()], incs=[(dve, 1)])
                    p.emit("vector", lambda v: v.tensor_copy(rhiB[:], recB[:]), incs=[(dve, 1)])
                    p.emit("vector", lambda v: v.tensor_tensor(rloA[:], recA[:], rhiA[:], ALU.subtract),
                           waits=[dve.at()], incs=[(dve, 1)])
                    p.emit("vector", lambda v: v.tensor_tensor(rloB[:], recB[:], rhiB[:], ALU.subtract),
                           incs=[(dve, 1)])
                    RL = dve.at()
                    p.emit("tensor", lambda t: t.matmul(P[6][0:64, :], ones_bf[0:1, 0:64], rhiA[:], start=True, stop=False),
                           waits=[RL])
                    p.emit("tensor", lambda t: t.matmul(P[6][0:64, :], ones_bf[0:1, 0:64], rloA[:], start=False, stop=True))
                    p.emit("tensor", lambda t: t.matmul(P[6][64:128, :], ones_bf[0:1, 0:64], rhiB[:], start=True, stop=False,
                                                        tile_position=(0, 64)))
                    p.emit("tensor", lambda t: t.matmul(P[6][64:128, :], ones_bf[0:1, 0:64], rloB[:], start=False, stop=True,
                                                        tile_position=(0, 64)), incs=[(tns, 1)])
                    p.emit("scalar", lambda s: s.copy(out=rbc[:], in_=P[6][:]), waits=[tns.at()], incs=[(act, 1)])
                    RBC = act.at()
                    prev_rbc = RBC
                    ob_ = w % 2
                    sto = DmaSlots.__call__ if False else None
                    p.emit("vector", (lambda ob_=ob_: lambda v: v.tensor_tensor(
                        o_stage[ob_][0:64, :], P[4][0:64, :], rbc[0:64, :], ALU.mult))(),
                        waits=[RBC, sto4[ob_].at()], incs=[(dve, 1)])
                    p.emit("vector", (lambda ob_=ob_: lambda v: v.tensor_tensor(
                        o_stage[ob_][64:128, :], P[5][0:64, :], rbc[64:128, :], ALU.mult))(),
                        incs=[(dve, 1)])
                    o_written.append(dve.at())
                    p.emit("sync", (lambda hp=hp, w=w, ob_=ob_: lambda s: s.dma_start(
                        out=o4T_h[hp * 128:(hp + 1) * 128, w * 512:(w + 1) * 512], in_=o_stage[ob_]))(),
                        waits=[o_written[-1]], incs=[(sto4[ob_], 16)])
                hp_mm_end[hp] = (tns, tns.n)
            O_ALL = [sto4[0].at(), sto4[1].at()]

            res_done = []
            oproj_mm = []
            for t in range(TT):
                b = t % 2
                war = [res_done[t - 2]] if t >= 2 else []
                p.emit("sync", (lambda t=t, b=b: lambda s: s.dma_start(
                    out=xt[b][:], in_=x4[OFF + t * 128:OFF + (t + 1) * 128, :]))(),
                    waits=war + [stn[b].at()], incs=[(ldx2[b], 16)])
                LDX = ldx2[b].at()
                war_o = [oproj_mm[t - 2]] if t >= 2 else []
                p.emit("sync", (lambda t=t, b=b: lambda s: s.dma_start(
                    out=oT_t[b][:], in_=o4T_h[:, t * 128:(t + 1) * 128].rearrange("(c p) n -> p c n", p=128)))(),
                    waits=O_ALL + war_o, incs=[(ldo4[b], 16)])
                LDO = ldo4[b].at()
                mm_at = []
                for dh in range(2):
                    bank = (t * 2 + dh) % 4
                    for ic in range(4):
                        p.emit("tensor", (lambda ic=ic, t=t, dh=dh, bank=bank, b=b: lambda tn: tn.matmul(
                            P[bank][:], oT_t[b][:, ic, :],
                            wo_sb[:, ic, dh * 512:(dh + 1) * 512], start=(ic == 0), stop=(ic == 3)))(),
                            waits=([LDO] + war) if (ic == 0 and dh == 0) else (),
                            incs=[(tns, 1)] if ic == 3 else ())
                    mm_at.append(tns.at())
                oproj_mm.append(tns.at())
                for dh in range(2):
                    bank = (t * 2 + dh) % 4
                    p.emit("vector", (lambda b=b, dh=dh, bank=bank: lambda v: v.tensor_add(
                        xt[b][:, dh * 512:(dh + 1) * 512], xt[b][:, dh * 512:(dh + 1) * 512], P[bank][:]))(),
                        waits=[mm_at[1], LDX] if dh == 0 else (), incs=[(dve, 1)] if dh == 1 else ())
                p.emit("vector", (lambda b=b: lambda v: v.tensor_copy(xn_sb[b][:], xt[b][:]))(),
                       waits=[dve.at()], incs=[(dve, 1)])
                res_done.append(dve.at())
                p.emit("sync", (lambda t=t, b=b: lambda s: s.dma_start(
                    out=x4[OFF + t * 128:OFF + (t + 1) * 128, :], in_=xt[b][:]))(),
                    waits=[res_done[-1]], incs=[(stn[b], 16)])
                p.emit("sync", (lambda t=t, b=b: lambda s: s.dma_start(
                    out=x4bf[OFF + t * 128:OFF + (t + 1) * 128, :], in_=xn_sb[b][:]))(), incs=[(stn[b], 16)])
            RES_ST = [stn[0].at(), stn[1].at()]
            OPROJ_END = (tns, tns.n)

            for (rowsrc, dst_bc, wpre) in ((lng, g_bc, []), (lnb, b_bc, None)):
                if wpre is None:
                    wpre = [dve.at()]
                p.emit("sync", (lambda l=l, rowsrc=rowsrc: lambda s: s.dma_start(
                    out=gb_row[:], in_=rowsrc[l].rearrange("(u d) -> u d", u=1)[0:1, 0:512]))(),
                    waits=wpre, incs=[(ld_c, 16)])
                p.emit("tensor", lambda t: t.matmul(P[6][:], ones_f[0:1, 0:128], gb_row[0:1, 0:512],
                                                    start=True, stop=True),
                       waits=[ld_c.at()] + wpre, incs=[(tns, 1)])
                GM1 = tns.at()
                p.emit("sync", (lambda l=l, rowsrc=rowsrc: lambda s: s.dma_start(
                    out=gb_row[:], in_=rowsrc[l].rearrange("(u d) -> u d", u=1)[0:1, 512:1024]))(),
                    waits=[GM1], incs=[(ld_c, 16)])
                p.emit("tensor", lambda t: t.matmul(P[7][:], ones_f[0:1, 0:128], gb_row[0:1, 0:512],
                                                    start=True, stop=True),
                       waits=[ld_c.at()], incs=[(tns, 1)])
                p.emit("vector", (lambda dst_bc=dst_bc: lambda v: v.tensor_copy(dst_bc[:, 0:512], P[6][:]))(),
                       waits=[tns.at()])
                p.emit("vector", (lambda dst_bc=dst_bc: lambda v: v.tensor_copy(dst_bc[:, 512:1024], P[7][:]))(),
                       incs=[(dve, 1)])
            GB_ALL = dve.at()
            p.emit("gpsimd", (lambda l=l: lambda g: g.dma_start(
                out=b2_row[:], in_=b2[l].rearrange("(u d) -> u d", u=1)))(), waits=lg, incs=[(ldb2s[0], 16)])
            W2L = ldb2s[0].at()

            hln_done = []
            hln_st_at = []
            for t in range(TT):
                b = t % 2
                war = [hln_done[t - 2]] if t >= 2 else []
                p.emit("sync", (lambda t=t, b=b: lambda s: s.dma_start(
                    out=xt[b][:, 0:512], in_=x4[OFF + t * 128:OFF + (t + 1) * 128, 0:512]))(),
                    waits=RES_ST + war, incs=[(ldx3[b], 16)])
                p.emit("sync", (lambda t=t, b=b: lambda s: s.dma_start(
                    out=xt[b][:, 512:1024], in_=x4[OFF - 1 + t * 128:OFF - 1 + (t + 1) * 128, 512:1024]))(),
                    incs=[(ldx3[b], 16)])
                rstd_at, aggr_at = emit_ln_stats(p, ln_pre, b, [xt[b][:, 0:512], xt[b][:, 512:1024]],
                                                 eps, dve, act, waits=[ldx3[b].at()])
                war2 = [stqn[b].at()]
                p.emit("vector", (lambda b=b: lambda v: v.tensor_scalar(
                    out=xt[b][:], in0=xt[b][:], scalar1=ln_pre.mv[b][:, 0:1], scalar2=ln_pre.rstd[b][:],
                    op0=ALU.subtract, op1=ALU.mult))(), waits=[rstd_at, aggr_at, GB_ALL], incs=[(dve, 1)])
                p.emit("vector", (lambda b=b: lambda v: v.tensor_mul(xt[b][:], xt[b][:], g_bc[:]))(),
                       waits=[dve.at()], incs=[(dve, 1)])
                p.emit("vector", (lambda b=b: lambda v: v.tensor_tensor(
                    hln_sb[b][:], xt[b][:], b_bc[:], ALU.add))(), waits=war2 + [dve.at()], incs=[(dve, 1)])
                hln_done.append(dve.at())
                p.emit("sync", (lambda t=t, b=b: lambda s: s.dma_start(
                    out=qn_h[t * 128:(t + 1) * 128, 0:512], in_=hln_sb[b][:, 0:512]))(),
                    waits=[hln_done[-1], TRQK], incs=[(stqn[b], 16)])
                p.emit("sync", (lambda t=t, b=b: lambda s: s.dma_start(
                    out=kn_h[t * 128:(t + 1) * 128, 0:512], in_=hln_sb[b][:, 512:1024]))(), incs=[(stqn[b], 16)])
                hln_st_at.append([stqn[0].at(), stqn[1].at()])
            HLN_ST = [stqn[0].at(), stqn[1].at()]

            ffn_done = []
            h2_gate = []
            ffn1_end = []
            prev_w1_reads = None
            prev_w2_reads = None
            for sl in range(NSLAB):
                war_t = [ffn1_end[-1]] if ffn1_end else []
                for c in range(NC_CHUNKS):
                    src = qn_h if c < 4 else kn_h
                    cc = c % 4
                    p.emit("sync", (lambda sl=sl, src=src, cc=cc, c=c: lambda s: s.dma_start(
                        out=hlnT_slab[:, c, :],
                        in_=src[sl * 512:(sl + 1) * 512, cc * 128:(cc + 1) * 128], transpose=True))(),
                        waits=(hln_st_at[sl * 4 + 3] + war_t) if c == 0 else (), incs=[(trh[0], 16)])
                TRH2 = trh[0].at()
                pair_mm = {}
                for ip in range(NH2 // 2):
                    bload = ip % 2
                    gate0 = []
                    if ip < 2:
                        if sl == 0:
                            gate0 = [OPROJ_END]
                            if prev_w1_reads is not None:
                                gate0 = gate0 + [prev_w1_reads]
                        else:
                            gate0 = [prev_w1_reads]
                    war_p = [pair_mm[2 * ip - 3]] if ip >= 2 else []
                    prep_w = PREP_W1 if sl == 0 else []
                    p.emit("gpsimd", (lambda l=l, ip=ip, bload=bload: lambda g: g.dma_start(
                        out=w1pa[bload][:],
                        in_=w1bfA[l][:, ip * 256:(ip + 1) * 256].rearrange("(c p) d -> p c d", p=128)))(),
                        waits=[ldw1[bload].at()] + war_p + gate0 + prep_w, incs=[(ldw1[bload], 16)])
                    p.emit("gpsimd", (lambda l=l, ip=ip, bload=bload: lambda g: g.dma_start(
                        out=w1pg[bload][:],
                        in_=w1bfB[l][:, ip * 256:(ip + 1) * 256].rearrange("(c p) d -> p c d", p=128)))(),
                        incs=[(ldw1[bload], 16)])
                    for u in range(2):
                        i = 2 * ip + u
                        wi = 128 if i < 21 else 42
                        p.emit("gpsimd", (lambda l=l, i=i, u=u, wi=wi, bload=bload: lambda g: g.dma_start(
                            out=b1p[bload][0:1, u * 256:u * 256 + wi],
                            in_=b1[l][i * 128:i * 128 + wi].rearrange("(u d) -> u d", u=1)))(),
                            incs=[(ldw1[bload], 16)])
                        p.emit("gpsimd", (lambda l=l, i=i, u=u, wi=wi, bload=bload: lambda g: g.dma_start(
                            out=b1p[bload][0:1, u * 256 + 128:u * 256 + 128 + wi],
                            in_=b1[l][HID + i * 128:HID + i * 128 + wi].rearrange("(u d) -> u d", u=1)))(),
                            incs=[(ldw1[bload], 16)])
                    LW1 = ldw1[bload].at()
                    for u in range(2):
                        i = 2 * ip + u
                        wi = 128 if i < 21 else 42
                        ba = 2 * u
                        bg = ba + 1
                        gate = [pair_mm[i - 2]] if i >= 2 else ([ffn_done[-1]] if ffn_done else [])
                        for ic in range(NC_CHUNKS):
                            p.emit("tensor", (lambda ic=ic, u=u, wi=wi, ba=ba, bload=bload: lambda tn: tn.matmul(
                                P[ba][0:wi, :], w1pa[bload][:, ic, u * 128:u * 128 + wi], hlnT_slab[:, ic, :],
                                start=(ic == 0), stop=False))(),
                                waits=([TRH2, LW1, W2L] + gate) if ic == 0 else ())
                        p.emit("tensor", (lambda u=u, wi=wi, ba=ba, bload=bload: lambda tn: tn.matmul(
                            P[ba][0:wi, :], b1p[bload][0:1, u * 256:u * 256 + wi], ones_bf[0:1, :],
                            start=False, stop=True))(), incs=[(tns, 1)])
                        for ic in range(NC_CHUNKS):
                            p.emit("tensor", (lambda ic=ic, u=u, wi=wi, bg=bg, bload=bload: lambda tn: tn.matmul(
                                P[bg][0:wi, :], w1pg[bload][:, ic, u * 128:u * 128 + wi], hlnT_slab[:, ic, :],
                                start=(ic == 0), stop=False))())
                        p.emit("tensor", (lambda u=u, wi=wi, bg=bg, bload=bload: lambda tn: tn.matmul(
                            P[bg][0:wi, :], b1p[bload][0:1, u * 256 + 128:u * 256 + 128 + wi], ones_bf[0:1, :],
                            start=False, stop=True))(), incs=[(tns, 1)])
                        MMP = tns.at()
                        pair_mm[i] = MMP
                        if i == NH2 - 1:
                            prev_w1_reads = MMP
                        p.emit("scalar", (lambda wi=wi, bg=bg: lambda s: s.activation(
                            out=gact[0:wi, :], in_=P[bg][0:wi, :], func=AF.Gelu))(),
                            waits=[MMP] + ([pair_mm[i - 1]] if i >= 1 else []), incs=[(act, 1)])
                        p.emit("vector", (lambda i=i, wi=wi, ba=ba: lambda v: v.tensor_tensor(
                            h2T_slab[0:wi, i, :], P[ba][0:wi, :], gact[0:wi, :], ALU.mult))(),
                            waits=[act.at()], incs=[(dve, 1)])
                        pair_mm[i] = dve.at()
                H2_ALL = dve.at()
                ffn1_end.append(H2_ALL)
                prev_ffn_reads = ffn_done[-1] if ffn_done else None
                w2mm = {}
                for i in range(NH2):
                    wi = 128 if i < 21 else 42
                    bl2 = i % 2
                    gate2 = []
                    if i < 2:
                        if sl == 0:
                            gate2 = [OPROJ_END]
                            if prev_w2_reads is not None:
                                gate2 = gate2 + [prev_w2_reads]
                        else:
                            gate2 = [prev_w2_reads]
                    p.emit("gpsimd", (lambda l=l, i=i, wi=wi, bl2=bl2: lambda g: g.dma_start(
                        out=w2p[bl2][0:wi, :], in_=w2bf[l][i * 128:i * 128 + wi, :]))(),
                        waits=[ldw2[bl2].at()] + ([w2mm[i - 2]] if i >= 2 else []) + gate2
                              + (PREP_W1 if sl == 0 else []), incs=[(ldw2[bl2], 16)])
                    LW2 = ldw2[bl2].at()
                    for t4 in range(4):
                        for dh in range(2):
                            bank = t4 * 2 + dh
                            p.emit("tensor", (lambda i=i, wi=wi, t4=t4, dh=dh, bank=bank, bl2=bl2: lambda tn: tn.matmul(
                                P[bank][:], h2T_slab[0:wi, i, t4 * 128:(t4 + 1) * 128],
                                w2p[bl2][0:wi, dh * 512:(dh + 1) * 512],
                                start=(i == 0), stop=False))(),
                                waits=([H2_ALL, LW2] + ([prev_ffn_reads] if prev_ffn_reads else [])) if (t4 == 0 and dh == 0) else (),
                                incs=[(tns, 1)] if (t4 == 3 and dh == 1) else ())
                    w2mm[i] = tns.at()
                prev_w2_reads = tns.at()
                for t4 in range(4):
                    for dh in range(2):
                        bank = t4 * 2 + dh
                        p.emit("tensor", (lambda dh=dh, bank=bank: lambda tn: tn.matmul(
                            P[bank][:], ones_bf[0:1, 0:128], b2_row[0:1, dh * 512:(dh + 1) * 512],
                            start=False, stop=True))(), incs=[(tns, 1)] if (t4 == 3 and dh == 1) else ())
                FFN2MM = tns.at()
                for t4 in range(4):
                    t = sl * 4 + t4
                    b = t % 2
                    war = [ffn_done[-2]] if len(ffn_done) >= 2 else []
                    p.emit("sync", (lambda t=t, b=b: lambda s: s.dma_start(
                        out=y4[b][:], in_=x4[OFF + t * 128:OFF + (t + 1) * 128, :]))(),
                        waits=war + [stn[b].at()] + [stx[b].at()], incs=[(ldx2[b], 16)])
                    LD4 = ldx2[b].at()
                    for dh in range(2):
                        bank = t4 * 2 + dh
                        p.emit("vector", (lambda b=b, dh=dh, bank=bank: lambda v: v.tensor_add(
                            y4[b][:, dh * 512:(dh + 1) * 512], y4[b][:, dh * 512:(dh + 1) * 512], P[bank][:]))(),
                            waits=[FFN2MM, LD4] if dh == 0 else (), incs=[(dve, 1)] if dh == 1 else ())
                    Y4 = dve.at()
                    ffn_done.append(Y4)
                    if l == 0:
                        p.emit("vector", (lambda b=b: lambda v: v.tensor_copy(xn_sb[b][:], y4[b][:]))(),
                               waits=[Y4], incs=[(dve, 1)])
                        ffn_done[-1] = dve.at()
                        p.emit("sync", (lambda t=t, b=b: lambda s: s.dma_start(
                            out=x4[OFF + t * 128:OFF + (t + 1) * 128, :], in_=y4[b][:]))(),
                            waits=[ffn_done[-1]], incs=[(stn[b], 16)])
                        p.emit("sync", (lambda t=t, b=b: lambda s: s.dma_start(
                            out=x4bf[OFF + t * 128:OFF + (t + 1) * 128, :], in_=xn_sb[b][:]))(), incs=[(stn[b], 16)])
                    else:
                        p.emit("sync", (lambda t=t, b=b: lambda s: s.dma_start(
                            out=xt[b][:], in_=x_in[t * 128:(t + 1) * 128, :]))(),
                            waits=[Y4], incs=[(ldx3[b], 16)])
                        LDXI = ldx3[b].at()
                        rstd_at, aggr_at = emit_ln_stats(p, ln_post, b, [y4[b][:, 0:512], y4[b][:, 512:1024]],
                                                         eps, dve, act, waits=[Y4])
                        p.emit("vector", (lambda b=b: lambda v: v.tensor_scalar(
                            out=xo_sb[b][:], in0=y4[b][:], scalar1=ln_post.mv[b][:, 0:1],
                            scalar2=ln_post.rstd[b][:], op0=ALU.subtract, op1=ALU.mult))(),
                            waits=[rstd_at, aggr_at, stx[b].at()], incs=[(dve, 1)])
                        p.emit("vector", (lambda b=b: lambda v: v.tensor_add(xo_sb[b][:], xo_sb[b][:], xt[b][:]))(),
                               waits=[LDXI, dve.at()], incs=[(dve, 1)])
                        ffn_done[-1] = dve.at()
                        p.emit("sync", (lambda t=t, b=b: lambda s: s.dma_start(
                            out=x_out[t * 128:(t + 1) * 128, :], in_=xo_sb[b][:]))(),
                            waits=[ffn_done[-1]], incs=[(stx[b], 16)])
                h2_gate.append(tns.at())
            LAYER_GATE = ffn_done[-1]

        return [stx[0].at(), stx[1].at()]


def phase4_consts(S):
    DIM_HEAD = 64
    WINDOW = 512
    half_idx = np.arange(0, DIM_HEAD, 2, dtype=np.float64)
    inv_freq = 1.0 / 10000.0 ** (half_idx / DIM_HEAD)
    t = np.arange(S, dtype=np.float64)
    freqs = t[:, None] * inv_freq[None, :]
    freqs = np.concatenate([freqs, freqs], axis=-1)
    base = (half_idx + 0.4 * DIM_HEAD) / (1.4 * DIM_HEAD)
    power = (t - S // 2) / float(WINDOW // 2)
    scale = base[None, :] ** power[:, None]
    scale = np.concatenate([scale, scale], axis=-1)
    cos, sin = np.cos(freqs), np.sin(freqs)
    tq_c = (cos * scale).astype(np.float32)
    tq_s = (sin * scale).astype(np.float32)
    tk_c = (cos / scale).astype(np.float32)
    tk_s = (sin / scale).astype(np.float32)
    NEG = np.float32(-1e30)
    m4 = np.zeros((8, 128, 512), np.float32)
    for rel in range(8):
        r = np.arange(128)[:, None]
        c = np.arange(512)[None, :]
        diff = c - r + 512 - 128 * rel
        m4[rel] = np.where((diff >= 0) & (diff <= 512), 0.0, NEG)
    return tq_c, tq_s, tk_c, tk_s, m4.astype(ml_dtypes.bfloat16)



def build_fused(S=4096):
    nc = bass.Bass("TRN2", target_bir_lowering=False)
    names_f32 = [
        ("L0_attn_q_w", [D, D]), ("L0_attn_q_b", [D]), ("L0_attn_k_w", [D, D]), ("L0_attn_k_b", [D]),
        ("L0_attn_v_w", [D, D]), ("L0_attn_v_b", [D]), ("L0_attn_o_w", [D, D]), ("L0_attn_o_b", [D]),
        ("rmT", [16, 256 * 16]), ("rupT", [16, 256 * 64]), ("wcTf", [16384, D]),
        ("L1_attn_w_qkv", [D, 3 * D]), ("L1_attn_b_qkv", [3 * D]),
        ("L1_attn_w_out", [D, D]), ("L1_attn_b_out", [D]),
        ("wqkv", [2, D, 3 * 512]), ("q_scale", [2, 64]), ("k_scale", [2, 64]),
        ("wo", [2, 512, D]), ("ln_g", [2, D]), ("ln_b", [2, D]),
        ("w1", [2, D, 2 * 2730]), ("b1", [2, 2 * 2730]), ("w2", [2, 2730, D]), ("b2", [2, D]),
        ("eye8", [8, 8]),
        ("tqc", [S, 64]), ("tqs", [S, 64]), ("tkc", [S, 64]), ("tks", [S, 64]),
    ]
    names_bf16 = [
        ("eyeb", [128, 128]), ("bandm", [6, 128, 512]), ("stridm", [S // 512, 128, 512]),
        ("mask4", [8, 128, 512]),
    ]
    A = {}
    for n, shp in names_f32:
        A[n] = nc.dram_tensor(n, shp, F32, kind="ExternalInput").ap()
    for n, shp in names_bf16:
        A[n] = nc.dram_tensor(n, shp, BF16, kind="ExternalInput").ap()
    x_in = nc.dram_tensor("x_in", [S, D], F32, kind="ExternalInput").ap()
    x_out = nc.dram_tensor("x_out", [S, D], F32, kind="ExternalOutput").ap()
    xb0 = nc.dram_tensor("xb0", [S, D], F32, kind="Internal").ap()
    xb1 = nc.dram_tensor("xb1", [S, D], F32, kind="Internal").ap()

    W1 = dict(q_w=A["L0_attn_q_w"], q_b=A["L0_attn_q_b"], k_w=A["L0_attn_k_w"], k_b=A["L0_attn_k_b"],
              v_w=A["L0_attn_v_w"], v_b=A["L0_attn_v_b"], o_w=A["L0_attn_o_w"], o_b=A["L0_attn_o_b"],
              eye8=A["eye8"])
    W2 = dict(rmT=A["rmT"], rupT=A["rupT"], wcTf=A["wcTf"])
    W3 = dict(w_qkv=A["L1_attn_w_qkv"], b_qkv=A["L1_attn_b_qkv"], w_out=A["L1_attn_w_out"],
              b_out=A["L1_attn_b_out"], eye8=A["eye8"], eyeb=A["eyeb"], bandm=A["bandm"],
              stridm=A["stridm"])
    W4 = dict(wqkv=A["wqkv"], q_scale=A["q_scale"], k_scale=A["k_scale"], wo=A["wo"],
              ln_g=A["ln_g"], ln_b=A["ln_b"], w1=A["w1"], b1=A["b1"], w2=A["w2"], b2=A["b2"],
              eyeb=A["eyeb"], tqc=A["tqc"], tqs=A["tqs"], tkc=A["tkc"], tks=A["tks"],
              mask4=A["mask4"])

    outer = contextlib.ExitStack()
    with outer:
        PQ = [outer.enter_context(nc.psum_tensor(f"pq{i}", [128, 1024], F32)) for i in range(4)]
        P = [PQ[i // 2][:, (i % 2) * 512:(i % 2) * 512 + 512] for i in range(8)]
        p = Prog(nc, outer)
        p.PQ = PQ
        p.AR = ArenaAlloc(nc, outer)
        tns = p.sem("tns")
        act = p.sem("act")
        dve = p.sem("dve")
        phases = [
            (emit_phase1, x_in, xb0, W1, "p1_"),
            (emit_phase2, xb0, xb1, W2, "p2_"),
            (emit_phase3, xb1, xb0, W3, "p3_"),
            (emit_phase4, xb0, x_out, W4, "p4_"),
        ]
        gate = []
        final = None
        for fn, xi, xo, Wd, pref in phases:
            p.set_gate(gate, pref)
            p.AR.reset()
            with contextlib.ExitStack() as ph_stack:
                final = fn(nc, p, ph_stack, P, tns, act, dve, S, xi, xo, Wd)
                gate = p.snapshot_all()
        p.finalize(final_waits=final)
    return nc




from concourse.bass_utils import run_bass_kernel_spmd

_S = 4096
_NCORES = 8


def kernel(**inputs):
    S = _S
    cores = list(range(_NCORES))
    f32 = np.float32
    x = np.ascontiguousarray(inputs["L0_attn_x"].astype(f32, copy=False))

    eye8 = np.eye(8, dtype=f32)
    eyeb = np.eye(128, dtype=f32).astype(ml_dtypes.bfloat16)
    bandm, stridm = phase3_masks(S)
    tqc, tqs_, tkc, tks_, m4 = phase4_consts(S)
    rmT = np.ascontiguousarray(inputs["L0_ffn_rail_maker"].transpose(3, 0, 1, 2).reshape(16, 256 * 16))
    rupT = np.ascontiguousarray(inputs["L0_ffn_rail_up_proj"].transpose(3, 0, 1, 2).reshape(16, 256 * 64))

    nc = build_fused(S)
    base = dict(
        L0_attn_q_w=inputs["L0_attn_q_w"], L0_attn_q_b=inputs["L0_attn_q_b"],
        L0_attn_k_w=inputs["L0_attn_k_w"], L0_attn_k_b=inputs["L0_attn_k_b"],
        L0_attn_v_w=inputs["L0_attn_v_w"], L0_attn_v_b=inputs["L0_attn_v_b"],
        L0_attn_o_w=inputs["L0_attn_o_w"], L0_attn_o_b=inputs["L0_attn_o_b"],
        rmT=rmT, rupT=rupT, wcTf=np.ascontiguousarray(inputs["L0_ffn_w_cproj"].T),
        L1_attn_w_qkv=inputs["L1_attn_w_qkv"], L1_attn_b_qkv=inputs["L1_attn_b_qkv"],
        L1_attn_w_out=inputs["L1_attn_w_out"], L1_attn_b_out=inputs["L1_attn_b_out"],
        wqkv=inputs["L1_ffn_wqkv"], q_scale=inputs["L1_ffn_q_scale"], k_scale=inputs["L1_ffn_k_scale"],
        wo=inputs["L1_ffn_wo"], ln_g=inputs["L1_ffn_ln_g"], ln_b=inputs["L1_ffn_ln_b"],
        w1=inputs["L1_ffn_w1"], b1=inputs["L1_ffn_b1"], w2=inputs["L1_ffn_w2"], b2=inputs["L1_ffn_b2"],
        eye8=eye8, eyeb=eyeb, bandm=bandm, stridm=stridm,
        tqc=tqc, tqs=tqs_, tkc=tkc, tks=tks_, mask4=m4)
    maps = [dict(base, x_in=x[i]) for i in cores]
    res = run_bass_kernel_spmd(nc, maps, core_ids=cores)
    out = np.stack([res.results[i]["x_out"] for i in cores], axis=0)
    return out.astype(np.float32, copy=False)


# version 08
# speedup vs baseline: 1.3925x; 1.0013x over previous
import contextlib
import numpy as np
import ml_dtypes
import concourse.bass as bass
import concourse.mybir as mybir
import bass_rust

F32 = mybir.dt.float32
BF16 = mybir.dt.bfloat16
AF = mybir.ActivationFunctionType
ALU = mybir.AluOpType
AX = mybir.AxisListType

D = 1024
NC_CHUNKS = D // 128

LIF_T = (1.0 / 3.709875, 1.95 / 3.709875, 2.95 / 3.709875, 3.8525 / 3.709875)


class Sem:
    __slots__ = ("h", "n", "name")

    def __init__(self, handle, name):
        self.h = handle
        self.n = 0
        self.name = name

    def at(self):
        return (self, self.n)


class Prog:
    def __init__(self, nc, stack):
        self.nc = nc
        self.stack = stack
        self.ops = {e: [] for e in ("sync", "tensor", "vector", "scalar", "gpsimd")}
        self._semi = 0
        self.prefix = ""
        self._gate = []
        self._gated = set()
        self._all_sems = []

    def sem(self, name=None):
        if name is None:
            name = f"s{self._semi}"
            self._semi += 1
        name = self.prefix + name
        h = self.stack.enter_context(self.nc.semaphore(name))
        sm = Sem(h, name)
        self._all_sems.append(sm)
        return sm

    def snapshot_all(self):
        return [(sm, sm.n) for sm in self._all_sems if sm.n > 0]

    def set_gate(self, gate, prefix):
        self._gate = list(gate)
        self._gated = set()
        self.prefix = prefix

    def emit(self, eng, fn, waits=(), incs=()):
        waits = list(waits)
        if eng not in self._gated:
            self._gated.add(eng)
            waits = self._gate + waits
        rec = []
        for s, amt in incs:
            s.n += amt
            rec.append((s, amt))
        self.ops[eng].append((fn, [(s, v) for (s, v) in waits if v > 0], rec))

    def finalize(self, final_waits=()):
        nc = self.nc
        with nc.Block() as block:
            for eng_name in ("sync", "tensor", "vector", "scalar", "gpsimd"):
                ops = self.ops[eng_name]
                if not ops and not (eng_name == "sync" and final_waits):
                    continue

                def make_body(eng_name, ops):
                    def body(eng):
                        last_wait = {}
                        for fn, waits, incs in ops:
                            for s, v in waits:
                                if last_wait.get(s.name, 0) >= v:
                                    continue
                                eng.wait_ge(s.h, v)
                                last_wait[s.name] = v
                            inst = fn(eng)
                            for s, amt in incs:
                                assert inst is not None
                                inst = inst.then_inc(s.h, amt)
                        if eng_name == "sync":
                            for s, v in final_waits:
                                if v > 0:
                                    eng.wait_ge(s.h, v)

                    return body

                getattr(block, eng_name)(make_body(eng_name, ops))


class ArenaAlloc:

    def __init__(self, nc, stack, total_bf16_cols=106336):
        self.t = stack.enter_context(nc.sbuf_tensor("the_arena", [128, total_bf16_cols], BF16))
        self.total = total_bf16_cols
        self.off = 0

    def reset(self):
        self.off = 0

    def alloc(self, shape, dtype):
        parts = shape[0]
        elems = 1
        for d_ in shape[1:]:
            elems *= d_
        esz = 4 if dtype == F32 else 2
        cols_bf = (elems * esz + 1) // 2
        off = (self.off + 31) & ~31
        self.off = off + cols_bf
        assert self.off <= self.total, f"arena overflow {self.off} > {self.total}"
        ap = self.t[0:parts, off:off + cols_bf]
        if dtype != BF16:
            ap = ap.bitcast(dtype)
        if len(shape) == 3:
            ap = ap.rearrange("p (a b) -> p a b", b=shape[2])
        elif len(shape) == 4:
            ap = ap.rearrange("p (a b c) -> p a b c", b=shape[2], c=shape[3])
        return ap


class DmaSlots:

    def __init__(self, p, name, n):
        self.sems = [p.sem(f"{name}{i}") for i in range(n)]

    def __getitem__(self, i):
        return self.sems[i % len(self.sems)]



class LNState:

    def __init__(self, p, nbuf=2):
        self.nbuf = nbuf
        self.stats = [p.AR.alloc([128, 12], F32) for i in range(nbuf)]
        self.mv = [p.AR.alloc([128, 2], F32) for i in range(nbuf)]
        self.rstd = [p.AR.alloc([128, 1], F32) for i in range(nbuf)]


def emit_ln_stats(p, ln, b, srcs, eps, dve, act, waits):
    nsrc = len(srcs)
    assert nsrc in (2, 4)
    for i, ap in enumerate(srcs):
        w = list(waits) if i == 0 else ()
        inc = [(dve, 1)] if i == nsrc - 1 else ()
        p.emit("vector", (lambda ap=ap, i=i, b=b: lambda v: v.bn_stats(ln.stats[b][:, 6 * i:6 * i + 6], ap))(), waits=w, incs=inc)
    p.emit("vector", lambda v, b=b: v.bn_aggr(ln.mv[b][:], ln.stats[b][:, 0:6 * nsrc]),
           waits=[dve.at()], incs=[(dve, 1)])
    aggr = dve.at()
    p.emit("scalar", lambda s, b=b: s.activation(out=ln.rstd[b][:], in_=ln.mv[b][:, 1:2], func=AF.Ln, bias=eps[:]),
           waits=[aggr], incs=[(act, 1)])
    p.emit("scalar", lambda s, b=b: s.activation(out=ln.rstd[b][:], in_=ln.rstd[b][:], func=AF.Exp, scale=-0.5),
           waits=[act.at()], incs=[(act, 1)])
    return act.at(), aggr



def emit_phase1(nc, p, stack, P, tns, act, dve, S, x_in, x_out, W):
    TT = S // 128
    QC = S // 512
    NK = S // 128
    qw, qb, kw, kb, vw, vb, ow, ob, eye8 = (W["q_w"], W["q_b"], W["k_w"], W["k_b"],
                                            W["v_w"], W["v_b"], W["o_w"], W["o_b"], W["eye8"])

    xnbf = nc.dram_tensor("p1_xnbf", [S, D], BF16, kind="Internal").ap()
    qcT_h = nc.dram_tensor("p1_qcT_h", [D, S], BF16, kind="Internal").ap()
    kcT_h = nc.dram_tensor("p1_kcT_h", [D, S], BF16, kind="Internal").ap()
    vblk_h = nc.dram_tensor("p1_vblk_h", [TT, 128, 16 * 65], BF16, kind="Internal").ap()

    if True:
        sb = lambda n, shp, d: p.AR.alloc(shp, d)

        eps = sb("eps", [128, 1], F32)
        neg4 = sb("neg4", [128, 1], F32)
        ones_bf = sb("ones_bf", [1, 128], BF16)

        xt = [sb(f"xt{i}", [128, D], F32) for i in range(2)]
        xn_sb = [sb(f"xn{i}", [128, D], BF16) for i in range(2)]
        ln_pre = LNState(p)

        xnT = sb("xnT", [128, NC_CHUNKS, S], BF16)

        qw_sb = sb("qw_sb", [128, NC_CHUNKS, D], BF16)
        kw_sb = sb("kw_sb", [128, NC_CHUNKS, D], BF16)
        vw_sb = sb("vw_sb", [128, NC_CHUNKS, D], BF16)
        ow_sb = vw_sb
        qb_sb = sb("qb_sb", [128, NC_CHUNKS], F32)
        kb_sb = sb("kb_sb", [128, NC_CHUNKS], F32)
        qb8 = sb("qb8", [8, 128], F32)
        kb8 = sb("kb8", [8, 128], F32)
        eye8_sb = sb("eye8_sb", [8, 8], F32)
        vb_row = sb("vb_row", [1, D], BF16)
        ob_row = sb("ob_row", [1, D], BF16)

        bi = [sb(f"bi{i}", [128, 512], BF16) for i in range(4)]
        c01 = sb("c01", [128, 512], BF16)
        c23 = sb("c23", [128, 512], BF16)
        cnt_sb = [sb(f"cnt{i}", [128, 512], BF16) for i in range(2)]
        vblk_sb = [sb(f"vblk{i}", [128, 16, 65], BF16) for i in range(2)]
        ps_qk = [P[0], P[1], P[2], P[3]]
        ps_v = [P[0], P[1], P[2], P[3]]

        qcT_sb = [sb("qcT0", [128, S], BF16)]
        kcT_sb = [sb("kcT0", [128, S], BF16)]
        vbA_sb = [sb("vbA0", [128, NK, 65], BF16)]
        vbB_sb = [sb("vbB0", [128, NK, 65], BF16)]
        Eab = [sb(f"Eab{i}", [128, 1024], BF16) for i in range(2)]
        rs4 = sb("rs4", [2, 512], F32)
        rs4b = sb("rs4b", [1, 512], F32)
        recA = sb("recA", [1, 512], F32)
        recB = sb("recB", [1, 512], F32)
        rhiA = sb("rhiA", [1, 512], BF16)
        rloA = sb("rloA", [1, 512], BF16)
        rhiB = sb("rhiB", [1, 512], BF16)
        rloB = sb("rloB", [1, 512], BF16)
        rbc = sb("rbc", [128, 512], F32)
        o_buf = xnT
        ps_s = [P[0], P[2]]
        ps_s2 = [P[1], P[3]]
        ps_sf = [p.PQ[0][:, 0:1024], p.PQ[1][:, 0:1024]]
        ps_oA = P[4]
        ps_oB = P[5]
        ps_r = P[6]

        xi2 = [sb(f"xi2{i}", [128, D], F32) for i in range(2)]
        xo_sb = [sb(f"xo{i}", [128, D], F32) for i in range(2)]
        ln_post = LNState(p)
        ps_y = [P[0], P[1], P[2], P[3]]

        ld_w = p.sem("ld_w")
        ld_c = p.sem("ld_c")
        ldx = DmaSlots(p, "ldx", 2)
        stn = DmaSlots(p, "stn", 2)
        tr = p.sem("tr")
        stqk = DmaSlots(p, "stqk", 2)
        stv = DmaSlots(p, "stv", 2)
        ldb = DmaSlots(p, "ldb", 2)
        ldbg = DmaSlots(p, "ldbg", 2)
        ldx2 = DmaSlots(p, "ldx2", 2)
        stx = DmaSlots(p, "stx", 2)

        p.emit("vector", lambda v: v.memset(eps[:], 1e-5))
        p.emit("vector", lambda v: v.memset(neg4[:], -4.0))
        p.emit("vector", lambda v: v.memset(ones_bf[:], 1.0))
        for i in range(2):
            p.emit("vector", (lambda i=i: lambda v: v.memset(vblk_sb[i][:, :, 64:65], 1.0))(), incs=[(dve, 1)])
        CONSTS = dve.at()

        for w_dram, w_tile in ((qw, qw_sb), (kw, kw_sb), (vw, vw_sb)):
            p.emit("gpsimd", (lambda w_dram=w_dram, w_tile=w_tile: lambda g: g.dma_start(
                out=w_tile[:], in_=w_dram.rearrange("(c p) d -> p c d", p=128)))(), incs=[(ld_w, 16)])
        for b_dram, b_tile in ((qb, qb8), (kb, kb8)):
            p.emit("gpsimd", (lambda b_dram=b_dram, b_tile=b_tile: lambda g: g.dma_start(
                out=b_tile[:], in_=b_dram.rearrange("(c p) -> c p", p=128)))(), incs=[(ld_w, 16)])
        p.emit("sync", lambda s_: s_.dma_start(out=eye8_sb[:], in_=eye8), incs=[(ld_c, 16)])
        for b_dram, b_tile in ((vb, vb_row), (ob, ob_row)):
            p.emit("gpsimd", (lambda b_dram=b_dram, b_tile=b_tile: lambda g: g.dma_start(
                out=b_tile[:], in_=b_dram.rearrange("(u d) -> u d", u=1)))(), incs=[(ld_w, 16)])
        W_ALL = ld_w.at()
        C_ALL = ld_c.at()
        p.emit("tensor", lambda t: t.matmul(P[7][0:128, 0:8], qb8[:], eye8_sb[:], start=True, stop=True),
               waits=[W_ALL, C_ALL], incs=[(tns, 1)])
        p.emit("vector", lambda v: v.tensor_copy(qb_sb[:], P[7][0:128, 0:8]), waits=[tns.at()], incs=[(dve, 1)])
        BQ = dve.at()
        p.emit("tensor", lambda t: t.matmul(P[7][0:128, 0:8], kb8[:], eye8_sb[:], start=True, stop=True),
               waits=[BQ], incs=[(tns, 1)])
        p.emit("vector", lambda v: v.tensor_copy(kb_sb[:], P[7][0:128, 0:8]), waits=[tns.at()], incs=[(dve, 1)])
        BIAS_T = dve.at()

        prol_done = []
        for t in range(TT):
            b = t % 2
            war = [prol_done[t - 2]] if t >= 2 else []
            war2 = [stn[b].at()]
            p.emit("sync", (lambda t=t, b=b: lambda s: s.dma_start(out=xt[b][:], in_=x_in[t * 128:(t + 1) * 128, :]))(),
                   waits=war, incs=[(ldx[b], 16)])
            rstd_at, aggr_at = emit_ln_stats(p, ln_pre, b,
                                    [xt[b][:, 0:512], xt[b][:, 512:1024]],
                                    eps, dve, act, waits=[ldx[b].at()])
            p.emit("vector", (lambda b=b: lambda v: v.tensor_scalar(
                out=xn_sb[b][:], in0=xt[b][:], scalar1=ln_pre.mv[b][:, 0:1], scalar2=ln_pre.rstd[b][:],
                op0=ALU.subtract, op1=ALU.mult))(), waits=[rstd_at, aggr_at] + war2, incs=[(dve, 1)])
            prol_done.append(dve.at())
            p.emit("sync", (lambda t=t, b=b: lambda s: s.dma_start(out=xnbf[t * 128:(t + 1) * 128, :], in_=xn_sb[b][:]))(),
                   waits=[prol_done[-1]], incs=[(stn[b], 16)])

        PROL_ST = [stn[0].at(), stn[1].at()]
        for c in range(NC_CHUNKS):
            p.emit("sync", (lambda c=c: lambda s: s.dma_start(
                out=xnT[:, c, :], in_=xnbf[:, c * 128:(c + 1) * 128], transpose=True))(),
                   waits=PROL_ST, incs=[(tr, 16)])
        TR_ALL = tr.at()

        unit_lif = []
        unit_idx = 0
        for w_tile, b_tile, outT in ((qw_sb, qb_sb, qcT_h), (kw_sb, kb_sb, kcT_h)):
            for oc in range(NC_CHUNKS):
                for nq in range(QC):
                    bank = unit_idx % 4
                    war = [unit_lif[unit_idx - 4]] if unit_idx >= 4 else []
                    for ic in range(NC_CHUNKS):
                        p.emit("tensor", (lambda w_tile=w_tile, ic=ic, oc=oc, nq=nq, bank=bank: lambda t: t.matmul(
                            ps_qk[bank][:], w_tile[:, ic, oc * 128:(oc + 1) * 128],
                            xnT[:, ic, nq * 512:(nq + 1) * 512],
                            start=(ic == 0), stop=(ic == NC_CHUNKS - 1)))(),
                            waits=([TR_ALL, W_ALL, BIAS_T] + war) if ic == 0 else (),
                            incs=[(tns, 1)] if ic == NC_CHUNKS - 1 else ())
                    mm = tns.at()
                    for j in range(4):
                        p.emit("vector", (lambda j=j, bank=bank, b_tile=b_tile, oc=oc: lambda v: v.tensor_scalar(
                            out=bi[j][:], in0=ps_qk[bank][:], scalar1=b_tile[:, oc:oc + 1], scalar2=float(LIF_T[j]),
                            op0=ALU.add, op1=ALU.is_gt))(),
                            waits=[mm, BIAS_T] if j == 0 else (), incs=[(dve, 1)] if j == 3 else ())
                    p.emit("vector", lambda v: v.tensor_add(c01[:], bi[0][:], bi[1][:]), waits=[dve.at()])
                    p.emit("vector", lambda v: v.tensor_add(c23[:], bi[2][:], bi[3][:]), incs=[(dve, 1)])
                    cb = unit_idx % 2
                    war3 = [stqk[cb].at()]
                    p.emit("vector", (lambda cb=cb: lambda v: v.tensor_add(cnt_sb[cb][:], c01[:], c23[:]))(),
                           waits=war3 + [dve.at()], incs=[(dve, 1)])
                    unit_lif.append(dve.at())
                    p.emit("sync", (lambda outT=outT, oc=oc, nq=nq, cb=cb: lambda s: s.dma_start(
                        out=outT[oc * 128:(oc + 1) * 128, nq * 512:(nq + 1) * 512], in_=cnt_sb[cb][:]))(),
                        waits=[unit_lif[-1]], incs=[(stqk[cb], 16)])
                    unit_idx += 1

        n_qk_stores = 2 * NC_CHUNKS * QC
        v_units = []
        for t in range(TT):
            vbuf = t % 2
            for dh in range(2):
                uidx = t * 2 + dh
                bank = uidx % 4
                war = ([v_units[uidx - 4]] if uidx >= 4 else []) + ([unit_lif[-4 + (uidx + 0)]] if uidx < 4 else [])
                for ic in range(NC_CHUNKS):
                    p.emit("tensor", (lambda ic=ic, t=t, dh=dh, bank=bank: lambda tn: tn.matmul(
                        ps_v[bank][:], xnT[:, ic, t * 128:(t + 1) * 128],
                        vw_sb[:, ic, dh * 512:(dh + 1) * 512],
                        start=(ic == 0), stop=False))(),
                        waits=([TR_ALL, W_ALL] + war) if ic == 0 else ())
                p.emit("tensor", (lambda dh=dh, bank=bank: lambda tn: tn.matmul(
                    ps_v[bank][:], ones_bf[0:1, 0:128], vb_row[0:1, dh * 512:(dh + 1) * 512],
                    start=False, stop=True))(), incs=[(tns, 1)])
                mm = tns.at()
                for j in range(4):
                    p.emit("vector", (lambda j=j, bank=bank: lambda v: v.tensor_scalar(
                        out=bi[j][:], in0=ps_v[bank][:], scalar1=float(LIF_T[j]), scalar2=None,
                        op0=ALU.is_gt))(), waits=[mm] if j == 0 else (), incs=[(dve, 1)] if j == 3 else ())
                p.emit("vector", lambda v: v.tensor_add(c01[:], bi[0][:], bi[1][:]), waits=[dve.at()])
                p.emit("vector", lambda v: v.tensor_add(c23[:], bi[2][:], bi[3][:]), incs=[(dve, 1)])
                war3 = [stv[t % 2].at()] if dh == 0 else []
                p.emit("vector", (lambda vbuf=vbuf, dh=dh: lambda v: v.tensor_tensor(
                    vblk_sb[vbuf][:, dh * 8:(dh + 1) * 8, 0:64],
                    c01[:].rearrange("p (h d) -> p h d", h=8),
                    c23[:].rearrange("p (h d) -> p h d", h=8), ALU.add))(),
                    waits=war3 + [dve.at()] + ([CONSTS] if t == 0 and dh == 0 else []), incs=[(dve, 1)])
                v_units.append(dve.at())
            p.emit("sync", (lambda t=t, vbuf=vbuf: lambda s: s.dma_start(
                out=vblk_h[t, :, :], in_=vblk_sb[vbuf][:].rearrange("p h c -> p (h c)")))(),
                waits=[v_units[-1]], incs=[(stv[vbuf], 16)])

        STEPA_ST = [stqk[0].at(), stqk[1].at(), stv[0].at(), stv[1].at()]
        V_MM_END = tns.at()
        p.emit("gpsimd", lambda g: g.dma_start(
            out=ow_sb[:], in_=ow.rearrange("(c p) d -> p c d", p=128)), waits=[V_MM_END], incs=[(ld_w, 16)])
        OW_LD = ld_w.at()

        hp_mm_end = [None] * 8
        o_written = []
        exp_bank_gate = {}
        scores_bank_gate = {}
        for hp in range(8):
            lb = 0
            war = [hp_mm_end[hp - 1]] if hp >= 1 else []
            p.emit("sync", (lambda hp=hp, lb=lb: lambda s: s.dma_start(
                out=qcT_sb[lb][:], in_=qcT_h[hp * 128:(hp + 1) * 128, :]))(),
                waits=STEPA_ST + war, incs=[(ldb[lb], 16)])
            p.emit("sync", (lambda hp=hp, lb=lb: lambda s: s.dma_start(
                out=kcT_sb[lb][:], in_=kcT_h[hp * 128:(hp + 1) * 128, :]))(), incs=[(ldb[lb], 16)])
            for head, vb_t in ((2 * hp, vbA_sb), (2 * hp + 1, vbB_sb)):
                p.emit("gpsimd", (lambda head=head, vb_t=vb_t, lb=lb: lambda g: g.dma_start(
                    out=vb_t[lb][:], in_=vblk_h[:, :, head * 65:(head + 1) * 65].rearrange("t p c -> p t c")))(),
                    waits=STEPA_ST + war, incs=[(ldbg[lb], 16)])
            LDB = ldb[lb].at()
            LDBG = ldbg[lb].at()

            for qc in range(QC):
                acc_war = [o_written[-1]] if o_written else []
                mm2_at = {}
                exp_at = {}
                def emit_mm1(kt):
                    bank = kt % 2
                    w1 = [LDB, LDBG] + (acc_war if kt == 0 else [])
                    g = scores_bank_gate.get(bank)
                    if g is not None:
                        w1 = w1 + [g]
                    p.emit("tensor", (lambda lb=lb, kt=kt, qc=qc, bank=bank: lambda t: t.matmul(
                        ps_s[bank][:], kcT_sb[lb][0:64, kt * 128:(kt + 1) * 128],
                        qcT_sb[lb][0:64, qc * 512:(qc + 1) * 512], start=True, stop=True))(),
                        waits=w1)
                    p.emit("tensor", (lambda lb=lb, kt=kt, qc=qc, bank=bank: lambda t: t.matmul(
                        ps_s2[bank][:], kcT_sb[lb][64:128, kt * 128:(kt + 1) * 128],
                        qcT_sb[lb][64:128, qc * 512:(qc + 1) * 512], start=True, stop=True,
                        tile_position=(64, 0)))(), incs=[(tns, 1)])
                    mm1 = tns.at()
                    ew = exp_bank_gate.get(bank)
                    p.emit("scalar", (lambda bank=bank: lambda s: s.activation(
                        out=Eab[bank][:], in_=ps_sf[bank], func=AF.Exp, scale=1.0 / 128.0, bias=neg4[:]))(),
                        waits=[mm1] + ([ew] if ew else []), incs=[(act, 1)])
                    exp_at[kt] = act.at()
                    scores_bank_gate[bank] = exp_at[kt]
                def emit_mm2(kt):
                    bank = kt % 2
                    p.emit("tensor", (lambda lb=lb, kt=kt, bank=bank: lambda t: t.matmul(
                        ps_oA[0:65, :], vbA_sb[lb][:, kt, :], Eab[bank][:, 0:512],
                        start=(kt == 0), stop=(kt == NK - 1)))(), waits=[exp_at[kt]])
                    p.emit("tensor", (lambda lb=lb, kt=kt, bank=bank: lambda t: t.matmul(
                        ps_oB[0:65, :], vbB_sb[lb][:, kt, :], Eab[bank][:, 512:1024],
                        start=(kt == 0), stop=(kt == NK - 1)))(), incs=[(tns, 1)])
                    exp_bank_gate[bank] = tns.at()
                emit_mm1(0)
                for kt in range(1, NK):
                    emit_mm1(kt)
                    emit_mm2(kt - 1)
                emit_mm2(NK - 1)
                MM2_ALL = tns.at()
                p.emit("vector", lambda v: v.tensor_scalar(
                    out=rs4[0:1, :], in0=ps_oA[64:65, :], scalar1=4.0, scalar2=None, op0=ALU.mult),
                    waits=[MM2_ALL], incs=[(dve, 1)])
                p.emit("vector", lambda v: v.tensor_scalar(
                    out=rs4b[:], in0=ps_oB[64:65, :], scalar1=4.0, scalar2=None, op0=ALU.mult),
                    incs=[(dve, 1)])
                p.emit("vector", lambda v: v.reciprocal(recA[:], rs4[0:1, :]), waits=[dve.at()], incs=[(dve, 1)])
                p.emit("vector", lambda v: v.reciprocal(recB[:], rs4b[:]), incs=[(dve, 1)])
                p.emit("vector", lambda v: v.tensor_copy(rhiA[:], recA[:]), waits=[dve.at()], incs=[(dve, 1)])
                p.emit("vector", lambda v: v.tensor_copy(rhiB[:], recB[:]), incs=[(dve, 1)])
                p.emit("vector", lambda v: v.tensor_tensor(rloA[:], recA[:], rhiA[:], ALU.subtract),
                       waits=[dve.at()], incs=[(dve, 1)])
                p.emit("vector", lambda v: v.tensor_tensor(rloB[:], recB[:], rhiB[:], ALU.subtract),
                       incs=[(dve, 1)])
                RL = dve.at()
                p.emit("tensor", lambda t: t.matmul(ps_r[0:64, :], ones_bf[0:1, 0:64], rhiA[:], start=True, stop=False),
                       waits=[RL])
                p.emit("tensor", lambda t: t.matmul(ps_r[0:64, :], ones_bf[0:1, 0:64], rloA[:], start=False, stop=True))
                p.emit("tensor", lambda t: t.matmul(ps_r[64:128, :], ones_bf[0:1, 0:64], rhiB[:], start=True, stop=False,
                                                    tile_position=(0, 64)))
                p.emit("tensor", lambda t: t.matmul(ps_r[64:128, :], ones_bf[0:1, 0:64], rloB[:], start=False, stop=True,
                                                    tile_position=(0, 64)), incs=[(tns, 1)])
                p.emit("scalar", lambda s: s.copy(out=rbc[:], in_=ps_r[:]), waits=[tns.at()], incs=[(act, 1)])
                RBC = act.at()
                p.emit("vector", (lambda hp=hp, qc=qc: lambda v: v.tensor_tensor(
                    o_buf[0:64, hp, qc * 512:(qc + 1) * 512], ps_oA[0:64, :], rbc[0:64, :], ALU.mult))(),
                    waits=[RBC], incs=[(dve, 1)])
                p.emit("vector", (lambda hp=hp, qc=qc: lambda v: v.tensor_tensor(
                    o_buf[64:128, hp, qc * 512:(qc + 1) * 512], ps_oB[0:64, :], rbc[64:128, :], ALU.mult))(),
                    incs=[(dve, 1)])
                o_written.append(dve.at())
            hp_mm_end[hp] = (tns, tns.n)

        O_ALL = o_written[-1]

        epi_done = []
        for t in range(TT):
            b = t % 2
            war = [epi_done[t - 2]] if t >= 2 else []
            p.emit("sync", (lambda t=t, b=b: lambda s: s.dma_start(out=xi2[b][:], in_=x_in[t * 128:(t + 1) * 128, :]))(),
                   waits=war + [stx[b].at()], incs=[(ldx2[b], 16)])
            LDX = ldx2[b].at()
            mm_at = []
            for dh in range(2):
                bank = (t * 2 + dh) % 4
                for ic in range(NC_CHUNKS):
                    p.emit("tensor", (lambda ic=ic, t=t, dh=dh, bank=bank: lambda tn: tn.matmul(
                        ps_y[bank][:], o_buf[:, ic, t * 128:(t + 1) * 128],
                        ow_sb[:, ic, dh * 512:(dh + 1) * 512], start=(ic == 0), stop=False))(),
                        waits=([O_ALL, OW_LD] + war) if ic == 0 else ())
                p.emit("tensor", (lambda dh=dh, bank=bank: lambda tn: tn.matmul(
                    ps_y[bank][:], ones_bf[0:1, 0:128], ob_row[0:1, dh * 512:(dh + 1) * 512],
                    start=False, stop=True))(), incs=[(tns, 1)])
                mm_at.append(tns.at())
            b0 = (t * 2) % 4
            b1 = (t * 2 + 1) % 4
            rstd_at, aggr_at = emit_ln_stats(p, ln_post, b,
                                    [ps_y[b0][:], ps_y[b1][:]],
                                    eps, dve, act, waits=[mm_at[1]])
            STX_PREV = stx[b].at()
            for dh, bank in ((0, b0), (1, b1)):
                p.emit("vector", (lambda b=b, dh=dh, bank=bank: lambda v: v.tensor_scalar(
                    out=xo_sb[b][:, dh * 512:(dh + 1) * 512], in0=ps_y[bank][:],
                    scalar1=ln_post.mv[b][:, 0:1], scalar2=ln_post.rstd[b][:],
                    op0=ALU.subtract, op1=ALU.mult))(), waits=[rstd_at, aggr_at, STX_PREV] if dh == 0 else (),
                    incs=[(dve, 1)] if dh == 1 else ())
            p.emit("vector", (lambda b=b: lambda v: v.tensor_add(xo_sb[b][:], xo_sb[b][:], xi2[b][:]))(),
                   waits=[LDX, dve.at()], incs=[(dve, 1)])
            epi_done.append(dve.at())
            p.emit("sync", (lambda t=t, b=b: lambda s: s.dma_start(out=x_out[t * 128:(t + 1) * 128, :], in_=xo_sb[b][:]))(),
                   waits=[epi_done[-1]], incs=[(stx[b], 16)])

        return [stx[0].at(), stx[1].at()]



def emit_prologue_ln(p, S, x_in, xnbf, xt, xn_sb, ln_pre, eps, ldx, stn, dve, act):
    TT = S // 128
    prol_done = []
    for t in range(TT):
        b = t % 2
        war = [prol_done[t - 2]] if t >= 2 else []
        war2 = [stn[b].at()]
        p.emit("sync", (lambda t=t, b=b: lambda s: s.dma_start(out=xt[b][:], in_=x_in[t * 128:(t + 1) * 128, :]))(),
               waits=war, incs=[(ldx[b], 16)])
        rstd_at, aggr_at = emit_ln_stats(p, ln_pre, b,
                                         [xt[b][:, 0:512], xt[b][:, 512:1024]],
                                         eps, dve, act, waits=[ldx[b].at()])
        p.emit("vector", (lambda b=b: lambda v: v.tensor_scalar(
            out=xn_sb[b][:], in0=xt[b][:], scalar1=ln_pre.mv[b][:, 0:1], scalar2=ln_pre.rstd[b][:],
            op0=ALU.subtract, op1=ALU.mult))(), waits=[rstd_at, aggr_at] + war2, incs=[(dve, 1)])
        prol_done.append(dve.at())
        p.emit("sync", (lambda t=t, b=b: lambda s: s.dma_start(out=xnbf[t * 128:(t + 1) * 128, :], in_=xn_sb[b][:]))(),
               waits=[prol_done[-1]], incs=[(stn[b], 16)])
    return [stn[0].at(), stn[1].at()]


def emit_postln_residual(p, t, b, ps_pair, x_in, x_out, xi2, xo_sb, ln_post, eps,
                         ldx2, stx, dve, act, mm_wait, extra_war):
    p.emit("sync", (lambda t=t, b=b: lambda s: s.dma_start(out=xi2[b][:], in_=x_in[t * 128:(t + 1) * 128, :]))(),
           waits=list(extra_war) + [stx[b].at()], incs=[(ldx2[b], 16)])
    LDX = ldx2[b].at()
    STX_PREV = stx[b].at()
    rstd_at, aggr_at = emit_ln_stats(p, ln_post, b, [ps_pair[0][:], ps_pair[1][:]],
                                     eps, dve, act, waits=[mm_wait])
    for dh in range(2):
        p.emit("vector", (lambda b=b, dh=dh: lambda v: v.tensor_scalar(
            out=xo_sb[b][:, dh * 512:(dh + 1) * 512], in0=ps_pair[dh][:],
            scalar1=ln_post.mv[b][:, 0:1], scalar2=ln_post.rstd[b][:],
            op0=ALU.subtract, op1=ALU.mult))(), waits=[rstd_at, aggr_at, STX_PREV] if dh == 0 else (),
            incs=[(dve, 1)] if dh == 1 else ())
    p.emit("vector", (lambda b=b: lambda v: v.tensor_add(xo_sb[b][:], xo_sb[b][:], xi2[b][:]))(),
           waits=[LDX, dve.at()], incs=[(dve, 1)])
    done = dve.at()
    p.emit("sync", (lambda t=t, b=b: lambda s: s.dma_start(out=x_out[t * 128:(t + 1) * 128, :], in_=xo_sb[b][:]))(),
           waits=[done], incs=[(stx[b], 16)])
    return done



def emit_phase2(nc, p, stack, P, tns, act, dve, S, x_in, x_out, W):
    TT = S // 128
    NSLAB = S // 512
    NF = 16384 // 128
    debug = False
    rmT, rupT = W["rmT"], W["rupT"]
    wcTf = W["wcTf"]

    xnbf = nc.dram_tensor("p2_xnbf", [S, D], BF16, kind="Internal").ap()
    wcT = nc.dram_tensor("p2_wcT", [16384, D], BF16, kind="Internal").ap()

    if True:
        sb = lambda n, shp, d: p.AR.alloc(shp, d)
        eps = sb("eps", [128, 1], F32)
        xt = [sb(f"xt{i}", [128, D], F32) for i in range(2)]
        xn_sb = [sb(f"xn{i}", [128, D], BF16) for i in range(2)]
        ln_pre = LNState(p)
        ln_post = LNState(p)
        xi2 = xt
        xo_sb = [sb(f"xo{i}", [128, D], F32) for i in range(2)]

        wcpc = [sb(f"wcpc{i}", [128, D], BF16) for i in range(2)]
        m_stat = sb("m_stat", [128, 128 * 128], BF16)
        hT = sb("hT", [128, NF, 512], BF16)
        hT_flat = hT[:].rearrange("p a b -> p (a b)")
        aT_sb = hT_flat[0:16, 0:4096]
        bT_sb = hT_flat[0:16, 4096:4096 + 16384]
        mst_sb = hT_flat[0:16, 20480:20480 + 16384]
        xnT_slab = sb("xnT_slab", [128, NC_CHUNKS, 512], BF16)
        wtile = [sb(f"wtile{i}", [128, D], BF16) for i in range(2)]

        ld_w = p.sem("ld_w")
        ld_c = p.sem("ld_c")
        ldx = DmaSlots(p, "ldx", 2)
        stn = DmaSlots(p, "stn", 2)
        ldx2 = DmaSlots(p, "ldx2", 2)
        stx = DmaSlots(p, "stx", 2)
        stwc = DmaSlots(p, "stwc", 2)
        ldwt = DmaSlots(p, "ldwt", 2)
        trx = p.sem("trx")

        p.emit("vector", lambda v: v.memset(eps[:], 1e-5))

        mdone = []
        gelu_gate = {0: 0, 1: 0}
        ldwc = DmaSlots(p, "ldwc", 2)
        PREP_ST = []
        for fc in range(NF):
            bpc = fc % 2
            p.emit("gpsimd", (lambda fc=fc, bpc=bpc: lambda g: g.dma_start(
                out=wcpc[bpc], in_=wcTf[fc * 128:(fc + 1) * 128, :]))(),
                waits=[stwc[bpc].at()], incs=[(ldwc[bpc], 16)])
            p.emit("scalar", (lambda fc=fc, bpc=bpc: lambda s: s.dma_start(
                out=wcT[fc * 128:(fc + 1) * 128, :], in_=wcpc[bpc]))(),
                waits=[ldwc[bpc].at()], incs=[(stwc[bpc], 16)])
            PREP_ST.append([stwc[0].at(), stwc[1].at()])
        WCT_ST = PREP_ST[-1]
        if debug:
            p.emit("sync", lambda s: s.dma_start(out=wct_dbg, in_=wcT), waits=WCT_ST, incs=[(trx, 16)])

        PROL = emit_prologue_ln(p, S, x_in, xnbf, xt, xn_sb, ln_pre, eps, ldx, stn, dve, act)
        PROL_DVE = dve.at()


        p.emit("gpsimd", lambda g: g.dma_start(out=aT_sb, in_=rmT), incs=[(ld_w, 16)])
        p.emit("gpsimd", lambda g: g.dma_start(out=bT_sb, in_=rupT), incs=[(ld_w, 16)])
        TRDONE = ld_w.at()
        for hr in range(256):
            bm = hr % 2
            war = [(dve, dve.n)] if False else ([mdone[hr - 2]] if hr >= 2 else [])
            p.emit("tensor", (lambda hr=hr, bm=bm: lambda t: t.matmul(
                P[6 + bm][0:16, 0:64], aT_sb[0:16, hr * 16:(hr + 1) * 16],
                bT_sb[0:16, hr * 64:(hr + 1) * 64], start=True, stop=True))(),
                waits=[TRDONE] + war, incs=[(tns, 1)])
            p.emit("vector", (lambda hr=hr, bm=bm: lambda v: v.tensor_copy(
                mst_sb[0:16, hr * 64:(hr + 1) * 64], P[6 + bm][0:16, 0:64]))(),
                waits=[tns.at()], incs=[(dve, 1)])
            mdone.append(dve.at())
        MDONE = dve.at()
        p.emit("vector", lambda v: v.memset(m_stat[:], 0.0), incs=[(dve, 1)])
        MZ = dve.at()
        for j in range(4):
            for side in range(2):
                pos = 2 * j + side
                out_ap = m_stat[16 * pos:16 * pos + 16, :].rearrange(
                    "p (h g f c) -> p h g f c", h=4, g=8, f=4)[:, :, :, j, side * 64:side * 64 + 64]
                in_ap = mst_sb.rearrange(
                    "p (h g e c) -> p h g e c", h=4, g=8, e=8)[:, :, :, 2 * j + side, :]
                p.emit("sync", (lambda out_ap=out_ap, in_ap=in_ap: lambda s_: s_.dma_start(
                    out=out_ap, in_=in_ap))(), waits=[MDONE, MZ], incs=[(trx, 16)])
        MSTAT = trx.at()

        ldxr = DmaSlots(p, "ldxr", 2)
        epi_done = []
        rails_end = []
        for sl in range(NSLAB):
            xb = 0
            war_xr = [rails_end[sl - 1]] if sl >= 1 else []
            for c in range(NC_CHUNKS):
                p.emit("sync", (lambda sl=sl, c=c: lambda s: s.dma_start(
                    out=xnT_slab[:, c, :],
                    in_=xnbf[sl * 512:(sl + 1) * 512, c * 128:(c + 1) * 128], transpose=True))(),
                    waits=(PROL + war_xr) if c == 0 else (), incs=[(ldxr[xb], 16)])
            XNR = ldxr[xb].at()
            for pr in range(128):
                bank = pr % 2
                w = [MSTAT, XNR] if pr == 0 else []
                if sl > 0 and pr == 0:
                    w = w + [epi_done[-1]]
                war = [(act, gelu_gate[bank])] if pr >= 2 or sl > 0 else []
                c = (pr % 32) // 4
                p.emit("tensor", (lambda pr=pr, c=c, bank=bank: lambda t: t.matmul(
                    P[bank][:], m_stat[:, pr * 128:(pr + 1) * 128],
                    xnT_slab[:, c, :],
                    start=True, stop=True))(), waits=w + war, incs=[(tns, 1)])
                p.emit("scalar", (lambda pr=pr, bank=bank: lambda s: s.activation(
                    out=hT[:, pr, :], in_=P[bank][:], func=AF.Gelu))(),
                    waits=[tns.at()], incs=[(act, 1)])
                gelu_gate[bank] = act.n
            rails_end.append(tns.at())
            GELU_ALL = act.at()
            if debug and sl == 0:
                p.emit("sync", lambda s: s.dma_start(out=h_dbg, in_=hT[:].rearrange("p a b -> p (a b)")),
                       waits=[GELU_ALL], incs=[(trx, 16)])
                p.emit("sync", lambda s: s.dma_start(out=m_dbg, in_=m_stat[:]), waits=[GELU_ALL], incs=[(trx, 16)])
            mm_done = {}
            for fc in range(NF):
                bwt = fc % 2
                p.emit("sync", (lambda fc=fc, bwt=bwt: lambda s: s.dma_start(
                    out=wtile[bwt][:], in_=wcT[fc * 128:(fc + 1) * 128, :]))(),
                    waits=(PREP_ST[fc] if sl == 0 else WCT_ST) +
                          ([mm_done[fc - 2]] if fc >= 2 else ([epi_done[-1]] if epi_done else [])),
                    incs=[(ldwt[bwt], 16)])
                LW = ldwt[bwt].at()
                for qt in range(4):
                    for dh in range(2):
                        bank = qt * 2 + dh
                        w0 = []
                        if qt == 0 and dh == 0:
                            w0 = [LW, GELU_ALL]
                            if fc == 0 and epi_done:
                                w0 = w0 + [epi_done[-1]]
                        p.emit("tensor", (lambda fc=fc, qt=qt, dh=dh, bank=bank, bwt=bwt: lambda t: t.matmul(
                            P[bank][:], hT[:, fc, qt * 128:(qt + 1) * 128],
                            wtile[bwt][:, dh * 512:(dh + 1) * 512],
                            start=(fc == 0), stop=(fc == NF - 1)))(),
                            waits=w0,
                            incs=[(tns, 1)] if (qt == 3 and dh == 1) else ())
                mm_done[fc] = tns.at()
            CPROJ = tns.at()
            for qt in range(4):
                t_glob = sl * 4 + qt
                b = t_glob % 2
                done = emit_postln_residual(
                    p, t_glob, b, [P[qt * 2], P[qt * 2 + 1]], x_in, x_out,
                    xi2, xo_sb, ln_post, eps, ldx2, stx, dve, act, CPROJ,
                    ([epi_done[-2]] if len(epi_done) >= 2 else [PROL_DVE]))
                epi_done.append(done)

        return [stx[0].at(), stx[1].at()]



def emit_phase3(nc, p, stack, P, tns, act, dve, S, x_in, x_out, W):
    TT = S // 128
    QC = S // 512
    NK = S // 128
    NSTR = S // 32
    wqkv, bqkv, wout, bout = W["w_qkv"], W["b_qkv"], W["w_out"], W["b_out"]
    eye8, eyeb, bandm, stridm = W["eye8"], W["eyeb"], W["bandm"], W["stridm"]

    xnbf = nc.dram_tensor("p3_xnbf", [S, D], BF16, kind="Internal").ap()
    qT_h = nc.dram_tensor("p3_qT_h", [D, S], BF16, kind="Internal").ap()
    kT_h = nc.dram_tensor("p3_kT_h", [D, S], BF16, kind="Internal").ap()
    kstr_h = nc.dram_tensor("p3_kstr_h", [D, 128], BF16, kind="Internal").ap()
    vblk_h = nc.dram_tensor("p3_vblk_h", [TT, 128, 16 * 65], BF16, kind="Internal").ap()

    if True:
        sb = lambda n, shp, d: p.AR.alloc(shp, d)

        eps = sb("eps", [128, 1], F32)
        nC = sb("nC", [128, 1], F32)
        ones_bf = sb("ones_bf", [1, 128], BF16)

        xt = [sb(f"xt{i}", [128, D], F32) for i in range(2)]
        xn_sb = [sb(f"xn{i}", [128, D], BF16) for i in range(2)]
        ln_pre = LNState(p)
        ln_post = LNState(p)
        xi2 = xt
        xo_sb = [sb(f"xo{i}", [128, D], F32) for i in range(2)]

        xnT = sb("xnT", [128, NC_CHUNKS, S], BF16)
        o_buf = xnT

        w_sb = sb("w_sb", [128, NC_CHUNKS, 3 * D], BF16)
        wo_view = w_sb[:].rearrange("p c d -> p (c d)")
        qb_sb = sb("qb_sb", [128, NC_CHUNKS], F32)
        kb_sb = sb("kb_sb", [128, NC_CHUNKS], F32)
        b8 = sb("b8", [8, 128], F32)
        eye8_sb = sb("eye8_sb", [8, 8], F32)
        eyeb_sb = sb("eyeb_sb", [128, 128], BF16)
        vb_row = sb("vb_row", [1, D], BF16)
        ob_row = sb("ob_row", [1, D], BF16)
        bandm_sb = sb("bandm_sb", [128, 6, 512], BF16)
        stridm_sb = sb("stridm_sb", [128, QC, 512], BF16)

        out_sb = [sb(f"out{i}", [128, 512], BF16) for i in range(2)]
        vblk_sb = [sb(f"vblk{i}", [128, 16, 65], BF16) for i in range(2)]
        kstr_sb = sb("kstr_sb", [128, 128], BF16)

        qT_sb = sb("qT_sb", [128, S], BF16)
        kT_sb = sb("kT_sb", [128, S], BF16)
        vbA_sb = sb("vbA_sb", [128, NK, 65], BF16)
        vbB_sb = sb("vbB_sb", [128, NK, 65], BF16)
        kstrp = sb("kstrp", [128, 128], BF16)
        vstrA = sb("vstrA", [128, 65], BF16)
        vstrB = sb("vstrB", [128, 65], BF16)
        Eab = [sb(f"Eab{i}", [128, 1024], BF16) for i in range(2)]
        rs4 = sb("rs4", [1, 512], F32)
        rs4b = sb("rs4b", [1, 512], F32)
        recA = sb("recA", [1, 512], F32)
        recB = sb("recB", [1, 512], F32)
        rhiA = sb("rhiA", [1, 512], BF16)
        rloA = sb("rloA", [1, 512], BF16)
        rhiB = sb("rhiB", [1, 512], BF16)
        rloB = sb("rloB", [1, 512], BF16)
        rbc = sb("rbc", [128, 512], F32)

        ps_qk = [P[0], P[1], P[2], P[3]]
        ps_v = [P[0], P[1], P[2], P[3]]
        ps_s = [P[0], P[2]]
        ps_s2 = [P[1], P[3]]
        ps_sf = [p.PQ[0][:, 0:1024], p.PQ[1][:, 0:1024]]
        ps_oA = P[4]
        ps_oB = P[5]
        ps_r = P[6]
        ps_y = [P[0], P[1], P[2], P[3]]

        ld_w = p.sem("ld_w")
        ld_c = p.sem("ld_c")
        ldx = DmaSlots(p, "ldx", 2)
        stn = DmaSlots(p, "stn", 2)
        tr = p.sem("tr")
        stqk = DmaSlots(p, "stqk", 2)
        stks = DmaSlots(p, "stks", 1)
        stv = DmaSlots(p, "stv", 2)
        ldb = DmaSlots(p, "ldb", 2)
        ldbg = DmaSlots(p, "ldbg", 2)
        ldx2 = DmaSlots(p, "ldx2", 2)
        stx = DmaSlots(p, "stx", 2)

        p.emit("vector", lambda v: v.memset(eps[:], 1e-5))
        p.emit("vector", lambda v: v.memset(nC[:], -16.0))
        p.emit("vector", lambda v: v.memset(ones_bf[:], 1.0))
        for i in range(2):
            p.emit("vector", (lambda i=i: lambda v: v.memset(vblk_sb[i][:, :, 64:65], 1.0))())
        p.emit("vector", lambda v: v.memset(vstrA[:], 0.0))
        p.emit("vector", lambda v: v.memset(vstrB[:], 0.0), incs=[(dve, 1)])
        CONSTS = dve.at()

        p.emit("gpsimd", lambda g: g.dma_start(
            out=w_sb[:], in_=wqkv.rearrange("(c p) d -> p c d", p=128)), incs=[(ld_w, 16)])
        p.emit("gpsimd", lambda g: g.dma_start(
            out=vb_row[:], in_=bqkv[2 * D:3 * D].rearrange("(u d) -> u d", u=1)), incs=[(ld_w, 16)])
        p.emit("gpsimd", lambda g: g.dma_start(
            out=ob_row[:], in_=bout.rearrange("(u d) -> u d", u=1)), incs=[(ld_w, 16)])
        p.emit("gpsimd", lambda g: g.dma_start(
            out=bandm_sb[:], in_=bandm.rearrange("m p c -> p m c")), incs=[(ld_w, 16)])
        p.emit("gpsimd", lambda g: g.dma_start(
            out=stridm_sb[:], in_=stridm.rearrange("m p c -> p m c")), incs=[(ld_w, 16)])
        p.emit("sync", lambda s_: s_.dma_start(out=eye8_sb[:], in_=eye8), incs=[(ld_c, 16)])
        p.emit("sync", lambda s_: s_.dma_start(out=eyeb_sb[:], in_=eyeb), incs=[(ld_c, 16)])
        p.emit("sync", lambda s_: s_.dma_start(
            out=b8[:], in_=bqkv[0:D].rearrange("(c p) -> c p", p=128)), incs=[(ld_c, 16)])
        W_ALL = ld_w.at()
        C_ALL = ld_c.at()
        p.emit("tensor", lambda t: t.matmul(P[7][0:128, 0:8], b8[:], eye8_sb[:], start=True, stop=True),
               waits=[W_ALL, C_ALL], incs=[(tns, 1)])
        p.emit("vector", lambda v: v.tensor_copy(qb_sb[:], P[7][0:128, 0:8]), waits=[tns.at()], incs=[(dve, 1)])
        BQ = dve.at()
        p.emit("sync", lambda s_: s_.dma_start(
            out=b8[:], in_=bqkv[D:2 * D].rearrange("(c p) -> c p", p=128)), waits=[BQ], incs=[(ld_c, 16)])
        p.emit("tensor", lambda t: t.matmul(P[7][0:128, 0:8], b8[:], eye8_sb[:], start=True, stop=True),
               waits=[ld_c.at(), BQ], incs=[(tns, 1)])
        p.emit("vector", lambda v: v.tensor_copy(kb_sb[:], P[7][0:128, 0:8]), waits=[tns.at()], incs=[(dve, 1)])
        BIAS_T = dve.at()

        PROL = emit_prologue_ln(p, S, x_in, xnbf, xt, xn_sb, ln_pre, eps, ldx, stn, dve, act)
        for c in range(NC_CHUNKS):
            p.emit("sync", (lambda c=c: lambda s: s.dma_start(
                out=xnT[:, c, :], in_=xnbf[:, c * 128:(c + 1) * 128], transpose=True))(),
                   waits=PROL, incs=[(tr, 16)])
        TR_ALL = tr.at()

        unit_done = []
        unit_idx = 0
        for proj, b_tile, outT in (("q", qb_sb, qT_h), ("k", kb_sb, kT_h)):
            woff = 0 if proj == "q" else D
            for oc in range(NC_CHUNKS):
                nstr_done = 0
                for nq in range(QC):
                    bank = unit_idx % 4
                    war = [unit_done[unit_idx - 4]] if unit_idx >= 4 else []
                    for ic in range(NC_CHUNKS):
                        p.emit("tensor", (lambda ic=ic, oc=oc, nq=nq, bank=bank, woff=woff: lambda t: t.matmul(
                            ps_qk[bank][:], w_sb[:, ic, woff + oc * 128:woff + (oc + 1) * 128],
                            xnT[:, ic, nq * 512:(nq + 1) * 512],
                            start=(ic == 0), stop=(ic == NC_CHUNKS - 1)))(),
                            waits=([TR_ALL, W_ALL, BIAS_T] + war) if ic == 0 else (),
                            incs=[(tns, 1)] if ic == NC_CHUNKS - 1 else ())
                    mm = tns.at()
                    ob_ = unit_idx % 2
                    war3 = [stqk[ob_].at()]
                    p.emit("vector", (lambda bank=bank, b_tile=b_tile, oc=oc, ob_=ob_: lambda v: v.tensor_scalar(
                        out=out_sb[ob_][:], in0=ps_qk[bank][:], scalar1=b_tile[:, oc:oc + 1], scalar2=None,
                        op0=ALU.add))(), waits=[mm] + war3, incs=[(dve, 1)])
                    if proj == "k":
                        kw_ = [stks[0].at()] if nq == 0 else []
                        p.emit("vector", (lambda bank=bank, b_tile=b_tile, oc=oc, nq=nq: lambda v: v.tensor_scalar(
                            out=kstr_sb[:, nq * 16:(nq + 1) * 16],
                            in0=ps_qk[bank][:].rearrange("p (a b) -> p a b", b=32)[:, :, 0],
                            scalar1=b_tile[:, oc:oc + 1], scalar2=None, op0=ALU.add))(), waits=kw_, incs=[(dve, 1)])
                    unit_done.append(dve.at())
                    p.emit("sync", (lambda outT=outT, oc=oc, nq=nq, ob_=ob_: lambda s: s.dma_start(
                        out=outT[oc * 128:(oc + 1) * 128, nq * 512:(nq + 1) * 512], in_=out_sb[ob_][:]))(),
                        waits=[unit_done[-1]], incs=[(stqk[ob_], 16)])
                    unit_idx += 1
                if proj == "k":
                    p.emit("sync", (lambda oc=oc: lambda s: s.dma_start(
                        out=kstr_h[oc * 128:(oc + 1) * 128, 0:NSTR], in_=kstr_sb[:, 0:NSTR]))(),
                        waits=[unit_done[-1]], incs=[(stks[0], 16)])

        v_units = []
        for t in range(TT):
            vbuf = t % 2
            for dh in range(2):
                uidx = t * 2 + dh
                bank = uidx % 4
                war = ([v_units[uidx - 4]] if uidx >= 4 else []) + ([unit_done[-4 + uidx]] if uidx < 4 else [])
                for ic in range(NC_CHUNKS):
                    p.emit("tensor", (lambda ic=ic, t=t, dh=dh, bank=bank: lambda tn: tn.matmul(
                        ps_v[bank][:], xnT[:, ic, t * 128:(t + 1) * 128],
                        w_sb[:, ic, 2 * D + dh * 512:2 * D + (dh + 1) * 512],
                        start=(ic == 0), stop=False))(),
                        waits=([TR_ALL, W_ALL] + war) if ic == 0 else ())
                p.emit("tensor", (lambda dh=dh, bank=bank: lambda tn: tn.matmul(
                    ps_v[bank][:], ones_bf[0:1, 0:128], vb_row[0:1, dh * 512:(dh + 1) * 512],
                    start=False, stop=True))(), incs=[(tns, 1)])
                mm = tns.at()
                war3 = [stv[t % 2].at()] if dh == 0 else []
                p.emit("vector", (lambda vbuf=vbuf, dh=dh, bank=bank: lambda v: v.tensor_copy(
                    vblk_sb[vbuf][:, dh * 8:(dh + 1) * 8, 0:64],
                    ps_v[bank][:].rearrange("p (h d) -> p h d", h=8)))(),
                    waits=[mm] + war3 + ([CONSTS] if t == 0 and dh == 0 else []), incs=[(dve, 1)])
                v_units.append(dve.at())
            p.emit("sync", (lambda t=t, vbuf=vbuf: lambda s: s.dma_start(
                out=vblk_h[t, :, :], in_=vblk_sb[vbuf][:].rearrange("p h c -> p (h c)")))(),
                waits=[v_units[-1]], incs=[(stv[vbuf], 16)])

        STEPA_ST = [stqk[0].at(), stqk[1].at(), stv[0].at(), stv[1].at(), stks[0].at()]
        V_MM_END = tns.at()
        p.emit("gpsimd", lambda g: g.dma_start(
            out=w_sb[:, :, 0:D],
            in_=wout.rearrange("(c p) d -> p c d", p=128)), waits=[V_MM_END], incs=[(ld_w, 16)])
        OW_LD = ld_w.at()

        hp_mm_end = [None] * 8
        o_written = []
        exp_bank_gate = {}
        scores_bank_gate = {}
        for hp in range(8):
            war = [hp_mm_end[hp - 1]] if hp >= 1 else []
            p.emit("sync", (lambda hp=hp: lambda s: s.dma_start(
                out=qT_sb[:], in_=qT_h[hp * 128:(hp + 1) * 128, :]))(),
                waits=STEPA_ST + war, incs=[(ldb[0], 16)])
            p.emit("sync", (lambda hp=hp: lambda s: s.dma_start(
                out=kT_sb[:], in_=kT_h[hp * 128:(hp + 1) * 128, :]))(), incs=[(ldb[0], 16)])
            p.emit("sync", (lambda hp=hp: lambda s: s.dma_start(
                out=kstrp[:, 0:NSTR], in_=kstr_h[hp * 128:(hp + 1) * 128, 0:NSTR]))(), incs=[(ldb[0], 16)])
            for head, vb_t in ((2 * hp, vbA_sb), (2 * hp + 1, vbB_sb)):
                p.emit("gpsimd", (lambda head=head, vb_t=vb_t: lambda g: g.dma_start(
                    out=vb_t[:], in_=vblk_h[:, :, head * 65:(head + 1) * 65].rearrange("t p c -> p t c")))(),
                    waits=STEPA_ST + war, incs=[(ldbg[0], 16)])
            for head, vs_t in ((2 * hp, vstrA), (2 * hp + 1, vstrB)):
                for rc in range(4):
                    p.emit("gpsimd", (lambda head=head, vs_t=vs_t, rc=rc: lambda g: g.dma_start(
                        out=vs_t[rc:NSTR:4, :], in_=vblk_h[:, rc * 32, head * 65:(head + 1) * 65]))(),
                        waits=[ldbg[0].at(), ld_w.at(), CONSTS], incs=[(ldbg[0], 16)])
            LDB = ldb[0].at()
            LDBG = ldbg[0].at()

            for qc in range(QC):
                acc_war = [o_written[-1]] if o_written else []
                kts = [kt for kt in range(max(0, 4 * qc - 1), min(NK, 4 * qc + 5))]
                tiles = [("band", kt) for kt in kts] + [("str", 0)]
                ntile = len(tiles)
                exp_at = {}
                tinfo = []
                for ti, (kind, kt) in enumerate(tiles):
                    if kind == "band":
                        midx = kt - (4 * qc - 1)
                        tinfo.append((kT_sb[0:64, kt * 128:(kt + 1) * 128],
                                      kT_sb[64:128, kt * 128:(kt + 1) * 128],
                                      bandm_sb[:, midx, :], 128,
                                      vbA_sb[:, kt, :], vbB_sb[:, kt, :]))
                    else:
                        tinfo.append((kstrp[0:64, 0:NSTR], kstrp[64:128, 0:NSTR],
                                      stridm_sb[:, qc, :], NSTR,
                                      vstrA[0:NSTR, :], vstrB[0:NSTR, :]))

                def emit_mm1(ti):
                    bank = ti % 2
                    lhsA, lhsB, maskap, rows, vA, vB = tinfo[ti]
                    w1 = [LDB, LDBG] + (acc_war if ti == 0 else [])
                    g = scores_bank_gate.get(bank)
                    if g is not None:
                        w1 = w1 + [g]
                    p.emit("tensor", (lambda lhsA=lhsA, qc=qc, bank=bank, rows=rows: lambda t: t.matmul(
                        ps_s[bank][0:rows, :], lhsA, qT_sb[0:64, qc * 512:(qc + 1) * 512],
                        start=True, stop=False))(), waits=w1)
                    p.emit("tensor", (lambda maskap=maskap, bank=bank, rows=rows: lambda t: t.matmul(
                        ps_s[bank][0:rows, :], eyeb_sb[:, 0:rows], maskap,
                        start=False, stop=True))())
                    p.emit("tensor", (lambda lhsB=lhsB, qc=qc, bank=bank, rows=rows: lambda t: t.matmul(
                        ps_s2[bank][0:rows, :], lhsB, qT_sb[64:128, qc * 512:(qc + 1) * 512],
                        start=True, stop=False, tile_position=(64, 0)))())
                    p.emit("tensor", (lambda maskap=maskap, bank=bank, rows=rows: lambda t: t.matmul(
                        ps_s2[bank][0:rows, :], eyeb_sb[:, 0:rows], maskap,
                        start=False, stop=True))(), incs=[(tns, 1)])
                    mm1 = tns.at()
                    ew = exp_bank_gate.get(bank)
                    p.emit("scalar", (lambda bank=bank, rows=rows: lambda s: s.activation(
                        out=Eab[bank][0:rows, :], in_=ps_sf[bank][0:rows, :], func=AF.Exp, scale=0.125,
                        bias=nC[0:rows, :]))(), waits=[mm1] + ([ew] if ew else []), incs=[(act, 1)])
                    exp_at[ti] = act.at()
                    scores_bank_gate[bank] = exp_at[ti]

                def emit_mm2(ti):
                    bank = ti % 2
                    lhsA, lhsB, maskap, rows, vA, vB = tinfo[ti]
                    p.emit("tensor", (lambda vA=vA, bank=bank, ti=ti, rows=rows: lambda t: t.matmul(
                        ps_oA[0:65, :], vA[0:rows, :], Eab[bank][0:rows, 0:512],
                        start=(ti == 0), stop=(ti == ntile - 1)))(), waits=[exp_at[ti]])
                    p.emit("tensor", (lambda vB=vB, bank=bank, ti=ti, rows=rows: lambda t: t.matmul(
                        ps_oB[0:65, :], vB[0:rows, :], Eab[bank][0:rows, 512:1024],
                        start=(ti == 0), stop=(ti == ntile - 1)))(), incs=[(tns, 1)])
                    exp_bank_gate[bank] = tns.at()

                emit_mm1(0)
                for ti in range(1, ntile):
                    emit_mm1(ti)
                    emit_mm2(ti - 1)
                emit_mm2(ntile - 1)
                MM2_ALL = tns.at()
                p.emit("vector", lambda v: v.tensor_copy(rs4[:], ps_oA[64:65, :]), waits=[MM2_ALL], incs=[(dve, 1)])
                p.emit("vector", lambda v: v.tensor_copy(rs4b[:], ps_oB[64:65, :]), incs=[(dve, 1)])
                p.emit("vector", lambda v: v.reciprocal(recA[:], rs4[:]), waits=[dve.at()], incs=[(dve, 1)])
                p.emit("vector", lambda v: v.reciprocal(recB[:], rs4b[:]), incs=[(dve, 1)])
                p.emit("vector", lambda v: v.tensor_copy(rhiA[:], recA[:]), waits=[dve.at()], incs=[(dve, 1)])
                p.emit("vector", lambda v: v.tensor_copy(rhiB[:], recB[:]), incs=[(dve, 1)])
                p.emit("vector", lambda v: v.tensor_tensor(rloA[:], recA[:], rhiA[:], ALU.subtract),
                       waits=[dve.at()], incs=[(dve, 1)])
                p.emit("vector", lambda v: v.tensor_tensor(rloB[:], recB[:], rhiB[:], ALU.subtract),
                       incs=[(dve, 1)])
                RL = dve.at()
                p.emit("tensor", lambda t: t.matmul(ps_r[0:64, :], ones_bf[0:1, 0:64], rhiA[:], start=True, stop=False),
                       waits=[RL])
                p.emit("tensor", lambda t: t.matmul(ps_r[0:64, :], ones_bf[0:1, 0:64], rloA[:], start=False, stop=True))
                p.emit("tensor", lambda t: t.matmul(ps_r[64:128, :], ones_bf[0:1, 0:64], rhiB[:], start=True, stop=False,
                                                    tile_position=(0, 64)))
                p.emit("tensor", lambda t: t.matmul(ps_r[64:128, :], ones_bf[0:1, 0:64], rloB[:], start=False, stop=True,
                                                    tile_position=(0, 64)), incs=[(tns, 1)])
                p.emit("scalar", lambda s: s.copy(out=rbc[:], in_=ps_r[:]), waits=[tns.at()], incs=[(act, 1)])
                RBC = act.at()
                p.emit("vector", (lambda hp=hp, qc=qc: lambda v: v.tensor_tensor(
                    o_buf[0:64, hp, qc * 512:(qc + 1) * 512], ps_oA[0:64, :], rbc[0:64, :], ALU.mult))(),
                    waits=[RBC], incs=[(dve, 1)])
                p.emit("vector", (lambda hp=hp, qc=qc: lambda v: v.tensor_tensor(
                    o_buf[64:128, hp, qc * 512:(qc + 1) * 512], ps_oB[0:64, :], rbc[64:128, :], ALU.mult))(),
                    incs=[(dve, 1)])
                o_written.append(dve.at())
            hp_mm_end[hp] = (tns, tns.n)

        O_ALL = o_written[-1]

        epi_done = []
        for t in range(TT):
            b = t % 2
            war = [epi_done[t - 2]] if t >= 2 else []
            mm_at = []
            for dh in range(2):
                bank = (t * 2 + dh) % 4
                for ic in range(NC_CHUNKS):
                    p.emit("tensor", (lambda ic=ic, t=t, dh=dh, bank=bank: lambda tn: tn.matmul(
                        ps_y[bank][:], o_buf[:, ic, t * 128:(t + 1) * 128],
                        w_sb[:, ic, dh * 512:(dh + 1) * 512], start=(ic == 0), stop=False))(),
                        waits=([O_ALL, OW_LD] + war) if ic == 0 else ())
                p.emit("tensor", (lambda dh=dh, bank=bank: lambda tn: tn.matmul(
                    ps_y[bank][:], ones_bf[0:1, 0:128], ob_row[0:1, dh * 512:(dh + 1) * 512],
                    start=False, stop=True))(), incs=[(tns, 1)])
                mm_at.append(tns.at())
            done = emit_postln_residual(p, t, b, [ps_y[(t * 2) % 4], ps_y[(t * 2 + 1) % 4]],
                                        x_in, x_out, xi2, xo_sb, ln_post, eps, ldx2, stx,
                                        dve, act, mm_at[1], war)
            epi_done.append(done)

        return [stx[0].at(), stx[1].at()]


def phase3_masks(S):
    QC = S // 512
    NEG = np.float32(-1e30)
    bandm = np.zeros((6, 128, 512), np.float32)
    for idx in range(6):
        m = 1 - idx
        r = np.arange(128)[:, None]
        c = np.arange(512)[None, :]
        diff = c - r + 128 * m
        bandm[idx] = np.where(np.abs(diff) <= 64, 0.0, NEG)
    stridm = np.zeros((QC, 128, 512), np.float32)
    for qc in range(QC):
        r = np.arange(128)[:, None]
        c = np.arange(512)[None, :]
        i = qc * 512 + c
        diff = i - 32 * r
        keep = ~(np.abs(diff) <= 64)
        if S // 32 < 128:
            keep = keep & (r < S // 32)
        stridm[qc] = np.where(keep, 0.0, NEG)
    bf = lambda a: a.astype(ml_dtypes.bfloat16)
    return bf(bandm), bf(stridm)



def emit_phase4(nc, p, stack, P, tns, act, dve, S, x_in, x_out, W):
    TT = S // 128
    NSLAB = S // 512
    NW = S // 512
    HID = 2730
    NH2 = 22
    INNER = 512
    wqkv, qscale, kscale, wo, lng, lnb = (W["wqkv"], W["q_scale"], W["k_scale"],
                                          W["wo"], W["ln_g"], W["ln_b"])
    w1, b1, w2, b2, eyeb = W["w1"], W["b1"], W["w2"], W["b2"], W["eyeb"]
    tqc, tqs, tkc, tks = W["tqc"], W["tqs"], W["tkc"], W["tks"]
    mask4 = W["mask4"]

    x4 = nc.dram_tensor("p4_x4p", [S + 128, D], F32, kind="Internal").ap()
    x4bf = nc.dram_tensor("p4_x4bfp", [S + 128, D], BF16, kind="Internal").ap()
    qn_h = nc.dram_tensor("p4_qn_h", [S, INNER], BF16, kind="Internal").ap()
    kn_h = nc.dram_tensor("p4_kn_h", [S, INNER], BF16, kind="Internal").ap()
    qT_h = nc.dram_tensor("p4_qT_h", [INNER, S], BF16, kind="Internal").ap()
    kT_h = nc.dram_tensor("p4_kT_h", [INNER, S], BF16, kind="Internal").ap()
    o4T_h = nc.dram_tensor("p4_o4T_h", [INNER, S], BF16, kind="Internal").ap()
    w1bfA = [nc.dram_tensor(f"p4_w1bfA{l}", [D, 2816], BF16, kind="Internal").ap() for l in range(2)]
    w1bfB = [nc.dram_tensor(f"p4_w1bfB{l}", [D, 2816], BF16, kind="Internal").ap() for l in range(2)]
    w2bf = [nc.dram_tensor(f"p4_w2bf{l}", [HID, D], BF16, kind="Internal").ap() for l in range(2)]
    OFF = 128

    if True:
        sb = lambda n, shp, d: p.AR.alloc(shp, d)

        eps = sb("eps", [128, 1], F32)
        ones_bf = sb("ones_bf", [1, 512], BF16)
        ones_f = sb("ones_f", [1, 128], F32)

        xt = [sb(f"xt{i}", [128, D], F32) for i in range(2)]
        xn_sb = [sb(f"xn{i}", [128, D], BF16) for i in range(2)]
        ln_pre = LNState(p)
        ln_post = LNState(p)
        xo_sb = [sb(f"xo{i}", [128, D], F32) for i in range(2)]

        tq_c = sb("tq_c", [128, TT, 64], F32)
        tq_s = sb("tq_s", [128, TT, 64], F32)
        tk_c = sb("tk_c", [128, TT, 64], F32)
        tk_s = sb("tk_s", [128, TT, 64], F32)
        qs_bc = sb("qs_bc", [128, 64], F32)
        ks_bc = sb("ks_bc", [128, 64], F32)
        g_bc = sb("g_bc", [128, D], F32)
        b_bc = sb("b_bc", [128, D], F32)
        sc_row = sb("sc_row", [1, 64], F32)
        gb_row = sb("gb_row", [1, 512], F32)

        b2_row = sb("b2_row", [1, D], BF16)
        mask_sb = sb("mask_sb", [128, 8, 512], BF16)
        eyeb_sb = sb("eyeb_sb", [128, 128], BF16)

        AR = max(20480 + 2 * S + TT * 520 + 2 * 512, 13312 + NH2 * 512 + 2048)
        arena = sb("arena", [128, AR], BF16)
        wq_sb = arena[:, 0:12288].rearrange("p (c d) -> p c d", d=3 * INNER)
        wo_sb = arena[:, 12288:16384].rearrange("p (c d) -> p c d", d=D)
        hT_slab = arena[:, 16384:20480].rearrange("p (c n) -> p c n", n=512)
        qT_sb = arena[:, 20480:20480 + S]
        kT_sb = arena[:, 20480 + S:20480 + 2 * S]
        _vb0 = 20480 + 2 * S
        vblk = arena[:, _vb0:_vb0 + TT * 520].rearrange("p (t h c) -> p t h c", h=8, c=65)
        o_stage = [arena[:, _vb0 + TT * 520 + i * 512:_vb0 + TT * 520 + (i + 1) * 512] for i in range(2)]
        w1pa = [arena[:, i * 2048:(i + 1) * 2048].rearrange("p (c d) -> p c d", d=256) for i in range(2)]
        w1pg = [arena[:, 4096 + i * 2048:4096 + (i + 1) * 2048].rearrange("p (c d) -> p c d", d=256) for i in range(2)]
        b1p = [arena[0:1, 8192 + i * 512:8192 + (i + 1) * 512] for i in range(2)]
        hlnT_slab = arena[:, 9216:9216 + 4096].rearrange("p (c n) -> p c n", n=512)
        h2T_slab = arena[:, 13312:13312 + NH2 * 512].rearrange("p (c n) -> p c n", n=512)
        w2p = [arena[:, 24576 + i * 1024:24576 + (i + 1) * 1024] for i in range(2)]
        oT_t = [sb(f"oT_t{i}", [128, 4, 128], BF16) for i in range(2)]

        ss8 = sb("ss8", [128, 8], F32)
        rq8 = sb("rq8", [128, 8], F32)
        t0 = sb("t0", [128, 512], F32)
        t1 = sb("t1", [128, 512], F32)
        t2 = sb("t2", [128, 512], F32)
        qn_sb2 = [sb(f"qn_sb{i}", [128, 512], BF16) for i in range(2)]
        hln_sb = [sb(f"hln{i}", [128, D], BF16) for i in range(2)]
        gact = sb("gact", [128, 512], BF16)
        Eab = [sb(f"Eab{i}", [128, 1024], BF16) for i in range(2)]
        cmx = sb("cmx", [128, 16], F32)
        cm1 = sb("cm1", [128, 1], F32)
        cm1b = sb("cm1b", [128, 1], BF16)
        negC = sb("negC", [128, 1], F32)
        rs4b = sb("rs4b", [1, 512], F32)
        recA = sb("recA", [1, 512], F32)
        recB = sb("recB", [1, 512], F32)
        rhiA = sb("rhiA", [1, 512], BF16)
        rloA = sb("rloA", [1, 512], BF16)
        rhiB = sb("rhiB", [1, 512], BF16)
        rloB = sb("rloB", [1, 512], BF16)
        rbc = sb("rbc", [128, 512], F32)
        rs4 = rbc[0:1, :]
        y4 = [sb(f"y4_{i}", [128, D], F32) for i in range(2)]

        w1stg = [sb(f"w1stg{i}", [128, 256], BF16) for i in range(2)]
        pw_ld = DmaSlots(p, "pw_ld", 2)
        pw_st = DmaSlots(p, "pw_st", 2)

        ld_w = p.sem("ld_w")
        ld_c = p.sem("ld_c")
        ldx = DmaSlots(p, "ldx", 2)
        stn = DmaSlots(p, "stn", 2)
        trh = DmaSlots(p, "trh", 1)
        stqn = DmaSlots(p, "stqn", 2)
        trqk = DmaSlots(p, "trqk", 1)
        ldb = DmaSlots(p, "ldb", 1)
        ldw1 = DmaSlots(p, "ldw1", 2)
        ldw2 = DmaSlots(p, "ldw2", 2)
        ldx2 = DmaSlots(p, "ldx2", 2)
        ldx3 = DmaSlots(p, "ldx3", 2)
        stx = DmaSlots(p, "stx", 2)
        ldms = DmaSlots(p, "ldms", 1)
        ldwq = DmaSlots(p, "ldwq", 1)
        ldb2s = DmaSlots(p, "ldb2s", 1)

        p.emit("vector", lambda v: v.memset(eps[:], 1e-5))
        p.emit("vector", lambda v: v.memset(ones_bf[:], 1.0))
        p.emit("vector", lambda v: v.memset(ones_f[:], 1.0))
        p.emit("vector", lambda v: v.memset(xo_sb[0][:], 0.0))
        p.emit("vector", lambda v: v.memset(xn_sb[0][:], 0.0))
        p.emit("vector", lambda v: v.memset(cmx[:], -3e38), incs=[(dve, 1)])
        CONSTS = dve.at()
        for tbl, dst in ((tqc, tq_c), (tqs, tq_s), (tkc, tk_c), (tks, tk_s)):
            p.emit("sync", (lambda tbl=tbl, dst=dst: lambda s: s.dma_start(
                out=dst[:], in_=tbl.rearrange("(tt p) d -> p tt d", p=128)))(), incs=[(ld_c, 16)])
        p.emit("sync", lambda s: s.dma_start(out=eyeb_sb[:], in_=eyeb), incs=[(ld_c, 16)])
        p.emit("gpsimd", lambda g: g.dma_start(out=mask_sb[:], in_=mask4.rearrange("m p c -> p m c")),
               incs=[(ldms[0], 16)])
        p.emit("sync", lambda s: s.dma_start(out=x4[0:128, :], in_=xo_sb[0][:]), waits=[CONSTS], incs=[(stn[0], 16)])
        p.emit("sync", lambda s: s.dma_start(out=x4bf[0:128, :], in_=xn_sb[0][:]), incs=[(stn[0], 16)])

        prol_done = []
        for t in range(TT):
            b = t % 2
            war = [prol_done[t - 2]] if t >= 2 else []
            p.emit("sync", (lambda t=t, b=b: lambda s: s.dma_start(out=xt[b][:], in_=x_in[t * 128:(t + 1) * 128, :]))(),
                   waits=war + [stn[b].at()], incs=[(ldx[b], 16)])
            rstd_at, aggr_at = emit_ln_stats(p, ln_pre, b, [xt[b][:, 0:512], xt[b][:, 512:1024]],
                                             eps, dve, act, waits=[ldx[b].at()])
            war2 = [stn[b].at()]
            p.emit("vector", (lambda b=b: lambda v: v.tensor_scalar(
                out=xt[b][:], in0=xt[b][:], scalar1=ln_pre.mv[b][:, 0:1], scalar2=ln_pre.rstd[b][:],
                op0=ALU.subtract, op1=ALU.mult))(), waits=[rstd_at, aggr_at] + war2, incs=[(dve, 1)])
            p.emit("vector", (lambda b=b: lambda v: v.tensor_copy(xn_sb[b][:], xt[b][:]))(),
                   waits=[dve.at()], incs=[(dve, 1)])
            prol_done.append(dve.at())
            p.emit("sync", (lambda t=t, b=b: lambda s: s.dma_start(
                out=x4[OFF + t * 128:OFF + (t + 1) * 128, :], in_=xt[b][:]))(),
                waits=[prol_done[-1]], incs=[(stn[b], 16)])
            p.emit("sync", (lambda t=t, b=b: lambda s: s.dma_start(
                out=x4bf[OFF + t * 128:OFF + (t + 1) * 128, :], in_=xn_sb[b][:]))(), incs=[(stn[b], 16)])
        X4_ST = [stn[0].at(), stn[1].at()]

        ldbg = DmaSlots(p, "ldbg", 1)
        sto4 = DmaSlots(p, "sto4", 2)
        ldo4 = DmaSlots(p, "ldo4", 2)

        LAYER_GATE = None
        for l in range(2):
            X4_CUR = [stn[0].at(), stn[1].at()]
            lg = [LAYER_GATE] if LAYER_GATE else []
            p.emit("gpsimd", (lambda l=l: lambda g: g.dma_start(
                out=wq_sb[:], in_=wqkv[l].rearrange("(c p) d -> p c d", p=128)))(), waits=lg, incs=[(ldwq[0], 16)])
            p.emit("gpsimd", (lambda l=l: lambda g: g.dma_start(
                out=wo_sb[:], in_=wo[l].rearrange("(c p) d -> p c d", p=128)))(), incs=[(ldwq[0], 16)])
            WQKV = ldwq[0].at()
            if l == 0:
                PIECES = [(k * 256, 256, 256) for k in range(10)] + [(2560, 170, 256)]
                pj = 0
                prep_jobs = []
                for l2 in range(2):
                    for r in range(8):
                        for (a0, aw, dw) in PIECES:
                            prep_jobs.append((w1[l2][r * 128:(r + 1) * 128, a0:a0 + aw],
                                              w1bfA[l2][r * 128:(r + 1) * 128, a0:a0 + dw], 128))
                            prep_jobs.append((w1[l2][r * 128:(r + 1) * 128, HID + a0:HID + a0 + aw],
                                              w1bfB[l2][r * 128:(r + 1) * 128, a0:a0 + dw], 128))
                for l2 in range(2):
                    for i2 in range(NH2):
                        wi2 = 128 if i2 < 21 else 42
                        for qh in range(4):
                            prep_jobs.append((w2[l2][i2 * 128:i2 * 128 + wi2, qh * 256:(qh + 1) * 256],
                                              w2bf[l2][i2 * 128:i2 * 128 + wi2, qh * 256:(qh + 1) * 256], wi2))
                for (src_ap, dst_ap, np_) in prep_jobs:
                    s_ = pj % 2
                    p.emit("gpsimd", (lambda src_ap=src_ap, np_=np_, s_=s_: lambda g: g.dma_start(
                        out=w1stg[s_][0:np_, 0:src_ap.shape[1]], in_=src_ap))(),
                        waits=[pw_st[s_].at()], incs=[(pw_ld[s_], 16)])
                    p.emit("gpsimd", (lambda dst_ap=dst_ap, np_=np_, s_=s_: lambda g: g.dma_start(
                        out=dst_ap, in_=w1stg[s_][0:np_, 0:dst_ap.shape[1]]))(),
                        waits=[pw_ld[s_].at()], incs=[(pw_st[s_], 16)])
                    pj += 1
                PREP_W1 = [pw_st[0].at(), pw_st[1].at()]
            p.emit("sync", (lambda l=l: lambda s: s.dma_start(
                out=sc_row[:], in_=qscale[l].rearrange("(u d) -> u d", u=1)))(), waits=lg, incs=[(ld_c, 16)])
            SC1 = ld_c.at()
            p.emit("tensor", lambda t: t.matmul(P[7][:, 0:64], ones_f[0:1, 0:128], sc_row[:],
                                                start=True, stop=True), waits=[SC1, CONSTS], incs=[(tns, 1)])
            p.emit("vector", lambda v: v.tensor_copy(qs_bc[:], P[7][:, 0:64]), waits=[tns.at()], incs=[(dve, 1)])
            QSB = dve.at()
            p.emit("sync", (lambda l=l: lambda s: s.dma_start(
                out=sc_row[:], in_=kscale[l].rearrange("(u d) -> u d", u=1)))(), waits=[QSB], incs=[(ld_c, 16)])
            p.emit("tensor", lambda t: t.matmul(P[7][:, 0:64], ones_f[0:1, 0:128], sc_row[:],
                                                start=True, stop=True), waits=[ld_c.at(), QSB], incs=[(tns, 1)])
            p.emit("vector", lambda v: v.tensor_copy(ks_bc[:], P[7][:, 0:64]), waits=[tns.at()], incs=[(dve, 1)])
            KSB = dve.at()

            vblk_written = []
            qn_store = []
            for sl in range(NSLAB):
                war_t = [qn_store[-1]] if sl >= 1 else []
                for c in range(NC_CHUNKS):
                    off = OFF if c < 4 else OFF - 1
                    p.emit("sync", (lambda sl=sl, c=c, off=off: lambda s: s.dma_start(
                        out=hT_slab[:, c, :],
                        in_=x4bf[off + sl * 512:off + (sl + 1) * 512, c * 128:(c + 1) * 128], transpose=True))(),
                        waits=(X4_CUR + war_t + [WQKV]) if c == 0 else (), incs=[(trh[0], 16)])
                TRH = trh[0].at()
                for t4 in range(4):
                    t = sl * 4 + t4
                    for oc in range(3):
                        for ic in range(NC_CHUNKS):
                            p.emit("tensor", (lambda ic=ic, oc=oc, t4=t4: lambda tn: tn.matmul(
                                P[oc][:], hT_slab[:, ic, t4 * 128:(t4 + 1) * 128],
                                wq_sb[:, ic, oc * 512:(oc + 1) * 512],
                                start=(ic == 0), stop=(ic == NC_CHUNKS - 1)))(),
                                waits=([TRH, WQKV] + ([vblk_written[-1]] if vblk_written else []) +
                                       ([QSB, KSB] if t == 0 else [])) if (oc == 0 and ic == 0) else (),
                                incs=[(tns, 1)] if ic == NC_CHUNKS - 1 else ())
                    MMQKV = tns.at()
                    for which, bank, tbl_c, tbl_s, s_bc, dst_h in (
                            ("q", 0, tq_c, tq_s, qs_bc, qn_h), ("k", 1, tk_c, tk_s, ks_bc, kn_h)):
                        p.emit("scalar", (lambda bank=bank: lambda s: s.activation(
                            out=t1[:], in_=P[bank][:], func=AF.Square))(),
                            waits=[MMQKV, dve.at()], incs=[(act, 1)])
                        p.emit("vector", (lambda: lambda v: v.reduce_sum(
                            ss8[:], t1[:].rearrange("p (h d) -> p h d", h=8), axis=AX.X))(),
                            waits=[act.at()], incs=[(dve, 1)])
                        p.emit("scalar", lambda s: s.activation(out=ss8[:], in_=ss8[:], func=AF.Sqrt),
                               waits=[dve.at()], incs=[(act, 1)])
                        p.emit("vector", lambda v: v.tensor_scalar(
                            out=ss8[:], in0=ss8[:], scalar1=1e-12, scalar2=None, op0=ALU.max),
                            waits=[act.at()], incs=[(dve, 1)])
                        p.emit("vector", lambda v: v.reciprocal(rq8[:], ss8[:]), waits=[dve.at()], incs=[(dve, 1)])
                        RQ = dve.at()
                        p.emit("vector", (lambda bank=bank: lambda v: v.tensor_tensor(
                            t0[:].rearrange("p (h d) -> p h d", h=8),
                            P[bank][:].rearrange("p (h d) -> p h d", h=8),
                            rq8[:].rearrange("p (h u) -> p h u", u=1).broadcast_to((128, 8, 64)), ALU.mult))(),
                            waits=[RQ], incs=[(dve, 1)])
                        p.emit("vector", (lambda s_bc=s_bc: lambda v: v.tensor_tensor(
                            t0[:].rearrange("p (h d) -> p h d", h=8),
                            t0[:].rearrange("p (h d) -> p h d", h=8),
                            s_bc[:].rearrange("p (u d) -> p u d", u=1).broadcast_to((128, 8, 64)), ALU.mult))(),
                            waits=[dve.at()], incs=[(dve, 1)])
                        p.emit("vector", (lambda tbl_c=tbl_c, t=t: lambda v: v.tensor_tensor(
                            t1[:].rearrange("p (h d) -> p h d", h=8),
                            t0[:].rearrange("p (h d) -> p h d", h=8),
                            tbl_c[:, t, :].rearrange("p (u d) -> p u d", u=1).broadcast_to((128, 8, 64)), ALU.mult))(),
                            waits=[dve.at()], incs=[(dve, 1)])
                        p.emit("vector", (lambda tbl_s=tbl_s, t=t: lambda v: v.tensor_tensor(
                            t2[:].rearrange("p (h d) -> p h d", h=8)[:, :, 0:32],
                            t0[:].rearrange("p (h d) -> p h d", h=8)[:, :, 32:64],
                            tbl_s[:, t, 0:32].rearrange("p (u d) -> p u d", u=1).broadcast_to((128, 8, 32)), ALU.mult))(),
                            waits=[dve.at()], incs=[(dve, 1)])
                        p.emit("vector", (lambda tbl_s=tbl_s, t=t: lambda v: v.tensor_tensor(
                            t2[:].rearrange("p (h d) -> p h d", h=8)[:, :, 32:64],
                            t0[:].rearrange("p (h d) -> p h d", h=8)[:, :, 0:32],
                            tbl_s[:, t, 32:64].rearrange("p (u d) -> p u d", u=1).broadcast_to((128, 8, 32)), ALU.mult))(),
                            incs=[(dve, 1)])
                        TROT = dve.at()
                        war_qn = [stqn[t % 2].at()]
                        qb_ = t % 2
                        p.emit("vector", (lambda qb_=qb_: lambda v: v.tensor_tensor(
                            qn_sb2[qb_][:].rearrange("p (h d) -> p h d", h=8)[:, :, 0:32],
                            t1[:].rearrange("p (h d) -> p h d", h=8)[:, :, 0:32],
                            t2[:].rearrange("p (h d) -> p h d", h=8)[:, :, 0:32], ALU.subtract))(),
                            waits=[TROT] + war_qn)
                        p.emit("vector", (lambda qb_=qb_: lambda v: v.tensor_tensor(
                            qn_sb2[qb_][:].rearrange("p (h d) -> p h d", h=8)[:, :, 32:64],
                            t1[:].rearrange("p (h d) -> p h d", h=8)[:, :, 32:64],
                            t2[:].rearrange("p (h d) -> p h d", h=8)[:, :, 32:64], ALU.add))(),
                            incs=[(dve, 1)])
                        QN = dve.at()
                        p.emit("sync", (lambda dst_h=dst_h, t=t, qb_=qb_: lambda s: s.dma_start(
                            out=dst_h[t * 128:(t + 1) * 128, :], in_=qn_sb2[qb_][:]))(),
                            waits=[QN], incs=[(stqn[t % 2], 16)])
                    p.emit("vector", (lambda t=t: lambda v: v.memset(vblk[:, t, :, 64:65], 1.0))(),
                           waits=[MMQKV])
                    p.emit("vector", (lambda t=t: lambda v: v.tensor_copy(
                        vblk[:, t, :, 0:64], P[2][:].rearrange("p (h d) -> p h d", h=8)))(),
                        incs=[(dve, 1)])
                    vblk_written.append(dve.at())
                qn_store.append((tns, tns.n))
            QKV_ALL_ST = [stqn[0].at(), stqn[1].at()]
            VBLK_ALL = vblk_written[-1]

            for src_h, dst_h in ((qn_h, qT_h), (kn_h, kT_h)):
                for c in range(4):
                    for sl2 in range(NSLAB):
                        p.emit("sync", (lambda src_h=src_h, c=c, sl2=sl2: lambda s: s.dma_start(
                            out=qn_sb2[0][:], in_=src_h[sl2 * 512:(sl2 + 1) * 512, c * 128:(c + 1) * 128],
                            transpose=True))(),
                            waits=QKV_ALL_ST + [trqk[0].at()], incs=[(trqk[0], 16)])
                        p.emit("sync", (lambda dst_h=dst_h, c=c, sl2=sl2: lambda s: s.dma_start(
                            out=dst_h[c * 128:(c + 1) * 128, sl2 * 512:(sl2 + 1) * 512], in_=qn_sb2[0][:]))(),
                            waits=[trqk[0].at()], incs=[(trqk[0], 16)])
            TRQK = trqk[0].at()

            o_written = []
            hp_mm_end = [None] * 4
            exp_bank_gate = {}
            scores_bank_gate = {}
            prev_rbc = None
            for hp in range(4):
                war = [hp_mm_end[hp - 1]] if hp >= 1 else []
                p.emit("sync", (lambda hp=hp: lambda s: s.dma_start(
                    out=qT_sb, in_=qT_h[hp * 128:(hp + 1) * 128, :]))(),
                    waits=[TRQK] + war, incs=[(ldb[0], 16)])
                p.emit("sync", (lambda hp=hp: lambda s: s.dma_start(
                    out=kT_sb, in_=kT_h[hp * 128:(hp + 1) * 128, :]))(), incs=[(ldb[0], 16)])
                LDB = ldb[0].at()
                for w in range(NW):
                    acc_war = [o_written[-1]] if o_written else []
                    kts = [kt for kt in range(max(0, 4 * (w - 1)), 4 * w + 4)]
                    ntile = len(kts)
                    for ti, kt in enumerate(kts):
                        bank = ti % 2
                        w1_ = [LDB, VBLK_ALL, ldms[0].at()] + (acc_war if ti == 0 else [])
                        g = scores_bank_gate.get(("p1", bank))
                        if g is not None:
                            w1_ = w1_ + [g]
                        g2 = scores_bank_gate.get(bank)
                        if g2 is not None:
                            w1_ = w1_ + [g2]
                        p.emit("tensor", (lambda kt=kt, w=w, bank=bank: lambda t: t.matmul(
                            P[bank][:], kT_sb[0:64, kt * 128:(kt + 1) * 128],
                            qT_sb[0:64, w * 512:(w + 1) * 512], start=True, stop=True))(), waits=w1_)
                        p.emit("tensor", (lambda kt=kt, w=w, bank=bank: lambda t: t.matmul(
                            P[2 + bank][:], kT_sb[64:128, kt * 128:(kt + 1) * 128],
                            qT_sb[64:128, w * 512:(w + 1) * 512], start=True, stop=True,
                            tile_position=(64, 0)))(), incs=[(tns, 1)])
                        mm1 = tns.at()
                        p.emit("vector", (lambda bank=bank, ti=ti: lambda v: v.reduce_max(
                            cmx[:, ti:ti + 1], P[bank][:].rearrange("p (a b) -> p a b", b=4)[:, :, 0], axis=AX.X))(),
                            waits=[mm1], incs=[(dve, 1)])
                        p.emit("vector", (lambda bank=bank, ti=ti: lambda v: v.reduce_max(
                            cmx[:, 8 + ti:9 + ti], P[2 + bank][:].rearrange("p (a b) -> p a b", b=4)[:, :, 0], axis=AX.X))(),
                            incs=[(dve, 1)])
                        scores_bank_gate[("p1", bank)] = dve.at()
                    MM1_ALL = tns.at()
                    CMX = dve.at()
                    p.emit("vector", (lambda ntile=ntile: lambda v: v.reduce_max(
                        cm1[:], cmx[:, 0:8 + ntile] if False else cmx[:, 0:16], axis=AX.X))(),
                        waits=[CMX], incs=[(dve, 1)])
                    p.emit("vector", lambda v: v.tensor_copy(cm1b[:], cm1[:]), waits=[dve.at()], incs=[(dve, 1)])
                    p.emit("tensor", lambda t: t.matmul(P[6][0:1, 0:128], cm1b[:], eyeb_sb[:],
                                                        start=True, stop=True),
                           waits=[dve.at()] + ([prev_rbc] if prev_rbc else []), incs=[(tns, 1)])
                    p.emit("vector", lambda v: v.reduce_max(negC[0:1, :], P[6][0:1, 0:128], axis=AX.X),
                           waits=[tns.at()], incs=[(dve, 1)])
                    p.emit("vector", lambda v: v.tensor_copy(cm1b[0:1, :], negC[0:1, :]),
                           waits=[dve.at()], incs=[(dve, 1)])
                    p.emit("tensor", lambda t: t.matmul(P[6][0:128, 0:1], ones_bf[0:1, 0:128], cm1b[0:1, 0:1],
                                                        start=True, stop=True), waits=[dve.at()], incs=[(tns, 1)])
                    p.emit("vector", lambda v: v.tensor_scalar(
                        out=negC[:], in0=P[6][0:128, 0:1], scalar1=-8.0, scalar2=None, op0=ALU.mult),
                        waits=[tns.at()], incs=[(dve, 1)])
                    NEGC = dve.at()
                    exp_at = {}

                    def emit_mm1b(ti):
                        kt = kts[ti]
                        bank = ti % 2
                        rel = kt - 4 * (w - 1)
                        w2_ = []
                        g = scores_bank_gate.get(bank)
                        if g is not None:
                            w2_ = w2_ + [g]
                        if ti in (0, 1):
                            w2_ = w2_ + [scores_bank_gate[("p1", (ntile - 1) % 2)], CMX, NEGC]
                        p.emit("tensor", (lambda kt=kt, w=w, bank=bank: lambda t: t.matmul(
                            P[2 * bank][:], kT_sb[0:64, kt * 128:(kt + 1) * 128],
                            qT_sb[0:64, w * 512:(w + 1) * 512], start=True, stop=False))(), waits=w2_)
                        p.emit("tensor", (lambda rel=rel, bank=bank: lambda t: t.matmul(
                            P[2 * bank][:], eyeb_sb[:], mask_sb[:, rel, :],
                            start=False, stop=True))())
                        p.emit("tensor", (lambda kt=kt, w=w, bank=bank: lambda t: t.matmul(
                            P[2 * bank + 1][:], kT_sb[64:128, kt * 128:(kt + 1) * 128],
                            qT_sb[64:128, w * 512:(w + 1) * 512], start=True, stop=False,
                            tile_position=(64, 0)))())
                        p.emit("tensor", (lambda rel=rel, bank=bank: lambda t: t.matmul(
                            P[2 * bank + 1][:], eyeb_sb[:], mask_sb[:, rel, :],
                            start=False, stop=True))(), incs=[(tns, 1)])
                        mm1b = tns.at()
                        ew = exp_bank_gate.get(bank)
                        p.emit("scalar", (lambda bank=bank: lambda s: s.activation(
                            out=Eab[bank][:], in_=p.PQ[bank][:, 0:1024], func=AF.Exp, scale=8.0, bias=negC[:]))(),
                            waits=[NEGC, mm1b] + ([ew] if ew else []), incs=[(act, 1)])
                        exp_at[ti] = act.at()
                        scores_bank_gate[bank] = exp_at[ti]

                    def emit_mm2b(ti):
                        kt = kts[ti]
                        bank = ti % 2
                        p.emit("tensor", (lambda kt=kt, hp=hp, bank=bank, ti=ti: lambda t: t.matmul(
                            P[4][0:65, :], vblk[:, kt, 2 * hp, :], Eab[bank][:, 0:512],
                            start=(ti == 0), stop=(ti == ntile - 1)))(), waits=[exp_at[ti]])
                        p.emit("tensor", (lambda kt=kt, hp=hp, bank=bank, ti=ti: lambda t: t.matmul(
                            P[5][0:65, :], vblk[:, kt, 2 * hp + 1, :], Eab[bank][:, 512:1024],
                            start=(ti == 0), stop=(ti == ntile - 1)))(), incs=[(tns, 1)])
                        exp_bank_gate[bank] = tns.at()

                    emit_mm1b(0)
                    for ti in range(1, ntile):
                        emit_mm1b(ti)
                        emit_mm2b(ti - 1)
                    emit_mm2b(ntile - 1)
                    MM2_ALL = tns.at()
                    RBC_PREV = o_written[-1] if o_written else (dve, 0)
                    p.emit("vector", lambda v: v.tensor_copy(rs4, P[4][64:65, :]), waits=[MM2_ALL, RBC_PREV], incs=[(dve, 1)])
                    p.emit("vector", lambda v: v.tensor_copy(rs4b[:], P[5][64:65, :]), incs=[(dve, 1)])
                    p.emit("vector", lambda v: v.reciprocal(recA[:], rs4), waits=[dve.at()], incs=[(dve, 1)])
                    p.emit("vector", lambda v: v.reciprocal(recB[:], rs4b[:]), incs=[(dve, 1)])
                    p.emit("vector", lambda v: v.tensor_copy(rhiA[:], recA[:]), waits=[dve.at()], incs=[(dve, 1)])
                    p.emit("vector", lambda v: v.tensor_copy(rhiB[:], recB[:]), incs=[(dve, 1)])
                    p.emit("vector", lambda v: v.tensor_tensor(rloA[:], recA[:], rhiA[:], ALU.subtract),
                           waits=[dve.at()], incs=[(dve, 1)])
                    p.emit("vector", lambda v: v.tensor_tensor(rloB[:], recB[:], rhiB[:], ALU.subtract),
                           incs=[(dve, 1)])
                    RL = dve.at()
                    p.emit("tensor", lambda t: t.matmul(P[6][0:64, :], ones_bf[0:1, 0:64], rhiA[:], start=True, stop=False),
                           waits=[RL])
                    p.emit("tensor", lambda t: t.matmul(P[6][0:64, :], ones_bf[0:1, 0:64], rloA[:], start=False, stop=True))
                    p.emit("tensor", lambda t: t.matmul(P[6][64:128, :], ones_bf[0:1, 0:64], rhiB[:], start=True, stop=False,
                                                        tile_position=(0, 64)))
                    p.emit("tensor", lambda t: t.matmul(P[6][64:128, :], ones_bf[0:1, 0:64], rloB[:], start=False, stop=True,
                                                        tile_position=(0, 64)), incs=[(tns, 1)])
                    p.emit("scalar", lambda s: s.copy(out=rbc[:], in_=P[6][:]), waits=[tns.at()], incs=[(act, 1)])
                    RBC = act.at()
                    prev_rbc = RBC
                    ob_ = w % 2
                    sto = DmaSlots.__call__ if False else None
                    p.emit("vector", (lambda ob_=ob_: lambda v: v.tensor_tensor(
                        o_stage[ob_][0:64, :], P[4][0:64, :], rbc[0:64, :], ALU.mult))(),
                        waits=[RBC, sto4[ob_].at()], incs=[(dve, 1)])
                    p.emit("vector", (lambda ob_=ob_: lambda v: v.tensor_tensor(
                        o_stage[ob_][64:128, :], P[5][0:64, :], rbc[64:128, :], ALU.mult))(),
                        incs=[(dve, 1)])
                    o_written.append(dve.at())
                    p.emit("sync", (lambda hp=hp, w=w, ob_=ob_: lambda s: s.dma_start(
                        out=o4T_h[hp * 128:(hp + 1) * 128, w * 512:(w + 1) * 512], in_=o_stage[ob_]))(),
                        waits=[o_written[-1]], incs=[(sto4[ob_], 16)])
                hp_mm_end[hp] = (tns, tns.n)
            O_ALL = [sto4[0].at(), sto4[1].at()]

            res_done = []
            oproj_mm = []
            for t in range(TT):
                b = t % 2
                war = [res_done[t - 2]] if t >= 2 else []
                p.emit("sync", (lambda t=t, b=b: lambda s: s.dma_start(
                    out=xt[b][:], in_=x4[OFF + t * 128:OFF + (t + 1) * 128, :]))(),
                    waits=war + [stn[b].at()], incs=[(ldx2[b], 16)])
                LDX = ldx2[b].at()
                war_o = [oproj_mm[t - 2]] if t >= 2 else []
                p.emit("sync", (lambda t=t, b=b: lambda s: s.dma_start(
                    out=oT_t[b][:], in_=o4T_h[:, t * 128:(t + 1) * 128].rearrange("(c p) n -> p c n", p=128)))(),
                    waits=O_ALL + war_o, incs=[(ldo4[b], 16)])
                LDO = ldo4[b].at()
                mm_at = []
                for dh in range(2):
                    bank = (t * 2 + dh) % 4
                    for ic in range(4):
                        p.emit("tensor", (lambda ic=ic, t=t, dh=dh, bank=bank, b=b: lambda tn: tn.matmul(
                            P[bank][:], oT_t[b][:, ic, :],
                            wo_sb[:, ic, dh * 512:(dh + 1) * 512], start=(ic == 0), stop=(ic == 3)))(),
                            waits=([LDO] + war) if (ic == 0 and dh == 0) else (),
                            incs=[(tns, 1)] if ic == 3 else ())
                    mm_at.append(tns.at())
                oproj_mm.append(tns.at())
                for dh in range(2):
                    bank = (t * 2 + dh) % 4
                    p.emit("vector", (lambda b=b, dh=dh, bank=bank: lambda v: v.tensor_add(
                        xt[b][:, dh * 512:(dh + 1) * 512], xt[b][:, dh * 512:(dh + 1) * 512], P[bank][:]))(),
                        waits=[mm_at[1], LDX] if dh == 0 else (), incs=[(dve, 1)] if dh == 1 else ())
                p.emit("vector", (lambda b=b: lambda v: v.tensor_copy(xn_sb[b][:], xt[b][:]))(),
                       waits=[dve.at()], incs=[(dve, 1)])
                res_done.append(dve.at())
                p.emit("sync", (lambda t=t, b=b: lambda s: s.dma_start(
                    out=x4[OFF + t * 128:OFF + (t + 1) * 128, :], in_=xt[b][:]))(),
                    waits=[res_done[-1]], incs=[(stn[b], 16)])
                p.emit("scalar", (lambda t=t, b=b: lambda s: s.dma_start(
                    out=x4bf[OFF + t * 128:OFF + (t + 1) * 128, :], in_=xn_sb[b][:]))(),
                    waits=[res_done[-1]], incs=[(stn[b], 16)])
            RES_ST = [stn[0].at(), stn[1].at()]
            OPROJ_END = (tns, tns.n)

            for (rowsrc, dst_bc, wpre) in ((lng, g_bc, []), (lnb, b_bc, None)):
                if wpre is None:
                    wpre = [dve.at()]
                p.emit("sync", (lambda l=l, rowsrc=rowsrc: lambda s: s.dma_start(
                    out=gb_row[:], in_=rowsrc[l].rearrange("(u d) -> u d", u=1)[0:1, 0:512]))(),
                    waits=wpre, incs=[(ld_c, 16)])
                p.emit("tensor", lambda t: t.matmul(P[6][:], ones_f[0:1, 0:128], gb_row[0:1, 0:512],
                                                    start=True, stop=True),
                       waits=[ld_c.at()] + wpre, incs=[(tns, 1)])
                GM1 = tns.at()
                p.emit("sync", (lambda l=l, rowsrc=rowsrc: lambda s: s.dma_start(
                    out=gb_row[:], in_=rowsrc[l].rearrange("(u d) -> u d", u=1)[0:1, 512:1024]))(),
                    waits=[GM1], incs=[(ld_c, 16)])
                p.emit("tensor", lambda t: t.matmul(P[7][:], ones_f[0:1, 0:128], gb_row[0:1, 0:512],
                                                    start=True, stop=True),
                       waits=[ld_c.at()], incs=[(tns, 1)])
                p.emit("vector", (lambda dst_bc=dst_bc: lambda v: v.tensor_copy(dst_bc[:, 0:512], P[6][:]))(),
                       waits=[tns.at()])
                p.emit("vector", (lambda dst_bc=dst_bc: lambda v: v.tensor_copy(dst_bc[:, 512:1024], P[7][:]))(),
                       incs=[(dve, 1)])
            GB_ALL = dve.at()
            p.emit("gpsimd", (lambda l=l: lambda g: g.dma_start(
                out=b2_row[:], in_=b2[l].rearrange("(u d) -> u d", u=1)))(), waits=lg, incs=[(ldb2s[0], 16)])
            W2L = ldb2s[0].at()

            hln_done = []
            hln_st_at = []
            for t in range(TT):
                b = t % 2
                war = [hln_done[t - 2]] if t >= 2 else []
                p.emit("sync", (lambda t=t, b=b: lambda s: s.dma_start(
                    out=xt[b][:, 0:512], in_=x4[OFF + t * 128:OFF + (t + 1) * 128, 0:512]))(),
                    waits=RES_ST + war, incs=[(ldx3[b], 16)])
                p.emit("sync", (lambda t=t, b=b: lambda s: s.dma_start(
                    out=xt[b][:, 512:1024], in_=x4[OFF - 1 + t * 128:OFF - 1 + (t + 1) * 128, 512:1024]))(),
                    incs=[(ldx3[b], 16)])
                rstd_at, aggr_at = emit_ln_stats(p, ln_pre, b, [xt[b][:, 0:512], xt[b][:, 512:1024]],
                                                 eps, dve, act, waits=[ldx3[b].at()])
                war2 = [stqn[b].at()]
                p.emit("vector", (lambda b=b: lambda v: v.tensor_scalar(
                    out=xt[b][:], in0=xt[b][:], scalar1=ln_pre.mv[b][:, 0:1], scalar2=ln_pre.rstd[b][:],
                    op0=ALU.subtract, op1=ALU.mult))(), waits=[rstd_at, aggr_at, GB_ALL], incs=[(dve, 1)])
                p.emit("vector", (lambda b=b: lambda v: v.tensor_mul(xt[b][:], xt[b][:], g_bc[:]))(),
                       waits=[dve.at()], incs=[(dve, 1)])
                p.emit("vector", (lambda b=b: lambda v: v.tensor_tensor(
                    hln_sb[b][:], xt[b][:], b_bc[:], ALU.add))(), waits=war2 + [dve.at()], incs=[(dve, 1)])
                hln_done.append(dve.at())
                p.emit("sync", (lambda t=t, b=b: lambda s: s.dma_start(
                    out=qn_h[t * 128:(t + 1) * 128, 0:512], in_=hln_sb[b][:, 0:512]))(),
                    waits=[hln_done[-1], TRQK], incs=[(stqn[b], 16)])
                p.emit("sync", (lambda t=t, b=b: lambda s: s.dma_start(
                    out=kn_h[t * 128:(t + 1) * 128, 0:512], in_=hln_sb[b][:, 512:1024]))(), incs=[(stqn[b], 16)])
                hln_st_at.append([stqn[0].at(), stqn[1].at()])
            HLN_ST = [stqn[0].at(), stqn[1].at()]

            ffn_done = []
            h2_gate = []
            ffn1_end = []
            prev_w1_reads = None
            prev_w2_reads = None
            for sl in range(NSLAB):
                war_t = [ffn1_end[-1]] if ffn1_end else []
                for c in range(NC_CHUNKS):
                    src = qn_h if c < 4 else kn_h
                    cc = c % 4
                    p.emit("sync", (lambda sl=sl, src=src, cc=cc, c=c: lambda s: s.dma_start(
                        out=hlnT_slab[:, c, :],
                        in_=src[sl * 512:(sl + 1) * 512, cc * 128:(cc + 1) * 128], transpose=True))(),
                        waits=(hln_st_at[sl * 4 + 3] + war_t) if c == 0 else (), incs=[(trh[0], 16)])
                TRH2 = trh[0].at()
                pair_mm = {}
                for ip in range(NH2 // 2):
                    bload = ip % 2
                    gate0 = []
                    if ip < 2:
                        if sl == 0:
                            gate0 = [OPROJ_END]
                            if prev_w1_reads is not None:
                                gate0 = gate0 + [prev_w1_reads]
                        else:
                            gate0 = [prev_w1_reads]
                    war_p = [pair_mm[2 * ip - 3]] if ip >= 2 else []
                    prep_w = PREP_W1 if sl == 0 else []
                    p.emit("gpsimd", (lambda l=l, ip=ip, bload=bload: lambda g: g.dma_start(
                        out=w1pa[bload][:],
                        in_=w1bfA[l][:, ip * 256:(ip + 1) * 256].rearrange("(c p) d -> p c d", p=128)))(),
                        waits=[ldw1[bload].at()] + war_p + gate0 + prep_w, incs=[(ldw1[bload], 16)])
                    p.emit("gpsimd", (lambda l=l, ip=ip, bload=bload: lambda g: g.dma_start(
                        out=w1pg[bload][:],
                        in_=w1bfB[l][:, ip * 256:(ip + 1) * 256].rearrange("(c p) d -> p c d", p=128)))(),
                        incs=[(ldw1[bload], 16)])
                    for u in range(2):
                        i = 2 * ip + u
                        wi = 128 if i < 21 else 42
                        p.emit("gpsimd", (lambda l=l, i=i, u=u, wi=wi, bload=bload: lambda g: g.dma_start(
                            out=b1p[bload][0:1, u * 256:u * 256 + wi],
                            in_=b1[l][i * 128:i * 128 + wi].rearrange("(u d) -> u d", u=1)))(),
                            incs=[(ldw1[bload], 16)])
                        p.emit("gpsimd", (lambda l=l, i=i, u=u, wi=wi, bload=bload: lambda g: g.dma_start(
                            out=b1p[bload][0:1, u * 256 + 128:u * 256 + 128 + wi],
                            in_=b1[l][HID + i * 128:HID + i * 128 + wi].rearrange("(u d) -> u d", u=1)))(),
                            incs=[(ldw1[bload], 16)])
                    LW1 = ldw1[bload].at()
                    for u in range(2):
                        i = 2 * ip + u
                        wi = 128 if i < 21 else 42
                        ba = 2 * u
                        bg = ba + 1
                        gate = [pair_mm[i - 2]] if i >= 2 else ([ffn_done[-1]] if ffn_done else [])
                        for ic in range(NC_CHUNKS):
                            p.emit("tensor", (lambda ic=ic, u=u, wi=wi, ba=ba, bload=bload: lambda tn: tn.matmul(
                                P[ba][0:wi, :], w1pa[bload][:, ic, u * 128:u * 128 + wi], hlnT_slab[:, ic, :],
                                start=(ic == 0), stop=False))(),
                                waits=([TRH2, LW1, W2L] + gate) if ic == 0 else ())
                        p.emit("tensor", (lambda u=u, wi=wi, ba=ba, bload=bload: lambda tn: tn.matmul(
                            P[ba][0:wi, :], b1p[bload][0:1, u * 256:u * 256 + wi], ones_bf[0:1, :],
                            start=False, stop=True))(), incs=[(tns, 1)])
                        for ic in range(NC_CHUNKS):
                            p.emit("tensor", (lambda ic=ic, u=u, wi=wi, bg=bg, bload=bload: lambda tn: tn.matmul(
                                P[bg][0:wi, :], w1pg[bload][:, ic, u * 128:u * 128 + wi], hlnT_slab[:, ic, :],
                                start=(ic == 0), stop=False))())
                        p.emit("tensor", (lambda u=u, wi=wi, bg=bg, bload=bload: lambda tn: tn.matmul(
                            P[bg][0:wi, :], b1p[bload][0:1, u * 256 + 128:u * 256 + 128 + wi], ones_bf[0:1, :],
                            start=False, stop=True))(), incs=[(tns, 1)])
                        MMP = tns.at()
                        pair_mm[i] = MMP
                        if i == NH2 - 1:
                            prev_w1_reads = MMP
                        p.emit("scalar", (lambda wi=wi, bg=bg: lambda s: s.activation(
                            out=gact[0:wi, :], in_=P[bg][0:wi, :], func=AF.Gelu))(),
                            waits=[MMP] + ([pair_mm[i - 1]] if i >= 1 else []), incs=[(act, 1)])
                        p.emit("vector", (lambda i=i, wi=wi, ba=ba: lambda v: v.tensor_tensor(
                            h2T_slab[0:wi, i, :], P[ba][0:wi, :], gact[0:wi, :], ALU.mult))(),
                            waits=[act.at()], incs=[(dve, 1)])
                        pair_mm[i] = dve.at()
                H2_ALL = dve.at()
                ffn1_end.append(H2_ALL)
                prev_ffn_reads = ffn_done[-1] if ffn_done else None
                w2mm = {}
                for i in range(NH2):
                    wi = 128 if i < 21 else 42
                    bl2 = i % 2
                    gate2 = []
                    if i < 2:
                        if sl == 0:
                            gate2 = [OPROJ_END]
                            if prev_w2_reads is not None:
                                gate2 = gate2 + [prev_w2_reads]
                        else:
                            gate2 = [prev_w2_reads]
                    p.emit("gpsimd", (lambda l=l, i=i, wi=wi, bl2=bl2: lambda g: g.dma_start(
                        out=w2p[bl2][0:wi, :], in_=w2bf[l][i * 128:i * 128 + wi, :]))(),
                        waits=[ldw2[bl2].at()] + ([w2mm[i - 2]] if i >= 2 else []) + gate2
                              + (PREP_W1 if sl == 0 else []), incs=[(ldw2[bl2], 16)])
                    LW2 = ldw2[bl2].at()
                    for t4 in range(4):
                        for dh in range(2):
                            bank = t4 * 2 + dh
                            p.emit("tensor", (lambda i=i, wi=wi, t4=t4, dh=dh, bank=bank, bl2=bl2: lambda tn: tn.matmul(
                                P[bank][:], h2T_slab[0:wi, i, t4 * 128:(t4 + 1) * 128],
                                w2p[bl2][0:wi, dh * 512:(dh + 1) * 512],
                                start=(i == 0), stop=False))(),
                                waits=([H2_ALL, LW2] + ([prev_ffn_reads] if prev_ffn_reads else [])) if (t4 == 0 and dh == 0) else (),
                                incs=[(tns, 1)] if (t4 == 3 and dh == 1) else ())
                    w2mm[i] = tns.at()
                prev_w2_reads = tns.at()
                for t4 in range(4):
                    for dh in range(2):
                        bank = t4 * 2 + dh
                        p.emit("tensor", (lambda dh=dh, bank=bank: lambda tn: tn.matmul(
                            P[bank][:], ones_bf[0:1, 0:128], b2_row[0:1, dh * 512:(dh + 1) * 512],
                            start=False, stop=True))(), incs=[(tns, 1)] if (t4 == 3 and dh == 1) else ())
                FFN2MM = tns.at()
                for t4 in range(4):
                    t = sl * 4 + t4
                    b = t % 2
                    war = [ffn_done[-2]] if len(ffn_done) >= 2 else []
                    p.emit("sync", (lambda t=t, b=b: lambda s: s.dma_start(
                        out=y4[b][:], in_=x4[OFF + t * 128:OFF + (t + 1) * 128, :]))(),
                        waits=war + [stn[b].at()] + [stx[b].at()], incs=[(ldx2[b], 16)])
                    LD4 = ldx2[b].at()
                    for dh in range(2):
                        bank = t4 * 2 + dh
                        p.emit("vector", (lambda b=b, dh=dh, bank=bank: lambda v: v.tensor_add(
                            y4[b][:, dh * 512:(dh + 1) * 512], y4[b][:, dh * 512:(dh + 1) * 512], P[bank][:]))(),
                            waits=[FFN2MM, LD4] if dh == 0 else (), incs=[(dve, 1)] if dh == 1 else ())
                    Y4 = dve.at()
                    ffn_done.append(Y4)
                    if l == 0:
                        p.emit("vector", (lambda b=b: lambda v: v.tensor_copy(xn_sb[b][:], y4[b][:]))(),
                               waits=[Y4], incs=[(dve, 1)])
                        ffn_done[-1] = dve.at()
                        p.emit("sync", (lambda t=t, b=b: lambda s: s.dma_start(
                            out=x4[OFF + t * 128:OFF + (t + 1) * 128, :], in_=y4[b][:]))(),
                            waits=[ffn_done[-1]], incs=[(stn[b], 16)])
                        p.emit("sync", (lambda t=t, b=b: lambda s: s.dma_start(
                            out=x4bf[OFF + t * 128:OFF + (t + 1) * 128, :], in_=xn_sb[b][:]))(), incs=[(stn[b], 16)])
                    else:
                        p.emit("sync", (lambda t=t, b=b: lambda s: s.dma_start(
                            out=xt[b][:], in_=x_in[t * 128:(t + 1) * 128, :]))(),
                            waits=[Y4], incs=[(ldx3[b], 16)])
                        LDXI = ldx3[b].at()
                        rstd_at, aggr_at = emit_ln_stats(p, ln_post, b, [y4[b][:, 0:512], y4[b][:, 512:1024]],
                                                         eps, dve, act, waits=[Y4])
                        p.emit("vector", (lambda b=b: lambda v: v.tensor_scalar(
                            out=xo_sb[b][:], in0=y4[b][:], scalar1=ln_post.mv[b][:, 0:1],
                            scalar2=ln_post.rstd[b][:], op0=ALU.subtract, op1=ALU.mult))(),
                            waits=[rstd_at, aggr_at, stx[b].at()], incs=[(dve, 1)])
                        p.emit("vector", (lambda b=b: lambda v: v.tensor_add(xo_sb[b][:], xo_sb[b][:], xt[b][:]))(),
                               waits=[LDXI, dve.at()], incs=[(dve, 1)])
                        ffn_done[-1] = dve.at()
                        p.emit("sync", (lambda t=t, b=b: lambda s: s.dma_start(
                            out=x_out[t * 128:(t + 1) * 128, :], in_=xo_sb[b][:]))(),
                            waits=[ffn_done[-1]], incs=[(stx[b], 16)])
                h2_gate.append(tns.at())
            LAYER_GATE = ffn_done[-1]

        return [stx[0].at(), stx[1].at()]


def phase4_consts(S):
    DIM_HEAD = 64
    WINDOW = 512
    half_idx = np.arange(0, DIM_HEAD, 2, dtype=np.float64)
    inv_freq = 1.0 / 10000.0 ** (half_idx / DIM_HEAD)
    t = np.arange(S, dtype=np.float64)
    freqs = t[:, None] * inv_freq[None, :]
    freqs = np.concatenate([freqs, freqs], axis=-1)
    base = (half_idx + 0.4 * DIM_HEAD) / (1.4 * DIM_HEAD)
    power = (t - S // 2) / float(WINDOW // 2)
    scale = base[None, :] ** power[:, None]
    scale = np.concatenate([scale, scale], axis=-1)
    cos, sin = np.cos(freqs), np.sin(freqs)
    tq_c = (cos * scale).astype(np.float32)
    tq_s = (sin * scale).astype(np.float32)
    tk_c = (cos / scale).astype(np.float32)
    tk_s = (sin / scale).astype(np.float32)
    NEG = np.float32(-1e30)
    m4 = np.zeros((8, 128, 512), np.float32)
    for rel in range(8):
        r = np.arange(128)[:, None]
        c = np.arange(512)[None, :]
        diff = c - r + 512 - 128 * rel
        m4[rel] = np.where((diff >= 0) & (diff <= 512), 0.0, NEG)
    return tq_c, tq_s, tk_c, tk_s, m4.astype(ml_dtypes.bfloat16)



def build_fused(S=4096):
    nc = bass.Bass("TRN2", target_bir_lowering=False)
    names_f32 = [
        ("L0_attn_q_w", [D, D]), ("L0_attn_q_b", [D]), ("L0_attn_k_w", [D, D]), ("L0_attn_k_b", [D]),
        ("L0_attn_v_w", [D, D]), ("L0_attn_v_b", [D]), ("L0_attn_o_w", [D, D]), ("L0_attn_o_b", [D]),
        ("rmT", [16, 256 * 16]), ("rupT", [16, 256 * 64]), ("wcTf", [16384, D]),
        ("L1_attn_w_qkv", [D, 3 * D]), ("L1_attn_b_qkv", [3 * D]),
        ("L1_attn_w_out", [D, D]), ("L1_attn_b_out", [D]),
        ("wqkv", [2, D, 3 * 512]), ("q_scale", [2, 64]), ("k_scale", [2, 64]),
        ("wo", [2, 512, D]), ("ln_g", [2, D]), ("ln_b", [2, D]),
        ("w1", [2, D, 2 * 2730]), ("b1", [2, 2 * 2730]), ("w2", [2, 2730, D]), ("b2", [2, D]),
        ("eye8", [8, 8]),
        ("tqc", [S, 64]), ("tqs", [S, 64]), ("tkc", [S, 64]), ("tks", [S, 64]),
    ]
    names_bf16 = [
        ("eyeb", [128, 128]), ("bandm", [6, 128, 512]), ("stridm", [S // 512, 128, 512]),
        ("mask4", [8, 128, 512]),
    ]
    A = {}
    for n, shp in names_f32:
        A[n] = nc.dram_tensor(n, shp, F32, kind="ExternalInput").ap()
    for n, shp in names_bf16:
        A[n] = nc.dram_tensor(n, shp, BF16, kind="ExternalInput").ap()
    x_in = nc.dram_tensor("x_in", [S, D], F32, kind="ExternalInput").ap()
    x_out = nc.dram_tensor("x_out", [S, D], F32, kind="ExternalOutput").ap()
    xb0 = nc.dram_tensor("xb0", [S, D], F32, kind="Internal").ap()
    xb1 = nc.dram_tensor("xb1", [S, D], F32, kind="Internal").ap()

    W1 = dict(q_w=A["L0_attn_q_w"], q_b=A["L0_attn_q_b"], k_w=A["L0_attn_k_w"], k_b=A["L0_attn_k_b"],
              v_w=A["L0_attn_v_w"], v_b=A["L0_attn_v_b"], o_w=A["L0_attn_o_w"], o_b=A["L0_attn_o_b"],
              eye8=A["eye8"])
    W2 = dict(rmT=A["rmT"], rupT=A["rupT"], wcTf=A["wcTf"])
    W3 = dict(w_qkv=A["L1_attn_w_qkv"], b_qkv=A["L1_attn_b_qkv"], w_out=A["L1_attn_w_out"],
              b_out=A["L1_attn_b_out"], eye8=A["eye8"], eyeb=A["eyeb"], bandm=A["bandm"],
              stridm=A["stridm"])
    W4 = dict(wqkv=A["wqkv"], q_scale=A["q_scale"], k_scale=A["k_scale"], wo=A["wo"],
              ln_g=A["ln_g"], ln_b=A["ln_b"], w1=A["w1"], b1=A["b1"], w2=A["w2"], b2=A["b2"],
              eyeb=A["eyeb"], tqc=A["tqc"], tqs=A["tqs"], tkc=A["tkc"], tks=A["tks"],
              mask4=A["mask4"])

    outer = contextlib.ExitStack()
    with outer:
        PQ = [outer.enter_context(nc.psum_tensor(f"pq{i}", [128, 1024], F32)) for i in range(4)]
        P = [PQ[i // 2][:, (i % 2) * 512:(i % 2) * 512 + 512] for i in range(8)]
        p = Prog(nc, outer)
        p.PQ = PQ
        p.AR = ArenaAlloc(nc, outer)
        tns = p.sem("tns")
        act = p.sem("act")
        dve = p.sem("dve")
        phases = [
            (emit_phase1, x_in, xb0, W1, "p1_"),
            (emit_phase2, xb0, xb1, W2, "p2_"),
            (emit_phase3, xb1, xb0, W3, "p3_"),
            (emit_phase4, xb0, x_out, W4, "p4_"),
        ]
        gate = []
        final = None
        for fn, xi, xo, Wd, pref in phases:
            p.set_gate(gate, pref)
            p.AR.reset()
            with contextlib.ExitStack() as ph_stack:
                final = fn(nc, p, ph_stack, P, tns, act, dve, S, xi, xo, Wd)
                gate = p.snapshot_all()
        p.finalize(final_waits=final)
    return nc




from concourse.bass_utils import run_bass_kernel_spmd

_S = 4096
_NCORES = 8


def kernel(**inputs):
    S = _S
    cores = list(range(_NCORES))
    f32 = np.float32
    x = np.ascontiguousarray(inputs["L0_attn_x"].astype(f32, copy=False))

    eye8 = np.eye(8, dtype=f32)
    eyeb = np.eye(128, dtype=f32).astype(ml_dtypes.bfloat16)
    bandm, stridm = phase3_masks(S)
    tqc, tqs_, tkc, tks_, m4 = phase4_consts(S)
    rmT = np.ascontiguousarray(inputs["L0_ffn_rail_maker"].transpose(3, 0, 1, 2).reshape(16, 256 * 16))
    rupT = np.ascontiguousarray(inputs["L0_ffn_rail_up_proj"].transpose(3, 0, 1, 2).reshape(16, 256 * 64))

    nc = build_fused(S)
    base = dict(
        L0_attn_q_w=inputs["L0_attn_q_w"], L0_attn_q_b=inputs["L0_attn_q_b"],
        L0_attn_k_w=inputs["L0_attn_k_w"], L0_attn_k_b=inputs["L0_attn_k_b"],
        L0_attn_v_w=inputs["L0_attn_v_w"], L0_attn_v_b=inputs["L0_attn_v_b"],
        L0_attn_o_w=inputs["L0_attn_o_w"], L0_attn_o_b=inputs["L0_attn_o_b"],
        rmT=rmT, rupT=rupT, wcTf=np.ascontiguousarray(inputs["L0_ffn_w_cproj"].T),
        L1_attn_w_qkv=inputs["L1_attn_w_qkv"], L1_attn_b_qkv=inputs["L1_attn_b_qkv"],
        L1_attn_w_out=inputs["L1_attn_w_out"], L1_attn_b_out=inputs["L1_attn_b_out"],
        wqkv=inputs["L1_ffn_wqkv"], q_scale=inputs["L1_ffn_q_scale"], k_scale=inputs["L1_ffn_k_scale"],
        wo=inputs["L1_ffn_wo"], ln_g=inputs["L1_ffn_ln_g"], ln_b=inputs["L1_ffn_ln_b"],
        w1=inputs["L1_ffn_w1"], b1=inputs["L1_ffn_b1"], w2=inputs["L1_ffn_w2"], b2=inputs["L1_ffn_b2"],
        eye8=eye8, eyeb=eyeb, bandm=bandm, stridm=stridm,
        tqc=tqc, tqs=tqs_, tkc=tkc, tks=tks_, mask4=m4)
    maps = [dict(base, x_in=x[i]) for i in cores]
    res = run_bass_kernel_spmd(nc, maps, core_ids=cores)
    out = np.stack([res.results[i]["x_out"] for i in cores], axis=0)
    return out.astype(np.float32, copy=False)
